# version 13
# baseline (speedup 1.0000x reference)
import numpy as np
import ml_dtypes
from contextlib import ExitStack

import concourse.bass as bass
import concourse.mybir as mybir
from concourse.bass_utils import run_bass_kernel_spmd

F32 = mybir.dt.float32
BF16 = mybir.dt.bfloat16
AF = mybir.ActivationFunctionType
ALU = mybir.AluOpType
AX = mybir.AxisListType
bf16 = ml_dtypes.bfloat16

NCORES = 8
P = 128
D = 1024
DK = D // P
T = 4096
NT = T // P
SLEN = 2048
NSENT = 2
NQ = 8
QT = T // NQ
QTT = QT // P
E = 8
II = 2816
ITH = 11
IHW = ITH * P
NH = 16
HD = 64
LN8 = float(np.log(8.0))

SPARSE_MOE = True

MOE_SEMS = ["gu", "yr", "mk", "sl", "hs", "yf", "zp", "mke", "xo", "e1", "e2", "e3",
            "w0", "w1", "w2", "w3"]

SEMS = ["dS", "dG", "d0", "d1", "d2", "d3", "d4", "d5", "g0", "g1", "g2", "g3", "g4", "tS", "vS", "sS", "bar"]


class NcProxy:

    def __init__(self, nc):
        self._nc = nc
        self._cnt = 0

    def __getattr__(self, name):
        return getattr(self._nc, name)

    def sbuf_tensor(self, name, shape, dt):
        self._cnt += 1
        return self._nc.sbuf_tensor(f"{name}_u{self._cnt}", shape, dt)

    def psum_tensor(self, name, shape, dt):
        self._cnt += 1
        return self._nc.psum_tensor(f"{name}_u{self._cnt}", shape, dt)


class Ctx:
    def __init__(self, nc, stack):
        self.nc = nc
        self.stack = stack
        self.sems = {}
        self.n = {}
        self.inputs = {}

    def sem_init(self):
        for name in SEMS:
            self.sems[name] = self.stack.enter_context(self.nc.semaphore(name))
            self.n[name] = 0

    def inc(self, instr, name, k=1):
        instr.then_inc(self.sems[name], k)
        self.n[name] += k
        return self.n[name]

    def wait(self, eng, name, tok):
        eng.wait_ge(self.sems[name], tok)

    def sb(self, name, shape, dt=F32):
        return self.stack.enter_context(self.nc.sbuf_tensor(name, shape, dt))

    def ps(self, name, shape, dt=F32):
        return self.stack.enter_context(self.nc.psum_tensor(name, shape, dt))


def barrier(ctx, block):
    bar = ctx.sems["bar"]
    tgt = ctx.n["bar"] + 5
    ctx.n["bar"] = tgt
    dtoks = {k: ctx.n[k] for k in ["dS", "d0", "d1", "d2", "d3", "d4", "d5"]}
    gtoks = {k: ctx.n[k] for k in ["dG", "g0", "g1", "g2", "g3", "g4"]}

    @block.sync
    def _(s):
        for k, v in dtoks.items():
            if v:
                s.wait_ge(ctx.sems[k], v)
        s.nop().then_inc(bar, 1)
        s.wait_ge(bar, tgt)

    @block.gpsimd
    def _(g):
        for k, v in gtoks.items():
            if v:
                g.wait_ge(ctx.sems[k], v)
        g.drain()
        g.nop().then_inc(bar, 1)
        g.wait_ge(bar, tgt)

    @block.tensor
    def _(t):
        t.drain()
        t.nop().then_inc(bar, 1)
        t.wait_ge(bar, tgt)

    @block.scalar
    def _(s):
        s.drain()
        s.nop().then_inc(bar, 1)
        s.wait_ge(bar, tgt)

    @block.vector
    def _(v):
        v.drain()
        v.nop().then_inc(bar, 1)
        v.wait_ge(bar, tgt)


def r_pkm(ap2d, p=P):
    return ap2d.rearrange("(k p) m -> p k m", p=p)


def r_pk(ap1d, p=P):
    return ap1d.rearrange("(k p) -> p k", p=p)


def emit_prologue(ctx, block, scr, need, stage="all"):
    inp = ctx.inputs

    casts = []
    if "moe" in need:
        casts.append((scr["rw_bf"], inp["L0_ffn_router_w"]))
    if "perf" in need:
        casts.append((scr["qkv_bf"], inp["L0_attn_qkv_w"]))
        casts.append((scr["proj_bf"], inp["L0_attn_proj_w"]))
        casts.append((scr["Wbig"][1024:1025, 0:D], inp["L0_attn_qkv_b"][2048:3072].unsqueeze(0)))
        casts.append((scr["projT_aug"][1024:1025, :], inp["L0_attn_proj_b"].unsqueeze(0)))
    if "l1att" in need:
        casts.append((scr["Wq_bf"], inp["L1_attn_Wq"]))
        casts.append((scr["Wk_bf"], inp["L1_attn_Wk"]))
        casts.append((scr["Wv_bf"], inp["L1_attn_Wv"]))
        casts.append((scr["Wg1_bf"], inp["L1_attn_Wg1"]))
        casts.append((scr["Wg2_bf"], inp["L1_attn_Wg2"]))
        casts.append((scr["Wo_bf"], inp["L1_attn_Wo"]))
    if "l1ffn" in need:
        casts.append((scr["Wup_bf"], inp["L1_ffn_W_up"]))
        casts.append((scr["bup_bf"], inp["L1_ffn_b_up"]))
        casts.append((scr["br_bf"], inp["L1_ffn_br"].unsqueeze(0)))
        casts.append((scr["bd_bf"], inp["L1_ffn_b_down"]))
    ctok = {}

    if stage in ("all", "casts"):
        @block.gpsimd
        def _(g):
            for j, (dst, src) in enumerate(casts):
                ctok[j] = ctx.inc(g.dma_start(out=dst, in_=src), "dG", 16)
        ctx.casts_done = ctx.n["dG"]
    if stage == "casts":
        return

    if "moe" in need:
        with ExitStack() as ls:
            en = ls.enter_context
            swg = [en(ctx.nc.sbuf_tensor(f"prg{i}", [P, DK, IHW], BF16)) for i in range(2)]
            swu = [en(ctx.nc.sbuf_tensor(f"pru{i}", [P, DK, IHW], BF16)) for i in range(2)]
            swd = [en(ctx.nc.sbuf_tensor(f"prd{i}", [P, ITH, D], BF16)) for i in range(2)]
            sin = [en(ctx.nc.sbuf_tensor(f"prf{i}", [P, D], F32)) for i in range(4)]
            sbin = [en(ctx.nc.sbuf_tensor(f"prc{i}", [P, IHW], BF16)) for i in range(4)]
            pwt = [en(ctx.nc.psum_tensor(f"prp{i}", [P, P], BF16)) for i in range(8)]
            tb = {k: ctx.n[k] for k in (SEMS + MOE_SEMS)}

            LDSL = ["d0", "d1", "e1", "e3"]
            tiles_w = [("g", it) for it in range(ITH)] + [("u", it) for it in range(ITH)] + [("d", kd) for kd in range(DK)]
            NTIL = len(tiles_w)
            sched = []
            for w in range(2 * E):
                e, ih = divmod(w, 2)
                for (kind, idx) in tiles_w:
                    sched.append((w, e, ih, kind, idx))
            G = len(sched)
            ntr_of = lambda kind: DK if kind in ("g", "u") else ITH
            cum_tr = [0] * (G + 1)
            cast_pre = [0] * (G + 1)
            for g in range(G):
                cum_tr[g + 1] = cum_tr[g] + ntr_of(sched[g][3])
                cast_pre[g + 1] = cast_pre[g] + (1 if sched[g][3] in ("g", "u") else 0)
            GSL = ["w0", "w1", "w2", "w3"]
            ld_cnt = {}
            gld_cnt = {}
            _lc = {0: 0, 1: 0, 2: 0, 3: 0}
            _gc = {0: 0, 1: 0, 2: 0, 3: 0}
            for g in range(G):
                if sched[g][3] in ("g", "u"):
                    _lc[g % 4] += 1
                    ld_cnt[g] = _lc[g % 4]
                else:
                    _gc[g % 4] += 1
                    gld_cnt[g] = _gc[g % 4]

            def src_ap(g):
                w, e, ih, kind, idx = sched[g]
                ihs = ih * IHW
                if kind == "g":
                    return inp["L0_ffn_gate_w"][e][ihs + idx * P:ihs + (idx + 1) * P, :]
                if kind == "u":
                    return inp["L0_ffn_up_w"][e][ihs + idx * P:ihs + (idx + 1) * P, :]
                return inp["L0_ffn_down_w"][e][idx * P:(idx + 1) * P, ihs:ihs + IHW]

            def wid(g):
                return sched[g][0]

            wsem = lambda w: "xo" if w % 2 == 0 else "e2"

            @block.sync
            def _(s):
                s.wait_ge(ctx.sems["dG"], ctx.casts_done)
                for g in range(G):
                    w = wid(g)
                    t = g % NTIL
                    if t == 0 and w >= 1:
                        pw = w - 1
                        e_, ih_ = divmod(pw, 2)
                        ihs_ = ih_ * IHW
                        bb = pw % 2
                        s.wait_ge(ctx.sems["vS"], tb["vS"] + cum_tr[NTIL * (pw + 1)])
                        ctx.inc(s.dma_start(out=r_pkm(scr["gwT"][e_][:, ihs_:ihs_ + IHW]), in_=swg[bb][:]), wsem(pw), 16)
                        ctx.inc(s.dma_start(out=r_pkm(scr["uwT"][e_][:, ihs_:ihs_ + IHW]), in_=swu[bb][:]), wsem(pw), 16)
                        ctx.inc(s.dma_start(out=r_pkm(scr["dwT"][e_][ihs_:ihs_ + IHW, :]), in_=swd[bb][:]), wsem(pw), 16)
                    if sched[g][3] not in ("g", "u"):
                        continue
                    if g >= 4:
                        s.wait_ge(ctx.sems["sS"], tb["sS"] + cast_pre[g - 3])
                    ctx.inc(s.dma_start(out=sin[g % 4][:, 0:D], in_=src_ap(g)), LDSL[g % 4], 16)
                pw = 2 * E - 1
                e_, ih_ = divmod(pw, 2)
                ihs_ = ih_ * IHW
                bb = pw % 2
                s.wait_ge(ctx.sems["vS"], tb["vS"] + cum_tr[G])
                ctx.inc(s.dma_start(out=r_pkm(scr["gwT"][e_][:, ihs_:ihs_ + IHW]), in_=swg[bb][:]), wsem(pw), 16)
                ctx.inc(s.dma_start(out=r_pkm(scr["uwT"][e_][:, ihs_:ihs_ + IHW]), in_=swu[bb][:]), wsem(pw), 16)
                ctx.inc(s.dma_start(out=r_pkm(scr["dwT"][e_][ihs_:ihs_ + IHW, :]), in_=swd[bb][:]), wsem(pw), 16)
                s.wait_ge(ctx.sems["xo"], ctx.n["xo"])
                s.wait_ge(ctx.sems["e2"], ctx.n["e2"])

            @block.scalar
            def _(s):
                for g in range(G):
                    if sched[g][3] not in ("g", "u"):
                        continue
                    s.wait_ge(ctx.sems[LDSL[g % 4]], tb[LDSL[g % 4]] + 16 * ld_cnt[g])
                    if g >= 4:
                        s.wait_ge(ctx.sems["tS"], tb["tS"] + cum_tr[g - 3])
                    ctx.inc(s.activation(sbin[g % 4][:, 0:D], sin[g % 4][:, 0:D], AF.Copy), "sS", 1)

            @block.gpsimd
            def _(gp):
                for g in range(G):
                    if sched[g][3] in ("g", "u"):
                        continue
                    gp.wait_ge(ctx.sems["tS"], tb["tS"] + cum_tr[g - 3])
                    ctx.inc(gp.dma_start(out=sbin[g % 4][:, 0:IHW], in_=src_ap(g)), GSL[g % 4], 16)

            @block.tensor
            def _(t):
                for g in range(G):
                    if sched[g][3] in ("g", "u"):
                        ctx.wait(t, "sS", tb["sS"] + cast_pre[g + 1])
                    else:
                        t.wait_ge(ctx.sems[GSL[g % 4]], tb[GSL[g % 4]] + 16 * gld_cnt[g])
                    for k in range(ntr_of(sched[g][3])):
                        trn = cum_tr[g] + k
                        if trn >= 8:
                            t.wait_ge(ctx.sems["vS"], tb["vS"] + trn - 7)
                        ctx.inc(t.transpose(pwt[trn % 8][:], sbin[g % 4][:, k * P:(k + 1) * P], ctx.ident[:]), "tS", 1)

            @block.vector
            def _(v):
                lastw = -1
                for g in range(G):
                    w, e, ih, kind, idx = sched[g]
                    if w != lastw and w >= 2:
                        v.wait_ge(ctx.sems[wsem(w)], tb[wsem(w)] + 48 * (w // 2))
                    lastw = max(lastw, w)
                    for k in range(ntr_of(kind)):
                        trn = cum_tr[g] + k
                        ctx.wait(v, "tS", tb["tS"] + trn + 1)
                        src = pwt[trn % 8][:]
                        if kind == "g":
                            dst = swg[w % 2][:, k, idx * P:(idx + 1) * P]
                        elif kind == "u":
                            dst = swu[w % 2][:, k, idx * P:(idx + 1) * P]
                        else:
                            dst = swd[w % 2][:, k, idx * P:(idx + 1) * P]
                        ctx.inc(v.tensor_copy(dst, src), "vS", 1)

        with ExitStack() as ls:
            st8 = ls.enter_context(ctx.nc.sbuf_tensor("prr", [P, DK, 8], BF16))
            srwi = ls.enter_context(ctx.nc.sbuf_tensor("prri", [8, D], BF16))
            prw = ls.enter_context(ctx.nc.psum_tensor("prwp", [P, 8], BF16))
            tb = {k: ctx.n[k] for k in SEMS}

            @block.sync
            def _(s):
                s.wait_ge(ctx.sems["dG"], ctx.casts_done)
                ctx.inc(s.dma_start(out=srwi[:], in_=scr["rw_bf"][:]), "dS", 16)
                s.wait_ge(ctx.sems["dS"], ctx.n["dS"])

            @block.tensor
            def _(t):
                ctx.wait(t, "dS", tb["dS"] + 16)
                for k in range(DK):
                    if k > 0:
                        ctx.wait(t, "vS", tb["vS"] + k)
                    ctx.inc(t.transpose(prw[:], srwi[:, k * P:(k + 1) * P], ctx.ident[0:8, 0:8]), "tS", 1)

            @block.vector
            def _(v):
                for k in range(DK):
                    ctx.wait(v, "tS", tb["tS"] + k + 1)
                    ctx.inc(v.tensor_copy(st8[:, k, :], prw[:]), "vS", 1)

            @block.sync
            def _(s):
                ctx.wait(s, "vS", tb["vS"] + DK)
                ctx.inc(s.dma_start(out=r_pkm(scr["rwT"]), in_=st8[:]), "dS", 16)

    if "perf" in need:
        with ExitStack() as ls:
            stg = ls.enter_context(ctx.nc.sbuf_tensor("pstg", [P, D], BF16))
            @block.sync
            def _(s):
                s.wait_ge(ctx.sems["dG"], ctx.casts_done)
                for k in range(DK):
                    s.wait_ge(ctx.sems["dS"], ctx.n["dS"])
                    ctx.inc(s.dma_start(out=stg[:], in_=scr["qkv_bf"][2048:3072, k * P:(k + 1) * P], transpose=True), "dS", 16)
                    s.wait_ge(ctx.sems["dS"], ctx.n["dS"])
                    ctx.inc(s.dma_start(out=scr["Wbig"][k * P:(k + 1) * P, 0:D], in_=stg[:]), "dS", 16)
                for k in range(DK):
                    s.wait_ge(ctx.sems["dS"], ctx.n["dS"])
                    ctx.inc(s.dma_start(out=stg[:], in_=scr["proj_bf"][:, k * P:(k + 1) * P], transpose=True), "dS", 16)
                    s.wait_ge(ctx.sems["dS"], ctx.n["dS"])
                    ctx.inc(s.dma_start(out=scr["projT_aug"][k * P:(k + 1) * P, 0:D], in_=stg[:]), "dS", 16)
                s.wait_ge(ctx.sems["dS"], ctx.n["dS"])

        with ExitStack() as ls:
            spm = ls.enter_context(ctx.nc.sbuf_tensor("ppm", [64, 64], F32))
            spmb = ls.enter_context(ctx.nc.sbuf_tensor("ppmb", [64, 64], BF16))
            spmT = ls.enter_context(ctx.nc.sbuf_tensor("ppmT", [64, 64], BF16))
            sqh = ls.enter_context(ctx.nc.sbuf_tensor("pqh", [64, D], BF16))
            sphi = ls.enter_context(ctx.nc.sbuf_tensor("pphi", [64, D], BF16))
            sbia = ls.enter_context(ctx.nc.sbuf_tensor("pbia", [64, 1], BF16))
            sbiaf = ls.enter_context(ctx.nc.sbuf_tensor("pbiaf", [64, 1], F32))
            sbrow = ls.enter_context(ctx.nc.sbuf_tensor("pbrow", [1, 64], BF16))
            stg2 = ls.enter_context(ctx.nc.sbuf_tensor("pstg2", [P, DK, 64], BF16))
            pph = ls.enter_context(ctx.nc.psum_tensor("ppph", [64, D], F32))
            ppT = ls.enter_context(ctx.nc.psum_tensor("pppT", [64, 64], BF16))
            ppT2 = ls.enter_context(ctx.nc.psum_tensor("pppT2", [P, 64], BF16))
            pbr = ls.enter_context(ctx.nc.psum_tensor("ppbr", [1, 64], F32))

            @block.sync
            def _(s):
                ctx.inc(s.dma_start(out=spm[:], in_=inp["L0_attn_projection_matrix"][:]), "dS", 16)
                s.wait_ge(ctx.sems["dS"], ctx.n["dS"])
            t_pm = ctx.n["dS"]

            @block.vector
            def _(v):
                ctx.wait(v, "dS", t_pm)
                ctx.inc(v.tensor_copy(spmb[:], spm[:]), "vS", 1)
            v_pm = ctx.n["vS"]

            @block.tensor
            def _(t):
                ctx.wait(t, "vS", v_pm)
                ctx.inc(t.transpose(ppT[:], spmb[:], ctx.ident[0:64, 0:64]), "tS", 1)
            t_pmT = ctx.n["tS"]

            @block.vector
            def _(v):
                ctx.wait(v, "tS", t_pmT)
                ctx.inc(v.tensor_copy(spmT[:], ppT[:]), "vS", 1)
            v_pmT = ctx.n["vS"]

            for hp in range(2 * NH):
                part, h = divmod(hp, NH)
                row0 = part * D + h * HD
                col0 = D + part * D + h * HD
                tb = {k: ctx.n[k] for k in SEMS}

                @block.sync
                def _(s):
                    ctx.wait(s, "tS", tb["tS"])
                    ctx.wait(s, "vS", tb["vS"])
                    s.wait_ge(ctx.sems["dS"], tb["dS"])
                    ctx.inc(s.dma_start(out=sqh[:], in_=scr["qkv_bf"][row0:row0 + HD, :]), "dS", 16)
                    ctx.inc(s.dma_start(out=sbiaf[:], in_=inp["L0_attn_qkv_b"][row0:row0 + HD].unsqueeze(1)), "dS", 16)
                    s.wait_ge(ctx.sems["dS"], ctx.n["dS"])

                @block.vector
                def _(v):
                    ctx.wait(v, "dS", tb["dS"] + 32)
                    ctx.inc(v.tensor_copy(sbia[:], sbiaf[:]), "vS", 1)

                @block.tensor
                def _(t):
                    ctx.wait(t, "dS", tb["dS"] + 32)
                    ctx.wait(t, "vS", max(v_pmT, tb["vS"] + 1))
                    ctx.wait(t, "sS", tb["sS"])
                    t.matmul(pph[:, 0:512], spmT[:], sqh[:, 0:512], start=True, stop=True)
                    ctx.inc(t.matmul(pph[:, 512:1024], spmT[:], sqh[:, 512:1024], start=True, stop=True), "tS", 1)
                    ctx.inc(t.matmul(pbr[:], sbia[:], spmT[:], start=True, stop=True), "tS", 1)

                @block.scalar
                def _(sc):
                    ctx.wait(sc, "tS", tb["tS"] + 2)
                    ctx.inc(sc.activation(sphi[:], pph[:], AF.Copy, scale=0.125), "sS", 1)
                    ctx.inc(sc.activation(sbrow[:], pbr[:], AF.Copy, scale=0.125), "sS", 1)

                @block.tensor
                def _(t):
                    ctx.wait(t, "sS", tb["sS"] + 2)
                    for k in range(DK):
                        if k > 0:
                            ctx.wait(t, "vS", tb["vS"] + 1 + k)
                        ctx.inc(t.transpose(ppT2[:], sphi[:, k * P:(k + 1) * P], ctx.ident[0:64, 0:64]), "tS", 1)

                @block.vector
                def _(v):
                    for k in range(DK):
                        ctx.wait(v, "tS", tb["tS"] + 2 + k + 1)
                        ctx.inc(v.tensor_copy(stg2[:, k, :], ppT2[:]), "vS", 1)

                @block.sync
                def _(s):
                    ctx.wait(s, "vS", tb["vS"] + 1 + DK)
                    ctx.inc(s.dma_start(out=r_pkm(scr["Wbig"][0:D, col0:col0 + HD]), in_=stg2[:]), "dS", 16)
                    ctx.wait(s, "sS", tb["sS"] + 2)
                    ctx.inc(s.dma_start(out=scr["Wbig"][1024:1025, col0:col0 + HD], in_=sbrow[:]), "dS", 16)

    if "l1att" in need:
        _emit_l1att_prep(ctx, block, scr)
    if "l1ffn" in need:
        _emit_l1ffn_prep(ctx, block, scr)


def _emit_l1att_prep(ctx, block, scr):
    inp = ctx.inputs
    with ExitStack() as ls:
        swo = ls.enter_context(ctx.nc.sbuf_tensor("lwo", [P, D], F32))
        sgam = ls.enter_context(ctx.nc.sbuf_tensor("lgam", [P, DK], F32))
        sbet = ls.enter_context(ctx.nc.sbuf_tensor("lbet", [P, DK], BF16))
        sbetf = ls.enter_context(ctx.nc.sbuf_tensor("lbetf", [P, DK], F32))
        swof = ls.enter_context(ctx.nc.sbuf_tensor("lwof", [P, D], BF16))
        swob = ls.enter_context(ctx.nc.sbuf_tensor("lwob", [P, DK, D], BF16))
        sbrow = ls.enter_context(ctx.nc.sbuf_tensor("lbrow", [1, D], BF16))
        pbr = ls.enter_context(ctx.nc.psum_tensor("lpbr", [1, D], F32))
        tb = {k: ctx.n[k] for k in SEMS}

        @block.sync
        def _(s):
            s.wait_ge(ctx.sems["dS"], tb["dS"])
            ctx.inc(s.dma_start(out=sgam[:], in_=r_pk(inp["L1_attn_gamma"]), allow_slow_non_contiguous=True), "dS", 16)
            ctx.inc(s.dma_start(out=sbetf[:], in_=r_pk(inp["L1_attn_beta"]), allow_slow_non_contiguous=True), "dS", 16)
            s.wait_ge(ctx.sems["dS"], ctx.n["dS"])

        @block.vector
        def _(v):
            ctx.wait(v, "dS", tb["dS"] + 32)
            ctx.inc(v.tensor_copy(sbet[:], sbetf[:]), "vS", 1)

        for k in range(DK):
            kb = {kk: ctx.n[kk] for kk in SEMS}

            @block.sync
            def _(s):
                ctx.wait(s, "vS", kb["vS"])
                s.wait_ge(ctx.sems["dS"], kb["dS"])
                ctx.inc(s.dma_start(out=swo[:], in_=inp["L1_attn_Wo"][k * P:(k + 1) * P, :]), "dS", 16)
                s.wait_ge(ctx.sems["dS"], ctx.n["dS"])

            @block.vector
            def _(v):
                ctx.wait(v, "dS", kb["dS"] + 16)
                ctx.inc(v.tensor_scalar(swof[:], swo[:], sgam[:, k:k + 1], None, ALU.mult), "vS", 1)

            @block.sync
            def _(s):
                ctx.wait(s, "vS", kb["vS"] + 1)
                ctx.inc(s.dma_start(out=scr["WoP_aug"][k * P:(k + 1) * P, :], in_=swof[:]), "dS", 16)

        kb = {kk: ctx.n[kk] for kk in SEMS}

        @block.sync
        def _(s):
            ctx.wait(s, "dG", ctx.casts_done)
            s.wait_ge(ctx.sems["dS"], kb["dS"])
            ctx.inc(s.dma_start(out=swob[:], in_=r_pkm(scr["Wo_bf"])), "dS", 16)
            s.wait_ge(ctx.sems["dS"], ctx.n["dS"])

        @block.tensor
        def _(t):
            ctx.wait(t, "dS", kb["dS"] + 16)
            ctx.wait(t, "vS", kb["vS"])
            for k in range(DK):
                t.matmul(pbr[:, 0:512], sbet[:, k:k + 1], swob[:, k, 0:512], start=(k == 0), stop=(k == DK - 1))
            for k in range(DK):
                mm = t.matmul(pbr[:, 512:1024], sbet[:, k:k + 1], swob[:, k, 512:1024], start=(k == 0), stop=(k == DK - 1))
            ctx.inc(mm, "tS", 1)

        @block.vector
        def _(v):
            ctx.wait(v, "tS", kb["tS"] + 1)
            ctx.inc(v.tensor_copy(sbrow[:], pbr[:]), "vS", 1)

        @block.sync
        def _(s):
            ctx.wait(s, "vS", kb["vS"] + 1)
            ctx.inc(s.dma_start(out=scr["WoP_aug"][1024:1025, :], in_=sbrow[:]), "dS", 16)


def _emit_l1ffn_prep(ctx, block, scr):
    inp = ctx.inputs
    with ExitStack() as ls:
        swd = ls.enter_context(ctx.nc.sbuf_tensor("fwd", [P, DK, HD], F32))
        swdb = ls.enter_context(ctx.nc.sbuf_tensor("fwdb", [P, DK, HD], BF16))
        swdf = ls.enter_context(ctx.nc.sbuf_tensor("fwdf", [P, DK, HD], BF16))
        sg = ls.enter_context(ctx.nc.sbuf_tensor("fg", [P, DK], F32))
        sbf = ls.enter_context(ctx.nc.sbuf_tensor("fbf", [P, DK], F32))
        sb16 = ls.enter_context(ctx.nc.sbuf_tensor("fb16", [P, DK], BF16))
        sbd = ls.enter_context(ctx.nc.sbuf_tensor("fbd", [1, HD], BF16))
        srow = ls.enter_context(ctx.nc.sbuf_tensor("frow", [1, HD], BF16))
        pbr = ls.enter_context(ctx.nc.psum_tensor("fpbr", [1, HD], F32))
        tb0 = {k: ctx.n[k] for k in SEMS}

        @block.sync
        def _(s):
            s.wait_ge(ctx.sems["dS"], tb0["dS"])
            ctx.inc(s.dma_start(out=sg[:], in_=r_pk(inp["L1_ffn_ln_g"]), allow_slow_non_contiguous=True), "dS", 16)
            ctx.inc(s.dma_start(out=sbf[:], in_=r_pk(inp["L1_ffn_ln_b"]), allow_slow_non_contiguous=True), "dS", 16)
            s.wait_ge(ctx.sems["dS"], ctx.n["dS"])

        @block.vector
        def _(v):
            ctx.wait(v, "dS", tb0["dS"] + 32)
            ctx.inc(v.tensor_copy(sb16[:], sbf[:]), "vS", 1)

        for e in range(E):
            kb = {kk: ctx.n[kk] for kk in SEMS}

            @block.sync
            def _(s):
                ctx.wait(s, "vS", kb["vS"])
                ctx.wait(s, "tS", kb["tS"])
                s.wait_ge(ctx.sems["dS"], kb["dS"])
                ctx.inc(s.dma_start(out=swd[:], in_=r_pkm(inp["L1_ffn_W_down"][e])), "dS", 16)
                ctx.wait(s, "dG", ctx.casts_done)
                ctx.inc(s.dma_start(out=sbd[:], in_=scr["bd_bf"][e].unsqueeze(0)), "dS", 16)
                s.wait_ge(ctx.sems["dS"], ctx.n["dS"])

            @block.vector
            def _(v):
                ctx.wait(v, "dS", kb["dS"] + 32)
                for k in range(DK):
                    v.tensor_scalar(swdf[:, k, :], swd[:, k, :], sg[:, k:k + 1], None, ALU.mult)
                ctx.inc(v.tensor_copy(swdb[:], swd[:]), "vS", 1)

            @block.tensor
            def _(t):
                ctx.wait(t, "vS", kb["vS"] + 1)
                for k in range(DK):
                    t.matmul(pbr[:], sb16[:, k:k + 1], swdb[:, k, :], start=(k == 0), stop=False)
                ctx.inc(t.matmul(pbr[:], ctx.cones[0:1, 0:1], sbd[:], start=False, stop=True), "tS", 1)

            @block.vector
            def _(v):
                ctx.wait(v, "tS", kb["tS"] + 1)
                ctx.inc(v.tensor_copy(srow[:], pbr[:]), "vS", 1)

            @block.sync
            def _(s):
                ctx.wait(s, "vS", kb["vS"] + 2)
                ctx.inc(s.dma_start(out=r_pkm(scr["WdP"][e]), in_=swdf[:]), "dS", 16)
                ctx.inc(s.dma_start(out=scr["bdP"][e].unsqueeze(0), in_=srow[:]), "dS", 16)


def emit_zt(ctx, block, scr, szst, zw_fin):
    ztb = {k: ctx.n[k] for k in ["d2", "d3", "d4", "d5"]}

    @block.sync
    def _(s):
        s.wait_ge(ctx.sems["g3"], zw_fin[0])
        s.wait_ge(ctx.sems["g4"], zw_fin[1])
        for k in range(DK):
            ts_ = f"d{2 + k % 2}"
            ws_ = f"d{4 + k % 2}"
            if k >= 2:
                ctx.wait(s, ws_, ztb[ws_] + 16 * (k // 2))
            ctx.inc(s.dma_start(out=szst[k % 2][:], in_=scr["z"][:, k * P:(k + 1) * P], transpose=True), ts_, 16)
            ctx.wait(s, ts_, ztb[ts_] + 16 * (k // 2 + 1))
            ctx.inc(s.dma_start(out=scr["zT"][k * P:(k + 1) * P, :], in_=szst[k % 2][:]), ws_, 16)
        s.wait_ge(ctx.sems["d4"], ctx.n["d4"])
        s.wait_ge(ctx.sems["d5"], ctx.n["d5"])


def emit_ln(ctx, block, scr, src_ap, hold=None, defer_zt=False, xtok=None):
    ls = hold if hold is not None else ExitStack()
    try:
        en = ls.enter_context
        sx = [en(ctx.nc.sbuf_tensor(f"lnx{i}", [P, D], F32)) for i in range(3)]
        sz = [en(ctx.nc.sbuf_tensor(f"lnz{i}", [P, D], BF16)) for i in range(2)]
        st12 = [en(ctx.nc.sbuf_tensor(f"lnst{i}", [P, 12], F32)) for i in range(2)]
        smv = [en(ctx.nc.sbuf_tensor(f"lnmv{i}", [P, 2], F32)) for i in range(2)]
        ssd = [en(ctx.nc.sbuf_tensor(f"lnsd{i}", [P, 1], F32)) for i in range(2)]
        srv = [en(ctx.nc.sbuf_tensor(f"lnrv{i}", [P, 1], F32)) for i in range(2)]
        szst = [en(ctx.nc.sbuf_tensor(f"lnzt{i}", [P, T], BF16)) for i in range(2)]
        tb = {k: ctx.n[k] for k in SEMS}
        tok_x = {}
        tok_zw = {}

        @block.gpsimd
        def _(g):
            for i in range(NT):
                if xtok is not None:
                    ctx.wait(g, xtok[i][0], xtok[i][1])
                if i >= 3:
                    ctx.wait(g, "vS", tb["vS"] + 4 * (i - 2))
                sx_sem = f"g{i % 3}"
                tok_x[i] = (sx_sem, ctx.inc(g.dma_start(out=sx[i % 3][:], in_=src_ap[i * P:(i + 1) * P, :]), sx_sem, 16))
                if i >= 1:
                    ctx.wait(g, "vS", tb["vS"] + 4 * i)
                    zw_sem = f"g{3 + (i - 1) % 2}"
                    tok_zw[i - 1] = (zw_sem, ctx.inc(g.dma_start(out=scr["z"][(i - 1) * P:i * P, :], in_=sz[(i - 1) % 2][:]), zw_sem, 16))
            ctx.wait(g, "vS", tb["vS"] + 4 * NT)
            zw_sem = f"g{3 + (NT - 1) % 2}"
            tok_zw[NT - 1] = (zw_sem, ctx.inc(g.dma_start(out=scr["z"][(NT - 1) * P:NT * P, :], in_=sz[(NT - 1) % 2][:]), zw_sem, 16))

        @block.vector
        def _(v):
            for i in range(NT):
                j = i % 2
                ctx.wait(v, tok_x[i][0], tok_x[i][1])
                if i >= 2:
                    ctx.wait(v, "vS", tb["vS"] + 4 * (i - 2) + 2)
                v.bn_stats(st12[j][:, 0:6], sx[i % 3][:, 0:512])
                ctx.inc(v.bn_stats(st12[j][:, 6:12], sx[i % 3][:, 512:1024]), "vS", 1)
                ctx.wait(v, "vS", tb["vS"] + 4 * i + 1)
                if i >= 2:
                    ctx.wait(v, "sS", tb["sS"] + i - 1)
                ctx.inc(v.bn_aggr(smv[j][:], st12[j][:]), "vS", 1)
                ctx.wait(v, "sS", tb["sS"] + i + 1)
                ctx.inc(v.reciprocal(srv[j][:], ssd[j][:]), "vS", 1)
                ctx.wait(v, "vS", tb["vS"] + 4 * i + 3)
                if i >= 2:
                    ctx.wait(v, tok_zw[i - 2][0], tok_zw[i - 2][1])
                ctx.inc(v.tensor_scalar(sz[j][:], sx[i % 3][:], smv[j][:, 0:1], srv[j][:], ALU.subtract, ALU.mult), "vS", 1)

        @block.scalar
        def _(s):
            for i in range(NT):
                j = i % 2
                ctx.wait(s, "vS", tb["vS"] + 4 * i + 2)
                ctx.inc(s.activation(ssd[j][:], smv[j][:, 1:2], AF.Sqrt, bias=ctx.ceps[:, 0:1]), "sS", 1)

        zw_fin = [ctx.n["g3"], ctx.n["g4"]]
        if defer_zt:
            return (szst, zw_fin)
        emit_zt(ctx, block, scr, szst, zw_fin)
    finally:
        if hold is None:
            ls.close()
    return None


def emit_router(ctx, block, scr):
    with ExitStack() as ls:
        srw = ls.enter_context(ctx.nc.sbuf_tensor("rt_rw", [P, DK, 8], BF16))
        rtb = [ls.enter_context(ctx.nc.sbuf_tensor(f"rt_b{i}", [P, DK, 512], BF16)) for i in range(2)]
        slg = ls.enter_context(ctx.nc.sbuf_tensor("rt_lg", [32, 512], F32))
        sh0 = ls.enter_context(ctx.nc.sbuf_tensor("rt_s0", [32, 512], F32))
        sh1 = ls.enter_context(ctx.nc.sbuf_tensor("rt_s1", [32, 512], F32))
        smk = [ls.enter_context(ctx.nc.sbuf_tensor(f"rt_mk{i}", [8, 512], BF16)) for i in range(2)]
        plg = [ls.enter_context(ctx.nc.psum_tensor(f"rt_p{i}", [8, 512], F32)) for i in range(2)]
        tb = {k: ctx.n[k] for k in SEMS}
        NCH = T // 512

        tok_rtb = {}

        @block.sync
        def _(s):
            s.wait_ge(ctx.sems["dS"], tb["dS"])
            ctx.inc(s.dma_start(out=srw[:], in_=r_pkm(scr["rwT"])), "dS", 16)
            s.wait_ge(ctx.sems["dS"], ctx.n["dS"])
            for c in range(NCH):
                if c >= 2:
                    ctx.wait(s, "tS", tb["tS"] + c - 1)
                sem = f"d{c % 2}"
                tok_rtb[c] = (sem, ctx.inc(s.dma_start(out=rtb[c % 2][:], in_=r_pkm(scr["zT"][:, c * 512:(c + 1) * 512])), sem, 16))

        @block.tensor
        def _(t):
            for c in range(NCH):
                ctx.wait(t, "dS", tb["dS"] + 16)
                ctx.wait(t, tok_rtb[c][0], tok_rtb[c][1])
                if c >= 2:
                    ctx.wait(t, "vS", tb["vS"] + 1 + 9 * (c - 1) + 1)
                for k in range(DK):
                    mm = t.matmul(plg[c % 2][:], srw[:, k, :], rtb[c % 2][:, k, :], start=(k == 0), stop=(k == DK - 1))
                ctx.inc(mm, "tS", 1)

        import os
        rdbg = int(os.environ.get("RDBG", "0"))
        tok_smk = {c: (f"g{c % 2}", tb[f"g{c % 2}"] + 16 * (c // 2 + 1)) for c in range(NCH)}

        @block.vector
        def _(v):
            m1 = [(i ^ 4) if i < 8 else i for i in range(32)]
            m2 = [(i ^ 2) if i < 8 else i for i in range(32)]
            m3 = [(i ^ 1) if i < 8 else i for i in range(32)]
            v.memset(slg[:], 0.0)
            v.memset(sh0[:], 0.0)
            ctx.inc(v.memset(sh1[:], 0.0), "vS", 1)
            v.wait_ge(ctx.sems["vS"], tb["vS"] + 1)
            for c in range(NCH):
                vb = tb["vS"] + 1 + 9 * c
                ctx.wait(v, "tS", tb["tS"] + c + 1)
                if c >= 1:
                    ctx.wait(v, "vS", vb - 1)
                ctx.inc(v.tensor_copy(slg[0:8, :], plg[c % 2][:]), "vS", 1)
                ctx.wait(v, "vS", vb + 1)
                ctx.inc(v.stream_shuffle(sh0[:], slg[:], m1), "vS", 1)
                ctx.wait(v, "vS", vb + 2)
                ctx.inc(v.tensor_tensor(sh1[:], slg[:], sh0[:], ALU.max), "vS", 1)
                ctx.wait(v, "vS", vb + 3)
                ctx.inc(v.stream_shuffle(sh0[:], sh1[:], m2), "vS", 1)
                ctx.wait(v, "vS", vb + 4)
                ctx.inc(v.tensor_tensor(sh1[:], sh1[:], sh0[:], ALU.max), "vS", 1)
                ctx.wait(v, "vS", vb + 5)
                ctx.inc(v.stream_shuffle(sh0[:], sh1[:], m3), "vS", 1)
                ctx.wait(v, "vS", vb + 6)
                ctx.inc(v.tensor_tensor(sh1[:], sh1[:], sh0[:], ALU.max), "vS", 1)
                ctx.wait(v, "vS", vb + 7)
                if c >= 2:
                    ctx.wait(v, tok_smk[c - 2][0], tok_smk[c - 2][1])
                ctx.inc(v.tensor_tensor(smk[c % 2][:], slg[0:8, :], sh1[0:8, :], ALU.is_ge), "vS", 1)
                ctx.inc(v.tensor_copy(sh0[0:1, 0:1], sh1[0:1, 0:1]), "vS", 1)

        @block.gpsimd
        def _(g):
            for c in range(NCH):
                ctx.wait(g, "vS", tb["vS"] + 1 + 9 * c + 8)
                sem = f"g{c % 2}"
                assert tok_smk[c] == (sem, ctx.inc(g.dma_start(out=scr["msk"][:, c * 512:(c + 1) * 512], in_=smk[c % 2][:]), sem, 16))
            g.wait_ge(ctx.sems["g0"], ctx.n["g0"])
            g.wait_ge(ctx.sems["g1"], ctx.n["g1"])


MOE_SPARSE_MOE = True



def emit_moe_sweeps(ctx, block, scr, src_ap, dst_ap, half):
    NW = NQ * 2 * E
    b = {k: ctx.n[k] for k in (SEMS + MOE_SEMS)}

    with ExitStack() as ls:
        en = ls.enter_context
        swg = [en(ctx.nc.sbuf_tensor(f"mwg{i}", [P, DK, IHW], BF16)) for i in range(2)]
        swu = [en(ctx.nc.sbuf_tensor(f"mwu{i}", [P, DK, IHW], BF16)) for i in range(2)]
        swd = [en(ctx.nc.sbuf_tensor(f"mwd{i}", [P, ITH, D], BF16)) for i in range(2)]
        zq = en(ctx.nc.sbuf_tensor("mzq", [P, DK, QT], BF16))
        zpr = en(ctx.nc.sbuf_tensor("mzp", [P, DK, QT], BF16))
        mrq = en(ctx.nc.sbuf_tensor("mmrq", [1, E, QT], BF16))
        smkr = en(ctx.nc.sbuf_tensor("mmkr", [P, QT], BF16))
        sh = en(ctx.nc.sbuf_tensor("msh", [P, ITH, QT], BF16))
        sil = [en(ctx.nc.sbuf_tensor(f"msil{i}", [P, QT], F32)) for i in range(2)]
        sy = en(ctx.nc.sbuf_tensor("msy", [P, QTT, D], F32))
        sxq = [en(ctx.nc.sbuf_tensor(f"msxq{i}", [P, D], F32)) for i in range(2)]
        pg = [en(ctx.nc.psum_tensor(f"mpg{i}", [P, QT], F32)) for i in range(2)]
        pu = [en(ctx.nc.psum_tensor(f"mpu{i}", [P, QT], F32)) for i in range(2)]
        py = [en(ctx.nc.psum_tensor(f"mpy{i}", [P, D], F32)) for i in range(2)]

        sync_ops = []
        for w in range(NW):
            if w % (2 * E) == 0:
                sync_ops.append(("mr", w // (2 * E)))
                sync_ops.append(("ze", w // (2 * E)))
            sync_ops.append(("w", w))
        tok_ze = {}
        tok_mr = {}
        tok_w = {}

        @block.sync
        def _(s):
            for op in sync_ops:
                if op[0] == "mr":
                    q = op[1]
                    if q > 0:
                        s.wait_ge(ctx.sems["mk"], b["mk"] + 8 * q)
                    tok_mr[q] = ("d3", ctx.inc(s.dma_start(out=mrq[:], in_=scr["msk"][:, q * QT:(q + 1) * QT].unsqueeze(0)), "d3", 16))
                elif op[0] == "ze":
                    q = op[1]
                    if q > 0:
                        s.wait_ge(ctx.sems["zp"], b["zp"] + 8 * q)
                    tok_ze[q] = ("d2", ctx.inc(s.dma_start(out=zq[:], in_=r_pkm(scr["zT"][:, q * QT:(q + 1) * QT])), "d2", 16))
                else:
                    w = op[1]
                    q, r = divmod(w, 2 * E)
                    e, ih = divmod(r, 2)
                    bb = w % 2
                    sem = f"d{bb}"
                    if w >= 2:
                        s.wait_ge(ctx.sems["yr"], b["yr"] + 4 * (w - 1))
                    ihs = ih * IHW
                    ctx.inc(s.dma_start(out=swg[bb][:], in_=r_pkm(scr["gwT"][e][:, ihs:ihs + IHW])), sem, 16)
                    ctx.inc(s.dma_start(out=swu[bb][:], in_=r_pkm(scr["uwT"][e][:, ihs:ihs + IHW])), sem, 16)
                    tok_w[w] = (sem, ctx.inc(s.dma_start(out=swd[bb][:], in_=r_pkm(scr["dwT"][e][ihs:ihs + IHW, :])), sem, 16))

        tok_xl = {}
        tok_st = {}

        @block.gpsimd
        def _(g):
            for i in range(NQ * QTT):
                j = i % 2
                if i >= 2:
                    ctx.wait(g, tok_st[i - 2][0], tok_st[i - 2][1])
                sem = f"g{2 + j}"
                tok_xl[i] = (sem, ctx.inc(g.dma_start(out=sxq[j][:], in_=src_ap[i * P:(i + 1) * P, :]), sem, 16))
                ctx.wait(g, "xo", b["xo"] + i + 1)
                tok_st[i] = (f"g{j}", ctx.inc(g.dma_start(out=dst_ap[i * P:(i + 1) * P, :], in_=sxq[j][:]), f"g{j}", 16))
            g.wait_ge(ctx.sems["g0"], ctx.n["g0"])
            g.wait_ge(ctx.sems["g1"], ctx.n["g1"])

        @block.tensor
        def _(t):
            n = 0
            m = 0
            for w in range(NW):
                q, r = divmod(w, 2 * E)
                e, ih = divmod(r, 2)
                bb = w % 2
                zidx = q * E + e
                ctx.wait(t, tok_w[w][0], tok_w[w][1])
                if ih == 0:
                    if zidx == 0:
                        ctx.wait(t, tok_mr[0][0], tok_mr[0][1])
                        ctx.inc(t.matmul(pg[0][:], ctx.cones[0:1, 0:P], mrq[0:1, 0, :], start=True, stop=True), "mk", 1)
                    ctx.wait(t, "zp", b["zp"] + zidx + 1)
                for it in range(ITH):
                    par = n % 2
                    if n >= 2:
                        ctx.wait(t, "hs", b["hs"] + n - 1)
                    for k in range(DK):
                        t.matmul(pg[par][:], swg[bb][:, k, it * P:(it + 1) * P], zpr[:, k, :], start=(k == 0), stop=(k == DK - 1))
                    for k in range(DK):
                        mm = t.matmul(pu[par][:], swu[bb][:, k, it * P:(it + 1) * P], zpr[:, k, :], start=(k == 0), stop=(k == DK - 1))
                    ctx.inc(mm, "gu", 1)
                    n += 1
                ctx.wait(t, "hs", b["hs"] + n)
                if ih == 1 and zidx + 1 < NQ * E:
                    zn = zidx + 1
                    qn, en_ = divmod(zn, E)
                    ctx.wait(t, tok_mr[qn][0], tok_mr[qn][1])
                    ctx.inc(t.matmul(pg[0][:], ctx.cones[0:1, 0:P], mrq[0:1, en_, :], start=True, stop=True), "mk", 1)
                for ts in range(QTT):
                    pyb = py[m % 2]
                    if m >= 2:
                        ctx.wait(t, "yf", b["yf"] + m - 1)
                    for it in range(ITH):
                        t.matmul(pyb[:, 0:512], sh[:, it, ts * P:(ts + 1) * P], swd[bb][:, it, 0:512], start=(it == 0), stop=(it == ITH - 1))
                        mm = t.matmul(pyb[:, 512:1024], sh[:, it, ts * P:(ts + 1) * P], swd[bb][:, it, 512:1024], start=(it == 0), stop=(it == ITH - 1))
                    ctx.inc(mm, "yr", 1)
                    m += 1

        @block.scalar
        def _(s):
            for n in range(NW * ITH):
                par = n % 2
                ctx.wait(s, "gu", b["gu"] + n + 1)
                if n >= 2:
                    ctx.wait(s, "hs", b["hs"] + n - 1)
                ctx.inc(s.activation(sil[par][:], pg[par][:], AF.Silu), "sl", 1)

        def zbuild(v, zidx):
            q, e = divmod(zidx, E)
            ctx.wait(v, "mk", b["mk"] + zidx + 1)
            if zidx >= 1:
                ctx.wait(v, "zp", b["zp"] + zidx)
            ctx.inc(v.tensor_copy(smkr[:], pg[0][:]), "mke", 1)
            ctx.wait(v, "mke", b["mke"] + zidx + 1)
            ctx.wait(v, tok_ze[q][0], tok_ze[q][1])
            ctx.wait(v, "gu", b["gu"] + ITH * 2 * zidx)
            for k in range(DK):
                ins = v.tensor_tensor(zpr[:, k, :], zq[:, k, :], smkr[:], ALU.mult)
            ctx.inc(ins, "zp", 1)

        @block.vector
        def _(v):
            n = 0
            m = 0
            zbuild(v, 0)
            for w in range(NW):
                q, r = divmod(w, 2 * E)
                e, ih = divmod(r, 2)
                zidx = q * E + e
                for it in range(ITH):
                    par = n % 2
                    ctx.wait(v, "sl", b["sl"] + n + 1)
                    ctx.inc(v.tensor_tensor(sh[:, it, :], sil[par][:], pu[par][:], ALU.mult), "hs", 1)
                    n += 1
                if ih == 1 and zidx + 1 < NQ * E:
                    zbuild(v, zidx + 1)
                for ts in range(QTT):
                    pyb = py[m % 2]
                    ctx.wait(v, "yr", b["yr"] + m + 1)
                    if w % (2 * E) == 0 and q > 0:
                        ctx.wait(v, "xo", b["xo"] + 4 * q)
                    if w % (2 * E) == 0:
                        ctx.inc(v.tensor_copy(sy[:, ts, :], pyb[:]), "yf", 1)
                    else:
                        ctx.inc(v.tensor_tensor(sy[:, ts, :], sy[:, ts, :], pyb[:], ALU.add), "yf", 1)
                    m += 1
                if w % (2 * E) == 2 * E - 1:
                    for ts in range(QTT):
                        i = q * QTT + ts
                        ctx.wait(v, "yf", b["yf"] + 64 * (q + 1))
                        ctx.wait(v, tok_xl[i][0], tok_xl[i][1])
                        ctx.inc(v.scalar_tensor_tensor(sxq[i % 2][:], sy[:, ts, :], half, sxq[i % 2][:], ALU.mult, ALU.add), "xo", 1)


PHASE_NEED = {"p1": "moe", "p2": "perf", "p3": "moe", "p4": "l1ffn", "p5": "l1att", "p6": "l1ffn"}


def make_scratch(ctx, need):
    nc = ctx.nc
    scr = {}

    def dram(name, shape, dt):
        return nc.dram_tensor(name, shape, dt, kind="Internal").ap()

    scr["x"] = dram("x_scr", [T, D], F32)
    scr["idx"] = dram("idx_scr", [T], mybir.dt.uint32)
    scr["zc"] = dram("zc_scr", [E * 768, D], BF16)
    scr["yc"] = dram("yc_scr", [E * 768, D], BF16)
    scr["z"] = dram("z_scr", [T, D], BF16)
    scr["zT"] = dram("zT_scr", [D, T], BF16)
    if "moe" in need:
        scr["msk"] = dram("msk_scr", [E, T], BF16)
        scr["gwT"] = [dram(f"gwT{e}", [D, II], BF16) for e in range(E)]
        scr["uwT"] = [dram(f"uwT{e}", [D, II], BF16) for e in range(E)]
        scr["dwT"] = [dram(f"dwT{e}", [II, D], BF16) for e in range(E)]
        scr["rw_bf"] = dram("rwbf", [E, D], BF16)
        scr["rwT"] = dram("rwT", [D, E], BF16)
    if "perf" in need:
        scr["qkv_bf"] = dram("qkvbf", [3 * D, D], BF16)
        scr["proj_bf"] = dram("projbf", [D, D], BF16)
        scr["Wbig"] = dram("Wbig", [D + 1, 3 * D], BF16)
        scr["projT_aug"] = dram("projTa", [D + 1, D], BF16)
    if "l1att" in need:
        scr["Wq_bf"] = dram("Wqbf", [D, D], BF16)
        scr["Wk_bf"] = dram("Wkbf", [D, D], BF16)
        scr["Wv_bf"] = dram("Wvbf", [D, D], BF16)
        scr["Wg1_bf"] = dram("Wg1bf", [D, HD], BF16)
        scr["Wg2_bf"] = dram("Wg2bf", [HD, D], BF16)
        scr["Wo_bf"] = dram("Wobf", [D, D], BF16)
        scr["WoP_aug"] = dram("WoPa", [D + 1, D], BF16)
    if "l1ffn" in need:
        scr["Wup_bf"] = dram("Wupbf", [E, HD, D], BF16)
        scr["bup_bf"] = dram("bupbf", [E, D], BF16)
        scr["br_bf"] = dram("brbf", [1, E], BF16)
        scr["bd_bf"] = dram("bdbf", [E, HD], BF16)
        scr["WdP"] = [dram(f"WdP{e}", [D, HD], BF16) for e in range(E)]
        scr["bdP"] = dram("bdP", [E, HD], BF16)
    return scr


INPUT_NAMES = [
    "L0_attn_qkv_w", "L0_attn_qkv_b", "L0_attn_proj_w", "L0_attn_proj_b",
    "L0_attn_projection_matrix", "L0_ffn_router_w", "L0_ffn_gate_w",
    "L0_ffn_up_w", "L0_ffn_down_w", "L1_attn_Wq", "L1_attn_Wk", "L1_attn_Wv",
    "L1_attn_Wo", "L1_attn_Wg1", "L1_attn_Wg2", "L1_attn_gamma",
    "L1_attn_beta", "L1_ffn_ln_g", "L1_ffn_ln_b", "L1_ffn_bn_g",
    "L1_ffn_bn_b", "L1_ffn_bn_mean", "L1_ffn_bn_var", "L1_ffn_Wr",
    "L1_ffn_br", "L1_ffn_W_down", "L1_ffn_b_down", "L1_ffn_W_up",
    "L1_ffn_b_up",
]
INPUT_SHAPES = {
    "L0_attn_qkv_w": [3 * D, D], "L0_attn_qkv_b": [3 * D],
    "L0_attn_proj_w": [D, D], "L0_attn_proj_b": [D],
    "L0_attn_projection_matrix": [64, 64], "L0_ffn_router_w": [E, D],
    "L0_ffn_gate_w": [E, II, D], "L0_ffn_up_w": [E, II, D],
    "L0_ffn_down_w": [E, D, II], "L1_attn_Wq": [D, D], "L1_attn_Wk": [D, D],
    "L1_attn_Wv": [D, D], "L1_attn_Wo": [D, D], "L1_attn_Wg1": [D, HD],
    "L1_attn_Wg2": [HD, D], "L1_attn_gamma": [D], "L1_attn_beta": [D],
    "L1_ffn_ln_g": [D], "L1_ffn_ln_b": [D], "L1_ffn_bn_g": [D],
    "L1_ffn_bn_b": [D], "L1_ffn_bn_mean": [D], "L1_ffn_bn_var": [D],
    "L1_ffn_Wr": [D, E], "L1_ffn_br": [E], "L1_ffn_W_down": [E, D, HD],
    "L1_ffn_b_down": [E, HD], "L1_ffn_W_up": [E, HD, D], "L1_ffn_b_up": [E, D],
}


def build_nc(phases, parts="all", dbg=False):
    nc = bass.Bass("TRN2", target_bir_lowering=False, debug=False)
    stack = ExitStack()
    ctx = Ctx(NcProxy(nc), stack)
    need = sorted({PHASE_NEED[p] for p in phases})
    with stack:
        ctx.sem_init()
        for s in MOE_SEMS:
            ctx.sems[s] = stack.enter_context(nc.semaphore(s))
            ctx.n[s] = 0
        for nm in INPUT_NAMES:
            ctx.inputs[nm] = nc.dram_tensor(nm, INPUT_SHAPES[nm], F32, kind="ExternalInput").ap()
        xin = nc.dram_tensor("xin", [T, D], F32, kind="ExternalInput").ap()
        xout = nc.dram_tensor("xout", [T, D], F32, kind="ExternalOutput").ap()
        c_id = nc.dram_tensor("c_ident", [P, P], BF16, kind="ExternalInput").ap()
        c_on = nc.dram_tensor("c_ones", [1, 512], BF16, kind="ExternalInput").ap()
        c_onf = nc.dram_tensor("c_onesf", [1, 512], F32, kind="ExternalInput").ap()
        c_onc = nc.dram_tensor("c_onec", [P, 1], BF16, kind="ExternalInput").ap()
        c_cm = nc.dram_tensor("c_cmask", [4 * P, 512], BF16, kind="ExternalInput").ap()
        ctx.c_e8 = nc.dram_tensor("c_e8", [8, 2], F32, kind="ExternalInput").ap()
        ctx.c_ecap = nc.dram_tensor("c_ecap", [1, 256], F32, kind="ExternalInput").ap()
        c_tri = nc.dram_tensor("c_tri", [P, P], BF16, kind="ExternalInput").ap()
        c_rc = nc.dram_tensor("c_rotc", [SLEN, HD], F32, kind="ExternalInput").ap()
        c_rs = nc.dram_tensor("c_rots", [SLEN, HD], F32, kind="ExternalInput").ap()

        scr = make_scratch(ctx, need)
        ctx.ident = ctx.sb("c_ident_sb", [P, P], BF16)
        ctx.cones = ctx.sb("c_ones_sb", [1, 512], BF16)
        ctx.conesf = ctx.sb("c_onesf_sb", [1, 512], F32)
        ctx.conec = ctx.sb("c_onec_sb", [P, 1], BF16)
        ctx.cmask = ctx.sb("c_cmask_sb", [P, 4, 512], BF16)
        ctx.ctri = ctx.sb("c_tri_sb", [P, P], BF16)
        ctx.ceps = ctx.sb("c_eps_sb", [P, 2], F32)

        block = stack.enter_context(nc.Block())

        @block.sync
        def _(s):
            ctx.inc(s.dma_start(out=ctx.ident[:], in_=c_id[:]), "dS", 16)
            ctx.inc(s.dma_start(out=ctx.cones[:], in_=c_on[:]), "dS", 16)
            ctx.inc(s.dma_start(out=ctx.conesf[:], in_=c_onf[:]), "dS", 16)
            ctx.inc(s.dma_start(out=ctx.conec[:], in_=c_onc[:]), "dS", 16)
            ctx.inc(s.dma_start(out=ctx.cmask[:], in_=c_cm.rearrange("(d p) m -> p d m", p=P)), "dS", 16)
            ctx.inc(s.dma_start(out=ctx.ctri[:], in_=c_tri[:]), "dS", 16)
            s.wait_ge(ctx.sems["dS"], ctx.n["dS"])
        cload = ctx.n["dS"]

        @block.tensor
        def _(t):
            ctx.wait(t, "dS", cload)

        @block.vector
        def _(v):
            ctx.wait(v, "dS", cload)
            v.memset(ctx.ceps[:, 0:1], 1e-5)
            ctx.inc(v.memset(ctx.ceps[:, 1:2], 1e-6), "vS", 1)

        emit_prologue(ctx, block, scr, need, stage="casts")
        first_ln_done = False
        if phases and phases[0] in ("p1", "p3") and parts in ("all", "ln", "router", "sweeps"):
            with ExitStack() as lnscope:
                ztinfo = emit_ln(ctx, block, scr, xin, hold=lnscope, defer_zt=True)
                emit_prologue(ctx, block, scr, need, stage="rest")
                emit_zt(ctx, block, scr, *ztinfo)
                barrier(ctx, block)
            first_ln_done = True
        else:
            emit_prologue(ctx, block, scr, need, stage="rest")
            barrier(ctx, block)

        for pi, ph in enumerate(phases):
            src = xin if pi == 0 else scr["x"]
            dst = xout if pi == len(phases) - 1 else scr["x"]
            xtk = getattr(ctx, "last_xtok", None) if pi > 0 else None
            ctx.last_xtok = None

            def pre_ln():
                if xtk is None:
                    barrier(ctx, block)
                    emit_ln(ctx, block, scr, src)
                else:
                    emit_ln(ctx, block, scr, src, xtok=xtk)
                barrier(ctx, block)

            if ph in ("p1", "p3"):
                if parts in ("all", "ln", "router", "sweeps"):
                    if not (pi == 0 and first_ln_done):
                        pre_ln()
                if parts in ("all", "router", "sweeps"):
                    emit_router(ctx, block, scr)
                    barrier(ctx, block)
                if parts in ("all", "sweeps"):
                    if SPARSE_MOE:
                        with ExitStack() as ms:
                            sidx = ms.enter_context(nc.sbuf_tensor(f"sidx{pi}", [P, NT], U32))
                            emit_moe_indices(ctx, block, scr, sidx)
                            barrier(ctx, block)
                            emit_moe_sparse(ctx, block, scr, src, dst, 0.5, sidx)
                    else:
                        emit_moe_sweeps(ctx, block, scr, src, dst, 0.5)
            elif ph == "p2":
                pre_ln()
                emit_perf(ctx, block, scr, src, dst)
            elif ph in ("p4", "p6"):
                pre_ln()
                emit_l1ffn(ctx, block, scr, src, dst)
            elif ph == "p5":
                pre_ln()
                emit_l1att(ctx, block, scr, src, dst, (c_rc, c_rs))
        barrier(ctx, block)

        if dbg:
            dz = nc.dram_tensor("dbg_z", [T, D], BF16, kind="ExternalOutput").ap()
            dzT = nc.dram_tensor("dbg_zT", [D, T], BF16, kind="ExternalOutput").ap()
            dmsk = nc.dram_tensor("dbg_msk", [E, T], BF16, kind="ExternalOutput").ap()

            @block.sync
            def _(s):
                ctx.inc(s.dma_start(out=dz, in_=scr["z"]), "dS", 16)
                ctx.inc(s.dma_start(out=dzT, in_=scr["zT"]), "dS", 16)
                if "msk" in scr:
                    ctx.inc(s.dma_start(out=dmsk, in_=scr["msk"]), "dS", 16)

        @block.sync
        def _(s):
            ctx.wait(s, "dS", ctx.n["dS"])

        @block.gpsimd
        def _(g):
            ctx.wait(g, "dG", ctx.n["dG"])
        barrier(ctx, block)
    return nc


def _host_consts():
    cs = {}
    cs["c_ident"] = np.eye(P).astype(bf16)
    cs["c_ones"] = np.ones((1, 512), dtype=bf16)
    cs["c_onesf"] = np.ones((1, 512), dtype=np.float32)
    cs["c_onec"] = np.ones((P, 1), dtype=bf16)
    cm = np.zeros((4, P, 512), dtype=np.float32)
    for d in range(4):
        for sl in range(P):
            cm[d, sl, d * P + sl:] = 1.0
    cs["c_cmask"] = cm.reshape(4 * P, 512).astype(bf16)
    cs["c_e8"] = np.stack([np.ones(8), np.arange(8)], axis=1).astype(np.float32)
    cs["c_ecap"] = np.tile(np.arange(8) * CAP - 1.0, NT).reshape(1, 256).astype(np.float32)
    cs["c_tri"] = np.triu(np.ones((P, P))).astype(bf16)
    inv_freq = (10000.0 ** (-(np.arange(HD // 2, dtype=np.float64) * 2.0) / HD))
    pos = np.arange(SLEN, dtype=np.float64)
    ang = pos[:, None] * inv_freq[None, :]
    cosv = np.cos(ang).astype(np.float32)
    sinv = np.sin(ang).astype(np.float32)
    cs["c_rotc"] = np.concatenate([cosv, cosv], axis=1)
    cs["c_rots"] = np.concatenate([-sinv, sinv], axis=1)
    return cs


def run_phases(phases, x_full, weights):
    nc = build_nc(phases)
    cs = _host_consts()
    in_maps = []
    for c in range(NCORES):
        m = {nm: weights[nm] for nm in INPUT_NAMES}
        m.update(cs)
        m["xin"] = np.ascontiguousarray(x_full[2 * c:2 * c + 2].reshape(T, D))
        in_maps.append(m)
    res = run_bass_kernel_spmd(nc, in_maps, core_ids=list(range(NCORES)))
    out = np.stack([res.results[c]["xout"].reshape(2, SLEN, D) for c in range(NCORES)])
    return out.reshape(16, SLEN, D)


def kernel(**inputs):
    w = {nm: np.ascontiguousarray(np.asarray(inputs[nm], dtype=np.float32)) for nm in INPUT_NAMES}
    x = np.asarray(inputs["L0_attn_x"], dtype=np.float32)
    x = run_phases(["p1", "p2", "p3", "p4", "p5", "p6"], x, w)
    return x.astype(np.float32)


def emit_l1ffn(ctx, block, scr, src_ap, dst_ap):
    inp = ctx.inputs
    NCH = T // 512
    CPS = NCH // NSENT

    with ExitStack() as outer:
        eno = outer.enter_context
        sscl = eno(ctx.nc.sbuf_tensor("f4scl", [1, E, NSENT], F32))
        srep = eno(ctx.nc.sbuf_tensor("f4rep", [64, E, NSENT], F32))
        srows = eno(ctx.nc.sbuf_tensor("f4rows", [1, NSENT, E, P], BF16))

        with ExitStack() as ls:
            en = ls.enter_context
            rtb = [en(ctx.nc.sbuf_tensor(f"f4b{i}", [P, DK, 512], BF16)) for i in range(2)]
            szs = en(ctx.nc.sbuf_tensor("f4zs", [P, DK, NSENT], F32))
            sr8 = en(ctx.nc.sbuf_tensor("f4r8", [P, DK], F32))
            sWr = en(ctx.nc.sbuf_tensor("f4wr", [P, DK, E], F32))
            sbr = en(ctx.nc.sbuf_tensor("f4br", [1, E], F32))
            scols = en(ctx.nc.sbuf_tensor("f4cols", [P, DK, 6], F32))
            sbmm = en(ctx.nc.sbuf_tensor("f4bmm", [P, DK], F32))
            srs = en(ctx.nc.sbuf_tensor("f4rs", [P, DK], F32))
            ssd2 = en(ctx.nc.sbuf_tensor("f4sd", [P, DK], F32))
            szbn = en(ctx.nc.sbuf_tensor("f4zbn", [P, DK, NSENT], F32))
            slog = en(ctx.nc.sbuf_tensor("f4log", [1, E], F32))
            smx = en(ctx.nc.sbuf_tensor("f4mx", [1, 1], F32))
            sex = en(ctx.nc.sbuf_tensor("f4ex", [1, E], F32))
            ssum = en(ctx.nc.sbuf_tensor("f4sum", [1, 1], F32))
            sprob = en(ctx.nc.sbuf_tensor("f4prob", [1, E], F32))
            sgate = en(ctx.nc.sbuf_tensor("f4gate", [1, 1], F32))
            smsk1 = en(ctx.nc.sbuf_tensor("f4msk", [1, E], F32))
            plog = en(ctx.nc.psum_tensor("f4plog", [1, E], F32))
            prep = en(ctx.nc.psum_tensor("f4prep", [64, E], F32))
            prow = en(ctx.nc.psum_tensor("f4prow", [1, P], F32))
            tb = {k: ctx.n[k] for k in SEMS}
            tok_c = {}

            @block.sync
            def _(s):
                s.wait_ge(ctx.sems["dS"], tb["dS"])
                ctx.inc(s.dma_start(out=sWr[:], in_=r_pkm(inp["L1_ffn_Wr"])), "dS", 16)
                ctx.inc(s.dma_start(out=sbr[:], in_=inp["L1_ffn_br"].unsqueeze(0)), "dS", 16)
                for j, nm in enumerate(["ln_g", "ln_b", "bn_g", "bn_b", "bn_mean", "bn_var"]):
                    ctx.inc(s.dma_start(out=scols[:, :, j], in_=r_pk(inp[f"L1_ffn_{nm}"]), allow_slow_non_contiguous=True), "dS", 16)
                s.wait_ge(ctx.sems["dS"], ctx.n["dS"])
                for c in range(NCH):
                    if c >= 2:
                        ctx.wait(s, "vS", tb["vS"] + 2 * (c - 1))
                    sem = f"d{c % 2}"
                    tok_c[c] = (sem, ctx.inc(s.dma_start(out=rtb[c % 2][:], in_=r_pkm(scr["zT"][:, c * 512:(c + 1) * 512])), sem, 16))
            ldtok = tb["dS"] + 16 * 8

            @block.vector
            def _(v):
                ctx.wait(v, "dS", ldtok)
                for c in range(NCH):
                    s_ = c // CPS
                    ctx.wait(v, tok_c[c][0], tok_c[c][1])
                    ctx.inc(v.tensor_reduce(sr8[:].unsqueeze(2), rtb[c % 2][:], axis=AX.X, op=ALU.add), "vS", 1)
                    ctx.wait(v, "vS", ctx.n["vS"])
                    if c % CPS == 0:
                        ctx.inc(v.tensor_copy(szs[:, :, s_], sr8[:]), "vS", 1)
                    else:
                        ctx.inc(v.tensor_tensor(szs[:, :, s_], szs[:, :, s_], sr8[:], ALU.add), "vS", 1)
                    ctx.wait(v, "vS", ctx.n["vS"])
                ctx.inc(v.tensor_tensor(sbmm[:], scols[:, :, 1], scols[:, :, 4], ALU.subtract), "vS", 1)
                ctx.wait(v, "sS", tb["sS"] + 1)
                ctx.inc(v.reciprocal(srs[:], ssd2[:]), "vS", 1)
                ctx.wait(v, "vS", ctx.n["vS"])
                for s_ in range(NSENT):
                    ctx.inc(v.tensor_scalar(szbn[:, :, s_], szs[:, :, s_], 1.0 / SLEN, None, ALU.mult), "vS", 1)
                    ctx.wait(v, "vS", ctx.n["vS"])
                    for op, other in [(ALU.mult, scols[:, :, 0]), (ALU.add, sbmm[:]), (ALU.mult, srs[:]),
                                      (ALU.mult, scols[:, :, 2]), (ALU.add, scols[:, :, 3])]:
                        ctx.inc(v.tensor_tensor(szbn[:, :, s_], szbn[:, :, s_], other, op), "vS", 1)
                        ctx.wait(v, "vS", ctx.n["vS"])
            vA = ctx.n["vS"]

            @block.scalar
            def _(s):
                ctx.wait(s, "dS", ldtok)
                ctx.wait(s, "vS", tb["vS"] + 2 * NCH + 1)
                ctx.inc(s.activation(ssd2[:], scols[:, :, 5], AF.Sqrt, bias=ctx.ceps[:, 0:1]), "sS", 1)

            for s_ in range(NSENT):
                kb = {k: ctx.n[k] for k in SEMS}

                @block.tensor
                def _(t):
                    ctx.wait(t, "vS", vA if s_ == 0 else kb["vS"])
                    ctx.wait(t, "dS", ldtok)
                    for k in range(DK):
                        t.matmul(plog[:], szbn[:, k:k + 1, s_], sWr[:, k, :], start=(k == 0), stop=False)
                    ctx.inc(t.matmul(plog[:], ctx.conesf[0:1, 0:1], sbr[:], start=False, stop=True), "tS", 1)

                @block.vector
                def _(v):
                    ctx.wait(v, "tS", kb["tS"] + 1)
                    for fn in [lambda: v.tensor_copy(slog[:], plog[:]),
                               lambda: v.reduce_max(smx[:], slog[:], axis=AX.X),
                               lambda: v.tensor_scalar(sex[:], slog[:], smx[:], None, ALU.subtract)]:
                        ctx.inc(fn(), "vS", 1)
                        ctx.wait(v, "vS", ctx.n["vS"])

                @block.scalar
                def _(s):
                    ctx.wait(s, "vS", kb["vS"] + 3)
                    ctx.inc(s.activation(sex[:], sex[:], AF.Exp), "sS", 1)

                @block.vector
                def _(v):
                    ctx.wait(v, "sS", kb["sS"] + 1)
                    for fn in [lambda: v.reduce_sum(ssum[:], sex[:], axis=AX.X),
                               lambda: v.reciprocal(ssum[:], ssum[:]),
                               lambda: v.tensor_scalar(sprob[:], sex[:], ssum[:], None, ALU.mult),
                               lambda: v.reduce_max(sgate[:], sprob[:], axis=AX.X),
                               lambda: v.tensor_scalar(smsk1[:], sprob[:], sgate[:], None, ALU.is_ge),
                               lambda: v.tensor_scalar(sscl[:, :, s_], smsk1[:], sgate[:], 0.5, ALU.mult, ALU.mult)]:
                        ctx.inc(fn(), "vS", 1)
                        ctx.wait(v, "vS", ctx.n["vS"])

                @block.tensor
                def _(t):
                    ctx.wait(t, "vS", kb["vS"] + 9)
                    ctx.inc(t.matmul(prep[:], ctx.conesf[0:1, 0:64], sscl[0:1, :, s_], start=True, stop=True), "tS", 1)
                    for e in range(E):
                        ctx.wait(t, "vS", kb["vS"] + 10 + e)
                        ctx.inc(t.matmul(prow[:], sscl[0:1, e:e + 1, s_], ctx.conesf[0:1, 0:P], start=True, stop=True), "tS", 1)

                @block.vector
                def _(v):
                    ctx.wait(v, "tS", kb["tS"] + 2)
                    ctx.inc(v.tensor_copy(srep[:, :, s_], prep[:]), "vS", 1)
                    for e in range(E):
                        ctx.wait(v, "tS", kb["tS"] + 2 + e + 1)
                        ctx.inc(v.tensor_copy(srows[:, s_, e, :], prow[:]), "vS", 1)

        barrier(ctx, block)

        with ExitStack() as ls:
            en = ls.enter_context
            rtb = [en(ctx.nc.sbuf_tensor(f"f4c{i}", [P, DK, 512], BF16)) for i in range(2)]
            sWd = en(ctx.nc.sbuf_tensor("f4wd", [P, DK, E, HD], BF16))
            sbdr = en(ctx.nc.sbuf_tensor("f4bdr", [1, E, HD], BF16))
            sWu = en(ctx.nc.sbuf_tensor("f4wu", [64, E, D], BF16))
            sbur = en(ctx.nc.sbuf_tensor("f4bur", [1, E, D], BF16))
            shtf = en(ctx.nc.sbuf_tensor("f4htf", [64, 512], F32))
            sht = en(ctx.nc.sbuf_tensor("f4ht", [64, E, 512], BF16))
            sxf = [en(ctx.nc.sbuf_tensor(f"f4x{i}", [P, D], F32)) for i in range(2)]
            phT = [en(ctx.nc.psum_tensor(f"f4ph{i}", [64, 512], F32)) for i in range(2)]
            py = [en(ctx.nc.psum_tensor(f"f4py{i}", [P, D], F32)) for i in range(2)]
            b = {k: ctx.n[k] for k in (SEMS + MOE_SEMS)}
            tok_c = {}
            tok_xg = {}
            tok_st = {}

            @block.sync
            def _(s):
                s.wait_ge(ctx.sems["dS"], b["dS"])
                for e in range(E):
                    ctx.inc(s.dma_start(out=sWd[:, :, e, :], in_=r_pkm(scr["WdP"][e])), "dS", 16)
                    ctx.inc(s.dma_start(out=sbdr[:, e, :], in_=scr["bdP"][e].unsqueeze(0)), "dS", 16)
                ctx.inc(s.dma_start(out=sWu[:], in_=scr["Wup_bf"].rearrange("e p m -> p e m")), "dS", 16)
                ctx.inc(s.dma_start(out=sbur[:], in_=scr["bup_bf"].unsqueeze(0)), "dS", 16)
                s.wait_ge(ctx.sems["dS"], ctx.n["dS"])
                for gc in range(NCH):
                    if gc >= 2:
                        ctx.wait(s, "gu", b["gu"] + E * (gc - 1))
                    sem = f"d{gc % 2}"
                    tok_c[gc] = (sem, ctx.inc(s.dma_start(out=rtb[gc % 2][:], in_=r_pkm(scr["zT"][:, gc * 512:(gc + 1) * 512])), sem, 16))
            wtok = b["dS"] + 16 * 18

            @block.gpsimd
            def _(g):
                for i in range(NT):
                    j = i % 2
                    if i >= 2:
                        ctx.wait(g, tok_st[i - 2][0], tok_st[i - 2][1])
                    sem = f"g{2 + j}"
                    tok_xg[i] = (sem, ctx.inc(g.dma_start(out=sxf[j][:], in_=src_ap[i * P:(i + 1) * P, :]), sem, 16))
                    ctx.wait(g, "yf", b["yf"] + i + 1)
                    tok_st[i] = (f"g{j}", ctx.inc(g.dma_start(out=dst_ap[i * P:(i + 1) * P, :], in_=sxf[j][:]), f"g{j}", 16))

            @block.tensor
            def _(t):
                n = 0
                m = 0
                ctx.wait(t, "dS", wtok)
                for gc in range(NCH):
                    s_ = gc // CPS
                    ctx.wait(t, tok_c[gc][0], tok_c[gc][1])
                    for e in range(E):
                        par = n % 2
                        if n >= 2:
                            ctx.wait(t, "sl", b["sl"] + n - 1)
                        for k in range(DK):
                            t.matmul(phT[par][:], sWd[:, k, e, :], rtb[gc % 2][:, k, :], start=(k == 0), stop=False)
                        mm = t.matmul(phT[par][:], sbdr[:, e, :], ctx.cones[0:1, 0:512], start=False, stop=True)
                        ctx.inc(mm, "gu", 1)
                        n += 1
                    ctx.wait(t, "hs", b["hs"] + E * (gc + 1))
                    for ts in range(4):
                        pyb = py[m % 2]
                        if m >= 2:
                            ctx.wait(t, "yf", b["yf"] + m - 1)
                        for e in range(E):
                            t.matmul(pyb[:, 0:512], sht[:, e, ts * P:(ts + 1) * P], sWu[:, e, 0:512], start=(e == 0), stop=False)
                            t.matmul(pyb[:, 512:1024], sht[:, e, ts * P:(ts + 1) * P], sWu[:, e, 512:1024], start=(e == 0), stop=False)
                            t.matmul(pyb[:, 0:512], srows[:, s_, e, :], sbur[:, e, 0:512], start=False, stop=(e == E - 1))
                            mm = t.matmul(pyb[:, 512:1024], srows[:, s_, e, :], sbur[:, e, 512:1024], start=False, stop=(e == E - 1))
                        ctx.inc(mm, "yr", 1)
                        m += 1

            @block.scalar
            def _(s):
                for n in range(NCH * E):
                    par = n % 2
                    ctx.wait(s, "gu", b["gu"] + n + 1)
                    if n >= 1:
                        ctx.wait(s, "hs", b["hs"] + n)
                    ctx.inc(s.activation(shtf[:], phT[par][:], AF.Relu), "sl", 1)

            @block.vector
            def _(v):
                n = 0
                m = 0
                for gc in range(NCH):
                    s_ = gc // CPS
                    for e in range(E):
                        ctx.wait(v, "sl", b["sl"] + n + 1)
                        if gc >= 1 and e == 0:
                            ctx.wait(v, "yr", b["yr"] + 4 * gc)
                        ctx.inc(v.tensor_scalar(sht[:, e, :], shtf[:], srep[:, e:e + 1, s_], None, ALU.mult), "hs", 1)
                        n += 1
                    for ts in range(4):
                        i = gc * 4 + ts
                        ctx.wait(v, "yr", b["yr"] + m + 1)
                        ctx.wait(v, tok_xg[i][0], tok_xg[i][1])
                        ctx.inc(v.tensor_tensor(sxf[i % 2][:], sxf[i % 2][:], py[m % 2][:], ALU.add), "yf", 1)
                        m += 1

            ctx.last_xtok = dict(tok_st)


def _tok24(nh):
    return 24 * (nh // NH) + nh % NH


def _l5_e2(nh):
    return 96 * (nh // NH) + 4 * (nh % NH)


def emit_l1att(ctx, block, scr, src_ap, dst_ap, rot):
    c_rc, c_rs = rot
    NTS = SLEN // P

    with ExitStack() as outer:
        eno = outer.enter_context
        sv = eno(ctx.nc.sbuf_tensor("a5v", [P, NTS, D], BF16))
        sqT = eno(ctx.nc.sbuf_tensor("a5qT", [P, NH // 2, SLEN], BF16))
        skT = eno(ctx.nc.sbuf_tensor("a5kT", [P, NH // 2, SLEN], BF16))
        sg1T = eno(ctx.nc.sbuf_tensor("a5g1", [64, 2 * 4, 512], BF16))
        srotc = [eno(ctx.nc.sbuf_tensor(f"a5rc{i}", [P, HD], F32)) for i in range(2)]
        srots = [eno(ctx.nc.sbuf_tensor(f"a5rs{i}", [P, HD], F32)) for i in range(2)]
        sWg2 = eno(ctx.nc.sbuf_tensor("a5wg2", [64, D], BF16))
        sWoP = eno(ctx.nc.sbuf_tensor("a5wop", [P, DK, D], BF16))
        sWoB = eno(ctx.nc.sbuf_tensor("a5wob", [1, D], BF16))

        for s_ in range(NSENT):
            with ExitStack() as ls:
                en = ls.enter_context
                sWq = en(ctx.nc.sbuf_tensor(f"s{s_}a5wq", [P, DK, D], BF16))
                sWk = en(ctx.nc.sbuf_tensor(f"s{s_}a5wk", [P, DK, D], BF16))
                sWv = en(ctx.nc.sbuf_tensor(f"s{s_}a5wv", [P, DK, D], BF16))
                sWg1 = en(ctx.nc.sbuf_tensor(f"s{s_}a5wg1", [P, DK, HD], BF16))
                szt = [en(ctx.nc.sbuf_tensor(f"s{s_}a5zt{i}", [P, DK, P], BF16)) for i in range(2)]
                rtb = [en(ctx.nc.sbuf_tensor(f"s{s_}a5rt0", [P, DK, 512], BF16))] * 2
                stmpf = en(ctx.nc.sbuf_tensor(f"s{s_}a5tmf", [P, D], F32))
                sswap = en(ctx.nc.sbuf_tensor(f"s{s_}a5swp", [P, D], F32))
                srotq = en(ctx.nc.sbuf_tensor(f"s{s_}a5rq", [P, D], BF16))
                pqkv = [en(ctx.nc.psum_tensor(f"s{s_}a5pq{i}", [P, D], F32)) for i in range(2)]
                ptr = [en(ctx.nc.psum_tensor(f"s{s_}a5ptr{i}", [64, P], BF16)) for i in range(2)]
                pg1 = en(ctx.nc.psum_tensor(f"s{s_}a5pg1", [64, 512], F32))
                b = {k: ctx.n[k] for k in (SEMS + MOE_SEMS)}
                tok_z = {}
                tok_g = {}

                def qkv_free(n):
                    m = n - 2
                    i2, p2 = divmod(m, 3)
                    if p2 <= 1:
                        return ("sl", b["sl"] + 2 * i2 + p2 + 1)
                    return ("hs", b["hs"] + i2 + 1)

                @block.sync
                def _(s):
                    s.wait_ge(ctx.sems["dS"], b["dS"])
                    if s_ == 0:
                        ctx.inc(s.dma_start(out=sWg2[:], in_=scr["Wg2_bf"][:]), "dS", 16)
                        ctx.inc(s.dma_start(out=sWoP[:], in_=r_pkm(scr["WoP_aug"][0:D, :])), "dS", 16)
                        ctx.inc(s.dma_start(out=sWoB[:], in_=scr["WoP_aug"][1024:1025, :]), "dS", 16)
                    ctx.inc(s.dma_start(out=sWq[:], in_=r_pkm(scr["Wq_bf"])), "dS", 16)
                    ctx.inc(s.dma_start(out=sWk[:], in_=r_pkm(scr["Wk_bf"])), "dS", 16)
                    ctx.inc(s.dma_start(out=sWv[:], in_=r_pkm(scr["Wv_bf"])), "dS", 16)
                    ctx.inc(s.dma_start(out=sWg1[:], in_=r_pkm(scr["Wg1_bf"])), "dS", 16)
                    s.wait_ge(ctx.sems["dS"], ctx.n["dS"])
                    for i in range(NTS):
                        if i >= 2:
                            ctx.wait(s, "gu", b["gu"] + 3 * (i - 1))
                        sem = f"d{i % 2}"
                        tok_z[i] = (sem, ctx.inc(s.dma_start(out=szt[i % 2][:], in_=r_pkm(scr["zT"][:, (s_ * SLEN + i * P):(s_ * SLEN + (i + 1) * P)])), sem, 16))
                        rsem = f"d{4 + i % 2}"
                        if i >= 2:
                            ctx.wait(s, "zp", b["zp"] + 2 * i - 2)
                        s.dma_start(out=srotc[i % 2][:], in_=c_rc[i * P:(i + 1) * P, :]).then_inc(ctx.sems[rsem], 16)
                        ctx.n[rsem] += 16
                        tok_r = ctx.inc(s.dma_start(out=srots[i % 2][:], in_=c_rs[i * P:(i + 1) * P, :]), rsem, 16)
                        tok_z[(i, "rot")] = (rsem, tok_r)
                    for c in range(4):
                        if c >= 1:
                            ctx.wait(s, "yr", b["yr"] + c)
                        sem = "d2"
                        tok_g[c] = (sem, ctx.inc(s.dma_start(out=rtb[0][:], in_=r_pkm(scr["zT"][:, (s_ * SLEN + c * 512):(s_ * SLEN + (c + 1) * 512)])), sem, 16))
                wtok = ctx.n["dS"]

                @block.tensor
                def _(t):
                    ctx.wait(t, "dS", wtok)
                    ntr = 0
                    for i in range(NTS):
                        ctx.wait(t, tok_z[i][0], tok_z[i][1])
                        for part, W in [(0, sWq), (1, sWk), (2, sWv)]:
                            n = 3 * i + part
                            par = n % 2
                            if n >= 2:
                                fs, ft = qkv_free(n)
                                ctx.wait(t, fs, ft)
                            for half in range(2):
                                for k in range(DK):
                                    mm = t.matmul(pqkv[par][:, half * 512:(half + 1) * 512], szt[i % 2][:, k, :],
                                                  W[:, k, half * 512:(half + 1) * 512], start=(k == 0), stop=(k == DK - 1))
                            ctx.inc(mm, "gu", 1)
                        for qk in range(2):
                            ctx.wait(t, "zp", b["zp"] + 2 * i + qk + 1)
                            for h in range(NH):
                                if ntr >= 2:
                                    ctx.wait(t, "mke", b["mke"] + ntr - 1)
                                ctx.inc(t.transpose(ptr[ntr % 2][:], srotq[:, h * HD:(h + 1) * HD], ctx.ident[:]), "e1", 1)
                                ntr += 1
                    for c in range(4):
                        ctx.wait(t, tok_g[c][0], tok_g[c][1])
                        if c >= 1:
                            ctx.wait(t, "yf", b["yf"] + c)
                        for k in range(DK):
                            mm = t.matmul(pg1[:], sWg1[:, k, :], rtb[0][:, k, :], start=(k == 0), stop=(k == DK - 1))
                        ctx.inc(mm, "yr", 1)

                @block.scalar
                def _(s):
                    for i in range(NTS):
                        for qk in range(2):
                            n = 3 * i + qk
                            nz = 2 * i + qk
                            ctx.wait(s, "gu", b["gu"] + n + 1)
                            if nz >= 1:
                                ctx.wait(s, "zp", b["zp"] + nz)
                            ctx.inc(s.activation(stmpf[:], pqkv[n % 2][:], AF.Silu), "sl", 1)

                @block.vector
                def _(v):
                    ntr = 0
                    for i in range(NTS):
                        c1b = srotc[i % 2][:].unsqueeze(1).broadcast_to((P, NH, HD))
                        s1b = srots[i % 2][:].unsqueeze(1).broadcast_to((P, NH, HD))
                        for qk in range(2):
                            nz = 2 * i + qk
                            if qk == 0:
                                ctx.wait(v, tok_z[(i, "rot")][0], tok_z[(i, "rot")][1])
                            ctx.wait(v, "sl", b["sl"] + nz + 1)
                            if nz >= 1:
                                ctx.wait(v, "zp", b["zp"] + nz)
                            swv = sswap[:].rearrange("p (h two f) -> p h two f", two=2, f=HD // 2)
                            inv = stmpf[:].rearrange("p (h two f) -> p h two f", two=2, f=HD // 2)
                            v.tensor_copy(swv[:, :, 0, :], inv[:, :, 1, :])
                            ctx.inc(v.tensor_copy(swv[:, :, 1, :], inv[:, :, 0, :]), "vS", 1)
                            ctx.wait(v, "vS", ctx.n["vS"])
                            v.tensor_tensor(stmpf[:].rearrange("p (h f) -> p h f", f=HD),
                                            stmpf[:].rearrange("p (h f) -> p h f", f=HD), c1b, ALU.mult)
                            ctx.inc(v.tensor_tensor(sswap[:].rearrange("p (h f) -> p h f", f=HD),
                                                    sswap[:].rearrange("p (h f) -> p h f", f=HD), s1b, ALU.mult), "vS", 1)
                            ctx.wait(v, "vS", ctx.n["vS"])
                            if nz >= 1:
                                ctx.wait(v, "e1", b["e1"] + 16 * nz)
                            ctx.inc(v.tensor_tensor(srotq[:], stmpf[:], sswap[:], ALU.add), "zp", 1)
                            dstT = sqT if qk == 0 else skT
                            for h in range(NH):
                                ctx.wait(v, "e1", b["e1"] + ntr + 1)
                                ctx.inc(v.tensor_copy(dstT[64 * (h // 8):64 * (h // 8) + 64, h % 8, i * P:(i + 1) * P], ptr[ntr % 2][:]), "mke", 1)
                                ntr += 1
                        nv = 3 * i + 2
                        ctx.wait(v, "gu", b["gu"] + nv + 1)
                        ctx.inc(v.tensor_copy(sv[:, i, :], pqkv[nv % 2][:]), "hs", 1)
                    for c in range(4):
                        ctx.wait(v, "yr", b["yr"] + c + 1)
                        ctx.inc(v.tensor_copy(sg1T[:, s_ * 4 + c, :], pg1[:]), "yf", 1)

            barrier(ctx, block)

            with ExitStack() as ls:
                en = ls.enter_context
                sE = [en(ctx.nc.sbuf_tensor(f"s{s_}a5e{i}", [P, 512], BF16)) for i in range(2)]
                st64w = [en(ctx.nc.sbuf_tensor(f"s{s_}a5t64{i}", [64, 512], BF16)) for i in range(2)]
                sattn4 = en(ctx.nc.sbuf_tensor(f"s{s_}a5at", [P, 4, D], BF16))
                sgt = en(ctx.nc.sbuf_tensor(f"s{s_}a5gt", [P, D], BF16))
                sato = en(ctx.nc.sbuf_tensor(f"s{s_}a5ato", [P, D], F32))
                snorm = en(ctx.nc.sbuf_tensor(f"s{s_}a5nrm", [P, D], BF16))
                snormT = en(ctx.nc.sbuf_tensor(f"s{s_}a5nrT", [P, DK, P], BF16))
                st12 = en(ctx.nc.sbuf_tensor(f"s{s_}a5st", [P, 12], F32))
                smv = en(ctx.nc.sbuf_tensor(f"s{s_}a5mv", [P, 2], F32))
                ssd = en(ctx.nc.sbuf_tensor(f"s{s_}a5sd", [P, 1], F32))
                srv = en(ctx.nc.sbuf_tensor(f"s{s_}a5rv", [P, 1], F32))
                sxf = [en(ctx.nc.sbuf_tensor(f"s{s_}a5x{i}", [P, D], F32)) for i in range(2)]
                pE = [en(ctx.nc.psum_tensor(f"s{s_}a5pE{i}", [P, 512], F32)) for i in range(2)]
                pav = en(ctx.nc.psum_tensor(f"s{s_}a5pav", [64, 512], F32))
                ptn = en(ctx.nc.psum_tensor(f"s{s_}a5ptn", [P, P], BF16))
                pgt = en(ctx.nc.psum_tensor(f"s{s_}a5pgt", [P, D], F32))
                pxo = en(ctx.nc.psum_tensor(f"s{s_}a5pxo", [P, D], F32))
                b3 = {k: ctx.n[k] for k in (SEMS + MOE_SEMS)}
                tok_xg = {}
                tok_st = {}
                NTC4 = NTS // 4

                def cum_pairs(tc):
                    return 32 * tc * (tc + 1)


                @block.gpsimd
                def _(g):
                    for i in range(NTS):
                        gt = s_ * NTS + i
                        j = i % 2
                        if i >= 2:
                            ctx.wait(g, tok_st[i - 2][0], tok_st[i - 2][1])
                        sem = f"g{2 + j}"
                        tok_xg[i] = (sem, ctx.inc(g.dma_start(out=sxf[j][:], in_=src_ap[gt * P:(gt + 1) * P, :]), sem, 16))
                        ctx.wait(g, "yf", b3["yf"] + i + 1)
                        tok_st[i] = (f"g{j}", ctx.inc(g.dma_start(out=dst_ap[gt * P:(gt + 1) * P, :], in_=sxf[j][:]), f"g{j}", 16))

                if s_ == 0:
                    ctx.last_xtok = {}
                for i_ in range(NTS):
                    ctx.last_xtok[s_ * NTS + i_] = tok_st[i_]

                @block.tensor
                def _(t):
                    ne = 0
                    for tc in range(NTC4):
                        jmax = 4 * (tc + 1)
                        for h in range(NH):
                            pb = 64 * (h // 8)
                            hr = h % 8
                            nh = tc * NH + h
                            for j in range(jmax):
                                par = ne % 2
                                if ne >= 2:
                                    ctx.wait(t, "hs", b3["hs"] + ne - 1)
                                ctx.inc(t.matmul(pE[par][:], skT[pb:pb + 64, hr, j * P:(j + 1) * P],
                                                 sqT[pb:pb + 64, hr, tc * 512:(tc + 1) * 512], start=True, stop=True), "gu", 1)
                                if j >= 1:
                                    ctx.wait(t, "hs", b3["hs"] + ne)
                                    ctx.inc(t.matmul(pav[:], sv[:, j - 1, h * HD:(h + 1) * HD], sE[(ne - 1) % 2][:],
                                                     start=(j - 1 == 0), stop=False), "yr", 1)
                                ne += 1
                            ctx.wait(t, "hs", b3["hs"] + ne)
                            ctx.inc(t.matmul(pav[:], sv[:, jmax - 1, h * HD:(h + 1) * HD], sE[(ne - 1) % 2][:],
                                             start=(jmax - 1 == 0), stop=True), "yr", 1)
                            ctx.wait(t, "mke", b3["mke"] + 48 * tc + h + 1)
                            for q in range(4):
                                if h == 0 and q == 0:
                                    ctx.wait(t, "mke", b3["mke"] + 48 * tc)
                                if nh > 0 or q > 0:
                                    ctx.wait(t, "zp", b3["zp"] + 68 * tc + 4 * h + q)
                                ctx.inc(t.transpose(ptn[:, 0:64], st64w[nh % 2][:, q * P:(q + 1) * P], ctx.ident[0:64, 0:64]), "e2", 1)
                        for q in range(4):
                            i = tc * 4 + q
                            if i >= 1:
                                ctx.wait(t, "sl", b3["sl"] + i)
                            t.matmul(pgt[:, 0:512], sg1T[:, s_ * 4 + tc, q * P:(q + 1) * P], sWg2[:, 0:512], start=True, stop=True)
                            ctx.inc(t.matmul(pgt[:, 512:1024], sg1T[:, s_ * 4 + tc, q * P:(q + 1) * P], sWg2[:, 512:1024], start=True, stop=True), "e3", 1)
                            ctx.wait(t, "zp", b3["zp"] + 68 * tc + 64 + q + 1)
                            for k in range(DK):
                                if k == 0:
                                    ctx.wait(t, "zp", b3["zp"] + 68 * tc + 64)
                                    ctx.wait(t, "mke", b3["mke"] + 48 * tc + 16 + 8 * q)
                                else:
                                    ctx.wait(t, "mke", b3["mke"] + 48 * tc + 16 + 8 * q + k)
                                ctx.inc(t.transpose(ptn[:], snorm[:, k * P:(k + 1) * P], ctx.ident[:]), "e2", 1)
                            ctx.wait(t, "mke", b3["mke"] + 48 * tc + 16 + 8 * (q + 1))
                            if i >= 1:
                                ctx.wait(t, "yf", b3["yf"] + i)
                            for half in range(2):
                                for k in range(DK):
                                    t.matmul(pxo[:, half * 512:(half + 1) * 512], snormT[:, k, :], sWoP[:, k, half * 512:(half + 1) * 512], start=(k == 0), stop=False)
                                mm = t.matmul(pxo[:, half * 512:(half + 1) * 512], ctx.cones[0:1, 0:P], sWoB[:, half * 512:(half + 1) * 512], start=False, stop=True)
                            ctx.inc(mm, "e1", 1)

                @block.scalar
                def _(s):
                    for i in range(NTS):
                        ctx.wait(s, "e3", b3["e3"] + i + 1)
                        if i >= 1:
                            ctx.wait(s, "vS", b3["vS"] + 4 * (i - 1) + 1)
                        ctx.inc(s.activation(sgt[:], pgt[:], AF.Sigmoid), "sl", 1)
                        ctx.wait(s, "vS", b3["vS"] + 4 * i + 3)
                        ctx.inc(s.activation(ssd[:], smv[:, 1:2], AF.Sqrt, bias=ctx.ceps[:, 0:1]), "sS", 1)

                @block.vector
                def _(v):
                    ne = 0
                    for tc in range(NTC4):
                        jmax = 4 * (tc + 1)
                        for h in range(NH):
                            nh = tc * NH + h
                            if h == 0 and tc >= 1:
                                ctx.wait(v, "vS", b3["vS"] + 16 * tc)
                            for j in range(jmax):
                                par = ne % 2
                                ctx.wait(v, "gu", b3["gu"] + ne + 1)
                                if ne >= 2:
                                    ctx.wait(v, "yr", b3["yr"] + ne - 1)
                                if j >= 4 * tc:
                                    ctx.inc(v.tensor_tensor(sE[par][:], pE[par][:], ctx.cmask[:, j - 4 * tc, :], ALU.mult), "hs", 1)
                                else:
                                    ctx.inc(v.tensor_copy(sE[par][:], pE[par][:]), "hs", 1)
                                ne += 1
                            ctx.wait(v, "yr", b3["yr"] + ne)
                            if nh >= 2:
                                ctx.wait(v, "e2", b3["e2"] + _l5_e2(nh - 2) + 4)
                            ctx.inc(v.tensor_copy(st64w[nh % 2][:], pav[:]), "mke", 1)
                            for q in range(4):
                                ctx.wait(v, "e2", b3["e2"] + _l5_e2(nh) + q + 1)
                                ctx.inc(v.tensor_copy(sattn4[:, q, h * HD:(h + 1) * HD], ptn[:, 0:64]), "zp", 1)
                        for q in range(4):
                            i = tc * 4 + q
                            vb = b3["vS"] + 4 * i
                            ctx.wait(v, "sl", b3["sl"] + i + 1)
                            ctx.wait(v, "zp", b3["zp"] + 68 * tc + 64)
                            ctx.inc(v.tensor_tensor(sato[:], sattn4[:, q, :], sgt[:], ALU.mult), "vS", 1)
                            ctx.wait(v, "vS", vb + 1)
                            v.bn_stats(st12[:, 0:6], sato[:, 0:512])
                            ctx.inc(v.bn_stats(st12[:, 6:12], sato[:, 512:1024]), "vS", 1)
                            ctx.wait(v, "vS", vb + 2)
                            ctx.inc(v.bn_aggr(smv[:], st12[:]), "vS", 1)
                            ctx.wait(v, "sS", b3["sS"] + i + 1)
                            ctx.inc(v.reciprocal(srv[:], ssd[:]), "vS", 1)
                            ctx.wait(v, "vS", vb + 4)
                            ctx.inc(v.tensor_scalar(snorm[:], sato[:], smv[:, 0:1], srv[:], ALU.subtract, ALU.mult), "zp", 1)
                            for k in range(DK):
                                ctx.wait(v, "e2", b3["e2"] + 96 * tc + 64 + 8 * q + k + 1)
                                ctx.inc(v.tensor_copy(snormT[:, k, :], ptn[:]), "mke", 1)
                            ctx.wait(v, "e1", b3["e1"] + i + 1)
                            ctx.wait(v, tok_xg[i][0], tok_xg[i][1])
                            ctx.inc(v.tensor_tensor(sxf[i % 2][:], sxf[i % 2][:], pxo[:], ALU.add), "yf", 1)

            barrier(ctx, block)


def emit_perf(ctx, block, scr, src_ap, dst_ap):
    NTS = SLEN // P
    NCB = 6

    def nphi(nc_):
        return 4 * (nc_ // NCB) + max(0, nc_ % NCB - 2)

    with ExitStack() as outer:
        eno = outer.enter_context
        swb = eno(ctx.nc.sbuf_tensor("p2wb", [P, DK, 3 * D], BF16))
        sbb = eno(ctx.nc.sbuf_tensor("p2bb", [1, 3 * D], BF16))
        spj = eno(ctx.nc.sbuf_tensor("p2pj", [P, DK, D], BF16))
        spjB = eno(ctx.nc.sbuf_tensor("p2pjB", [1, D], BF16))
        sqp = eno(ctx.nc.sbuf_tensor("p2qp", [P, NTS, D], BF16))
        skp = eno(ctx.nc.sbuf_tensor("p2kp", [P, NTS, D], BF16))
        svp = eno(ctx.nc.sbuf_tensor("p2vp", [P, NTS, D], BF16))
        skv = eno(ctx.nc.sbuf_tensor("p2kv", [64, NH * (HD + 1)], BF16))
        tb0 = {k: ctx.n[k] for k in SEMS}

        @block.sync
        def _(s):
            s.wait_ge(ctx.sems["dS"], tb0["dS"])
            ctx.inc(s.dma_start(out=swb[:], in_=r_pkm(scr["Wbig"][0:D, :])), "dS", 16)
            ctx.inc(s.dma_start(out=sbb[:], in_=scr["Wbig"][1024:1025, :]), "dS", 16)
            ctx.inc(s.dma_start(out=spj[:], in_=r_pkm(scr["projT_aug"][0:D, :])), "dS", 16)
            ctx.inc(s.dma_start(out=spjB[:], in_=scr["projT_aug"][1024:1025, :]), "dS", 16)
            s.wait_ge(ctx.sems["dS"], ctx.n["dS"])
        wtok0 = ctx.n["dS"]

        for s_ in range(NSENT):
            with ExitStack() as ls:
                en = ls.enter_context
                szt = [en(ctx.nc.sbuf_tensor(f"x{s_}p2z{i}", [P, DK, P], BF16)) for i in range(2)]
                sneg = [en(ctx.nc.sbuf_tensor(f"x{s_}p2n{i}", [P, 8], F32)) for i in range(2)]
                pbig = [en(ctx.nc.psum_tensor(f"x{s_}p2b{i}", [P, 512], F32)) for i in range(2)]
                b = {k: ctx.n[k] for k in (SEMS + MOE_SEMS)}
                tok_z = {}

                @block.sync
                def _(s):
                    for i in range(NTS):
                        if i >= 2:
                            ctx.wait(s, "gu", b["gu"] + NCB * (i - 1))
                        sem = f"d{i % 2}"
                        tok_z[i] = (sem, ctx.inc(s.dma_start(out=szt[i % 2][:], in_=r_pkm(scr["zT"][:, (s_ * SLEN + i * P):(s_ * SLEN + (i + 1) * P)])), sem, 16))

                @block.tensor
                def _(t):
                    ctx.wait(t, "dS", wtok0)
                    nc_ = 0
                    for i in range(NTS):
                        ctx.wait(t, tok_z[i][0], tok_z[i][1])
                        for cb in range(NCB):
                            par = nc_ % 2
                            if nc_ >= 2:
                                ctx.wait(t, "hs", b["hs"] + nc_ - 1)
                            for k in range(DK):
                                t.matmul(pbig[par][:], szt[i % 2][:, k, :], swb[:, k, cb * 512:(cb + 1) * 512], start=(k == 0), stop=False)
                            mm = t.matmul(pbig[par][:], ctx.cones[0:1, 0:P], sbb[0:1, cb * 512:(cb + 1) * 512], start=False, stop=True)
                            ctx.inc(mm, "gu", 1)
                            nc_ += 1

                @block.scalar
                def _(s):
                    nc_ = 0
                    for i in range(NTS):
                        for cb in range(NCB):
                            par = nc_ % 2
                            if cb >= 2:
                                dst = sqp if cb < 4 else skp
                                col = (cb - 2) % 2
                                ctx.wait(s, "gu", b["gu"] + nc_ + 1)
                                ctx.wait(s, "mk", b["mk"] + 2 * (nphi(nc_) + 1))
                                for hseg in range(8):
                                    ctx.inc(s.activation(dst[:, i, col * 512 + hseg * HD: col * 512 + (hseg + 1) * HD],
                                                         pbig[par][:, hseg * HD:(hseg + 1) * HD], AF.Exp,
                                                         bias=sneg[nphi(nc_) % 2][:, hseg:hseg + 1]), "sl", 1)
                            nc_ += 1

                @block.vector
                def _(v):
                    nc_ = 0
                    for i in range(NTS):
                        for cb in range(NCB):
                            par = nc_ % 2
                            if cb < 2:
                                ctx.wait(v, "gu", b["gu"] + nc_ + 1)
                                ctx.inc(v.tensor_copy(svp[:, i, cb * 512:(cb + 1) * 512], pbig[par][:]), "hs", 1)
                            else:
                                dst = sqp if cb < 4 else skp
                                col = (cb - 2) % 2
                                np_ = nphi(nc_)
                                ctx.wait(v, "gu", b["gu"] + nc_ + 1)
                                if np_ >= 2:
                                    ctx.wait(v, "sl", b["sl"] + 8 * (np_ - 1))
                                ctx.inc(v.reduce_max(sneg[np_ % 2][:].unsqueeze(2),
                                                     pbig[par][:].rearrange("p (h f) -> p h f", f=HD),
                                                     axis=AX.X, negate=True), "mk", 1)
                                ctx.wait(v, "mk", ctx.n["mk"])
                                ctx.inc(v.tensor_scalar(sneg[np_ % 2][:], sneg[np_ % 2][:], -LN8, None, ALU.add), "mk", 1)
                                ctx.wait(v, "sl", b["sl"] + 8 * (np_ + 1))
                                ctx.inc(v.tensor_scalar(dst[:, i, col * 512:(col + 1) * 512],
                                                        dst[:, i, col * 512:(col + 1) * 512], 1e-6, None, ALU.add), "hs", 1)
                            nc_ += 1

            barrier(ctx, block)

            with ExitStack() as ls:
                en = ls.enter_context
                pkv = en(ctx.nc.psum_tensor(f"x{s_}p2kv", [64, NH * P], F32))
                b = {k: ctx.n[k] for k in (SEMS + MOE_SEMS)}

                @block.tensor
                def _(t):
                    for i in range(NTS):
                        for h in range(NH):
                            st = (i == 0 and h % 4 == 0)
                            sp = (i == NTS - 1 and h % 4 == 3)
                            t.matmul(pkv[:, h * P:h * P + HD], skp[:, i, h * HD:(h + 1) * HD],
                                     svp[:, i, h * HD:(h + 1) * HD], start=st, stop=False)
                            mm = t.matmul(pkv[:, h * P + HD:h * P + HD + 1], skp[:, i, h * HD:(h + 1) * HD],
                                          ctx.conec[0:P, 0:1], start=False, stop=sp)
                    ctx.inc(mm, "e1", 1)

                @block.vector
                def _(v):
                    ctx.wait(v, "e1", b["e1"] + 1)
                    for h in range(NH):
                        ctx.inc(v.tensor_copy(skv[:, h * (HD + 1):(h + 1) * (HD + 1)], pkv[:, h * P:h * P + HD + 1]), "e2", 1)

            barrier(ctx, block)

            with ExitStack() as ls:
                en = ls.enter_context
                sqpt = [en(ctx.nc.sbuf_tensor(f"x{s_}p2qt{i}", [64, P], BF16)) for i in range(2)]
                sattn = en(ctx.nc.sbuf_tensor(f"x{s_}p2at", [P, D], BF16))
                satT = en(ctx.nc.sbuf_tensor(f"x{s_}p2aT", [P, DK, P], BF16))
                sdiv = en(ctx.nc.sbuf_tensor(f"x{s_}p2dv", [P, 1], F32))
                sxf = [en(ctx.nc.sbuf_tensor(f"x{s_}p2x{i}", [P, D], F32)) for i in range(2)]
                ptr = [en(ctx.nc.psum_tensor(f"x{s_}p2tr{i}", [64, P], BF16)) for i in range(2)]
                paug = [en(ctx.nc.psum_tensor(f"x{s_}p2au{i}", [P, HD + 1], F32)) for i in range(2)]
                ptn = en(ctx.nc.psum_tensor(f"x{s_}p2tn", [P, P], BF16))
                pxo = en(ctx.nc.psum_tensor(f"x{s_}p2xo", [P, D], F32))
                b = {k: ctx.n[k] for k in (SEMS + MOE_SEMS)}
                tok_xg = {}
                tok_st = {}

                @block.gpsimd
                def _(g):
                    for i in range(NTS):
                        gt = s_ * NTS + i
                        j = i % 2
                        if i >= 2:
                            ctx.wait(g, tok_st[i - 2][0], tok_st[i - 2][1])
                        sem = f"g{2 + j}"
                        tok_xg[i] = (sem, ctx.inc(g.dma_start(out=sxf[j][:], in_=src_ap[gt * P:(gt + 1) * P, :]), sem, 16))
                        ctx.wait(g, "yf", b["yf"] + i + 1)
                        tok_st[i] = (f"g{j}", ctx.inc(g.dma_start(out=dst_ap[gt * P:(gt + 1) * P, :], in_=sxf[j][:]), f"g{j}", 16))

                if s_ == 0:
                    ctx.last_xtok = {}
                for i_ in range(NTS):
                    ctx.last_xtok[s_ * NTS + i_] = tok_st[i_]

                @block.tensor
                def _(t):
                    nh = 0
                    for i in range(NTS):
                        for h in range(NH):
                            if nh >= 2:
                                ctx.wait(t, "mke", b["mke"] + nh - 1)
                            ctx.inc(t.transpose(ptr[nh % 2][:], sqp[:, i, h * HD:(h + 1) * HD], ctx.ident[:]), "e3", 1)
                            ctx.wait(t, "mke", b["mke"] + nh + 1)
                            if nh >= 2:
                                ctx.wait(t, "yr", b["yr"] + nh - 1)
                            ctx.inc(t.matmul(paug[nh % 2][:], sqpt[nh % 2][:], skv[:, h * (HD + 1):(h + 1) * (HD + 1)],
                                             start=True, stop=True), "gu", 1)
                            nh += 1
                        ctx.wait(t, "yr", b["yr"] + 16 * (i + 1))
                        for k in range(DK):
                            if i > 0 or k > 0:
                                ctx.wait(t, "zp", b["zp"] + i * DK + k)
                            ctx.inc(t.transpose(ptn[:], sattn[:, k * P:(k + 1) * P], ctx.ident[:]), "e2", 1)
                        ctx.wait(t, "zp", b["zp"] + (i + 1) * DK)
                        if i >= 1:
                            ctx.wait(t, "yf", b["yf"] + i)
                        for half in range(2):
                            for k in range(DK):
                                t.matmul(pxo[:, half * 512:(half + 1) * 512], satT[:, k, :], spj[:, k, half * 512:(half + 1) * 512], start=(k == 0), stop=False)
                            mm = t.matmul(pxo[:, half * 512:(half + 1) * 512], ctx.cones[0:1, 0:P], spjB[0:1, half * 512:(half + 1) * 512], start=False, stop=True)
                        ctx.inc(mm, "mk", 1)

                @block.vector
                def _(v):
                    nh = 0
                    for i in range(NTS):
                        for h in range(NH):
                            if h == 0 and i >= 1:
                                ctx.wait(v, "e2", b["e2"] + i * DK)
                            ctx.wait(v, "e3", b["e3"] + nh + 1)
                            ctx.inc(v.tensor_copy(sqpt[nh % 2][:], ptr[nh % 2][:]), "mke", 1)
                            ctx.wait(v, "gu", b["gu"] + nh + 1)
                            ctx.inc(v.tensor_scalar(sdiv[:], paug[nh % 2][:, HD:HD + 1], 1e-6, None, ALU.add), "vS", 1)
                            ctx.wait(v, "vS", ctx.n["vS"])
                            ctx.inc(v.reciprocal(sdiv[:], sdiv[:]), "vS", 1)
                            ctx.wait(v, "vS", ctx.n["vS"])
                            ctx.inc(v.tensor_scalar(sattn[:, h * HD:(h + 1) * HD], paug[nh % 2][:, 0:HD], sdiv[:], None, ALU.mult), "yr", 1)
                            nh += 1
                        for k in range(DK):
                            ctx.wait(v, "e2", b["e2"] + i * DK + k + 1)
                            ctx.inc(v.tensor_copy(satT[:, k, :], ptn[:]), "zp", 1)
                        ctx.wait(v, "mk", b["mk"] + i + 1)
                        ctx.wait(v, tok_xg[i][0], tok_xg[i][1])
                        ctx.inc(v.tensor_tensor(sxf[i % 2][:], sxf[i % 2][:], pxo[:], ALU.add), "yf", 1)

            barrier(ctx, block)


CAP = 768
NTS6 = CAP // P
U32 = mybir.dt.uint32


def emit_moe_indices(ctx, block, scr, sidx):
    import bass_rust
    with ExitStack() as ls:
        en = ls.enter_context
        mb = en(ctx.nc.sbuf_tensor("ixmb", [8, T], BF16))
        m32 = en(ctx.nc.sbuf_tensor("ixm", [32, T], F32))
        c32 = en(ctx.nc.sbuf_tensor("ixc", [32, T], F32))
        t32 = en(ctx.nc.sbuf_tensor("ixt", [32, T], F32))
        smT = en(ctx.nc.sbuf_tensor("ixsmt", [P, NT, 8], BF16))
        s8 = en(ctx.nc.sbuf_tensor("ixs8", [P, NT, 8], F32))
        slot = en(ctx.nc.sbuf_tensor("ixslt", [P, NT], F32))
        ctmp = en(ctx.nc.sbuf_tensor("ixct", [1, 256], F32))
        c2 = en(ctx.nc.sbuf_tensor("ixc2", [1, 256], F32))
        c2b = en(ctx.nc.sbuf_tensor("ixc2b", [1, 256], F32))
        scap = en(ctx.nc.sbuf_tensor("ixcap", [1, 256], F32))
        szt = [en(ctx.nc.sbuf_tensor(f"ixz{i}", [P, D], BF16)) for i in range(2)]
        pmT = [en(ctx.nc.psum_tensor(f"ixpt{i}", [P, 8], BF16)) for i in range(2)]
        pall = en(ctx.nc.psum_tensor("ixpa", [P, NT * 8], F32))
        prow = en(ctx.nc.psum_tensor("ixpr", [1, 256], F32))
        pbc = en(ctx.nc.psum_tensor("ixpb", [P, NT * 8], F32))
        b = {k: ctx.n[k] for k in (SEMS + MOE_SEMS)}
        first = not getattr(ctx, "zc_zeroed", False)
        ctx.zc_zeroed = True

        @block.sync
        def _(s):
            s.wait_ge(ctx.sems["dS"], b["dS"])
            ctx.inc(s.dma_start(out=mb[:], in_=scr["msk"][:]), "dS", 16)
            ctx.inc(s.dma_start(out=scap[:], in_=ctx.c_ecap[:]), "dS", 16)
            s.wait_ge(ctx.sems["dS"], ctx.n["dS"])
        ldtok = ctx.n["dS"]

        zctok = None
        if first:
            szero = en(ctx.nc.sbuf_tensor("ixzr", [P, D], BF16))

            @block.vector
            def _(v):
                ctx.inc(v.memset(szero[:], 0.0), "vS", 1)
            zrtok = ctx.n["vS"]

            @block.sync
            def _(s):
                ctx.wait(s, "vS", zrtok)
                for r in range(E * CAP // P):
                    ctx.inc(s.dma_start(out=scr["zc"][r * P:(r + 1) * P, :], in_=szero[:]), "dS", 16)
                s.wait_ge(ctx.sems["dS"], ctx.n["dS"])
            zctok = ctx.n["dS"]

        def vstep(v, ins):
            ctx.inc(ins, "vS", 1)
            ctx.wait(v, "vS", ctx.n["vS"])

        @block.vector
        def _(v):
            ctx.wait(v, "dS", ldtok)
            v.memset(m32[:], 0.0)
            vstep(v, v.memset(t32[:], 0.0))
            vstep(v, v.tensor_copy(m32[0:8, :], mb[:]))
            for sh in (1, 2, 4):
                mp = [(i - sh) if (8 > i >= sh) else 31 for i in range(32)]
                srcb = m32 if sh == 1 else c32
                vstep(v, v.stream_shuffle(t32[:], srcb[:], mp))
                vstep(v, v.tensor_tensor(c32[:], srcb[:], t32[:], ALU.add))
            vstep(v, v.tensor_scalar(t32[:], c32[:], 1.0, None, ALU.is_equal))
            vstep(v, v.tensor_tensor(m32[:], m32[:], t32[:], ALU.mult))
            ctx.inc(v.tensor_copy(mb[:], m32[0:8, :]), "mk", 1)
        vtf = ctx.n["mk"]

        @block.tensor
        def _(t):
            ctx.wait(t, "mk", vtf)
            for i in range(NT):
                if i >= 2:
                    ctx.wait(t, "yr", b["yr"] + i - 1)
                ctx.inc(t.transpose(pmT[i % 2][:], mb[:, i * P:(i + 1) * P], ctx.ident[0:8, 0:8]), "gu", 1)
            ctx.wait(t, "yr", b["yr"] + NT)
            ctx.inc(t.matmul(pall[:], ctx.ctri[:], smT[:].rearrange("p t e -> p (t e)"), start=True, stop=True), "sl", 1)
            ctx.inc(t.matmul(prow[:], ctx.conec[:], smT[:].rearrange("p t e -> p (t e)"), start=True, stop=True), "sl", 1)
            ctx.wait(t, "hs", b["hs"] + 1)
            ctx.inc(t.matmul(pbc[:], ctx.conesf[0:1, 0:P], c2b[:], start=True, stop=True), "sl", 1)

        @block.vector
        def _(v):
            for i in range(NT):
                ctx.wait(v, "gu", b["gu"] + i + 1)
                ctx.inc(v.tensor_copy(smT[:, i, :], pmT[i % 2][:]), "yr", 1)
            ctx.wait(v, "sl", b["sl"] + 2)
            vstep(v, v.tensor_copy(ctmp[:], prow[:]))
            vstep(v, v.memset(c2[:, 0:8], 0.0))
            vstep(v, v.tensor_copy(c2[:, 8:256], ctmp[:, 0:248]))
            a, bb = c2, c2b
            for s_ in (8, 16, 32, 64, 128):
                v.tensor_copy(bb[:, 0:s_], a[:, 0:s_])
                vstep(v, v.tensor_tensor(bb[:, s_:256], a[:, s_:256], a[:, 0:256 - s_], ALU.add))
                a, bb = bb, a
            assert a is c2b
            ctx.inc(v.tensor_tensor(c2b[:], a[:], scap[:], ALU.add), "hs", 1)
            ctx.wait(v, "sl", b["sl"] + 3)
            vstep(v, v.tensor_copy(s8[:].rearrange("p t e -> p (t e)"), pbc[:]))
            vstep(v, v.tensor_tensor(s8[:].rearrange("p t e -> p (t e)"), s8[:].rearrange("p t e -> p (t e)"), pall[:], ALU.add))
            vstep(v, v.tensor_tensor(s8[:].rearrange("p t e -> p (t e)"), s8[:].rearrange("p t e -> p (t e)"), smT[:].rearrange("p t e -> p (t e)"), ALU.mult))
            vstep(v, v.tensor_reduce(slot[:].unsqueeze(2), s8[:], axis=AX.X, op=ALU.add))
            vstep(v, v.tensor_scalar(slot[:], slot[:], float(E * CAP - 1), None, ALU.min))
            ctx.inc(v.tensor_copy(sidx[:], slot[:]), "zp", 1)
        slottok = ctx.n["zp"]

        tok_zl = {}

        @block.sync
        def _(s):
            for i in range(NT):
                sem = f"d{4 + i % 2}"
                if i >= 2:
                    ctx.wait(s, f"g{i % 2}", b[f"g{i % 2}"] + 16 * (i // 2))
                tok_zl[i] = (sem, ctx.inc(s.dma_start(out=szt[i % 2][:], in_=scr["z"][i * P:(i + 1) * P, :]), sem, 16))

        @block.gpsimd
        def _(g):
            ctx.wait(g, "zp", slottok)
            if zctok is not None:
                ctx.wait(g, "dS", zctok)
            for i in range(NT):
                ctx.wait(g, tok_zl[i][0], tok_zl[i][1])
                ctx.inc(g.indirect_dma_start(out=scr["zc"][:], out_offset=bass_rust.IndirectOffsetOnAxis(axis=0, ap=sidx[:, i:i + 1]),
                                             in_=szt[i % 2][:], in_offset=None), f"g{i % 2}", 16)
            g.wait_ge(ctx.sems["g0"], ctx.n["g0"])
            g.wait_ge(ctx.sems["g1"], ctx.n["g1"])


def emit_moe_sparse(ctx, block, scr, src_ap, dst_ap, half, sidx):
    import bass_rust
    b = {k: ctx.n[k] for k in (SEMS + MOE_SEMS)}
    with ExitStack() as ls:
        en = ls.enter_context
        swg = [en(ctx.nc.sbuf_tensor(f"swg{i}", [P, DK, IHW], BF16)) for i in range(2)]
        swu = [en(ctx.nc.sbuf_tensor(f"swu{i}", [P, DK, IHW], BF16)) for i in range(2)]
        swd = [en(ctx.nc.sbuf_tensor(f"swd{i}", [P, ITH, D], BF16)) for i in range(2)]
        szc = [en(ctx.nc.sbuf_tensor(f"szc{i}", [P, DK, CAP], BF16)) for i in range(2)]
        sh = en(ctx.nc.sbuf_tensor("ssh", [P, ITH, CAP], BF16))
        sil = [en(ctx.nc.sbuf_tensor(f"ssil{i}", [P, 512], BF16)) for i in range(2)]
        syb = en(ctx.nc.sbuf_tensor("ssyb", [P, NTS6, D], BF16))
        sxf = [en(ctx.nc.sbuf_tensor(f"ssx{i}", [P, D], F32)) for i in range(2)]
        syg = [en(ctx.nc.sbuf_tensor(f"ssyg{i}", [P, D], BF16)) for i in range(2)]
        pg = [en(ctx.nc.psum_tensor(f"spg{i}", [P, 512], F32)) for i in range(2)]
        pu = [en(ctx.nc.psum_tensor(f"spu{i}", [P, 512], F32)) for i in range(2)]
        py = [en(ctx.nc.psum_tensor(f"spy{i}", [P, D], F32)) for i in range(2)]
        TCS = [(0, 512), (512, 256)]
        tok_yw = {e: ("g3", b["g3"] + 16 * (e + 1)) for e in range(E)}

        tok_zc = {}
        tok_w = {}

        @block.sync
        def _(s):
            s.wait_ge(ctx.sems["g0"], b["g0"])
            s.wait_ge(ctx.sems["g1"], b["g1"])
            for e in range(E):
                sem = f"d{2 + e % 2}"
                if e >= 2:
                    ctx.wait(s, "gu", b["gu"] + 44 * (e - 1))
                for k in range(DK):
                    ctx.inc(s.dma_start(out=szc[e % 2][:, k, :], in_=scr["zc"][e * CAP:(e + 1) * CAP, k * P:(k + 1) * P], transpose=True), sem, 16)
                tok_zc[e] = (sem, ctx.n[sem])
                for ih in range(2):
                    w = 2 * e + ih
                    wsem = f"d{w % 2}"
                    if w >= 2:
                        ctx.wait(s, "yr", b["yr"] + 6 * (w - 1))
                    ihs = ih * IHW
                    ctx.inc(s.dma_start(out=swg[w % 2][:], in_=r_pkm(scr["gwT"][e][:, ihs:ihs + IHW])), wsem, 16)
                    ctx.inc(s.dma_start(out=swu[w % 2][:], in_=r_pkm(scr["uwT"][e][:, ihs:ihs + IHW])), wsem, 16)
                    ctx.inc(s.dma_start(out=swd[w % 2][:], in_=r_pkm(scr["dwT"][e][ihs:ihs + IHW, :])), wsem, 16)
                    tok_w[w] = (wsem, ctx.n[wsem])

        @block.tensor
        def _(t):
            n = 0
            m = 0
            for e in range(E):
                ctx.wait(t, tok_zc[e][0], tok_zc[e][1])
                for ih in range(2):
                    w = 2 * e + ih
                    ctx.wait(t, tok_w[w][0], tok_w[w][1])
                    for (tco, tcn) in TCS:
                        for it in range(ITH):
                            par = n % 2
                            if n >= 2:
                                ctx.wait(t, "hs", b["hs"] + n - 1)
                            for k in range(DK):
                                t.matmul(pg[par][:, 0:tcn], swg[w % 2][:, k, it * P:(it + 1) * P], szc[e % 2][:, k, tco:tco + tcn], start=(k == 0), stop=(k == DK - 1))
                            for k in range(DK):
                                mm = t.matmul(pu[par][:, 0:tcn], swu[w % 2][:, k, it * P:(it + 1) * P], szc[e % 2][:, k, tco:tco + tcn], start=(k == 0), stop=(k == DK - 1))
                            ctx.inc(mm, "gu", 1)
                            n += 1
                    ctx.wait(t, "hs", b["hs"] + n)
                    for ts in range(NTS6):
                        pyb = py[m % 2]
                        if m >= 2:
                            ctx.wait(t, "yf", b["yf"] + m - 1)
                        for it in range(ITH):
                            t.matmul(pyb[:, 0:512], sh[:, it, ts * P:(ts + 1) * P], swd[w % 2][:, it, 0:512], start=(it == 0), stop=(it == ITH - 1))
                            mm = t.matmul(pyb[:, 512:1024], sh[:, it, ts * P:(ts + 1) * P], swd[w % 2][:, it, 512:1024], start=(it == 0), stop=(it == ITH - 1))
                        ctx.inc(mm, "yr", 1)
                        m += 1

        @block.scalar
        def _(s):
            n = 0
            for e in range(E):
                for ih in range(2):
                    for (tco, tcn) in TCS:
                        for it in range(ITH):
                            par = n % 2
                            ctx.wait(s, "gu", b["gu"] + n + 1)
                            if n >= 2:
                                ctx.wait(s, "hs", b["hs"] + n - 1)
                            ctx.inc(s.activation(sil[par][:, 0:tcn], pg[par][:, 0:tcn], AF.Silu), "sl", 1)
                            n += 1


        @block.vector
        def _(v):
            n = 0
            m = 0
            for e in range(E):
                for ih in range(2):
                    for (tco, tcn) in TCS:
                        for it in range(ITH):
                            par = n % 2
                            ctx.wait(v, "sl", b["sl"] + n + 1)
                            ctx.inc(v.tensor_tensor(sh[:, it, tco:tco + tcn], sil[par][:, 0:tcn], pu[par][:, 0:tcn], ALU.mult), "hs", 1)
                            n += 1
                    for ts in range(NTS6):
                        ctx.wait(v, "yr", b["yr"] + m + 1)
                        if ih == 0 and e >= 1:
                            ctx.wait(v, "g3", tok_yw[e - 1][1])
                        if ih == 0:
                            ctx.inc(v.tensor_copy(syb[:, ts, :], py[m % 2][:]), "yf", 1)
                        else:
                            ctx.inc(v.tensor_tensor(syb[:, ts, :], syb[:, ts, :], py[m % 2][:], ALU.add), "yf", 1)
                        m += 1
        @block.gpsimd
        def _(g):
            for e in range(E):
                ctx.wait(g, "yf", b["yf"] + 12 * (e + 1))
                tok = ctx.inc(g.dma_start(out=r_pkm(scr["yc"][e * CAP:(e + 1) * CAP, :]), in_=syb[:]), "g3", 16)
                assert tok == tok_yw[e][1]
            g.wait_ge(ctx.sems["g3"], ctx.n["g3"])
        ycdone = ctx.n["g3"]

        tok_xl = {}
        tok_gg = {}
        st_slot = lambda i: ("g2" if i % 2 == 0 else "g4")
        tok_st = {i: (st_slot(i), b[st_slot(i)] + 16 * (i // 2 + 1)) for i in range(NT)}

        @block.sync
        def _(s):
            for i in range(NT):
                j = i % 2
                sem = f"d{4 + j}"
                if i >= 2:
                    ctx.wait(s, "xo", b["xo"] + i - 1)
                    ctx.wait(s, tok_st[i - 2][0], tok_st[i - 2][1])
                tok_xl[i] = (sem, ctx.inc(s.dma_start(out=sxf[j][:], in_=src_ap[i * P:(i + 1) * P, :]), sem, 16))

        @block.gpsimd
        def _(g):
            g.wait_ge(ctx.sems["g3"], ycdone)
            for i in range(NT):
                j = i % 2
                sem = f"g{j}"
                if i >= 2:
                    ctx.wait(g, "xo", b["xo"] + i - 1)
                tok_gg[i] = (sem, ctx.inc(g.indirect_dma_start(out=syg[j][:], out_offset=None, in_=scr["yc"][:],
                                                               in_offset=bass_rust.IndirectOffsetOnAxis(axis=0, ap=sidx[:, i:i + 1])), sem, 16))
                ctx.wait(g, "xo", b["xo"] + i + 1)
                tok = ctx.inc(g.dma_start(out=dst_ap[i * P:(i + 1) * P, :], in_=sxf[j][:]), tok_st[i][0], 16)
                assert tok == tok_st[i][1]

        @block.vector
        def _(v):
            for i in range(NT):
                j = i % 2
                ctx.wait(v, tok_xl[i][0], tok_xl[i][1])
                ctx.wait(v, tok_gg[i][0], tok_gg[i][1])
                ctx.inc(v.scalar_tensor_tensor(sxf[j][:], syg[j][:], half, sxf[j][:], ALU.mult, ALU.add), "xo", 1)

        ctx.last_xtok = dict(tok_st)


# version 15
# speedup vs baseline: 1.0077x; 1.0077x over previous
import numpy as np
import ml_dtypes
from contextlib import ExitStack

import concourse.bass as bass
import concourse.mybir as mybir
from concourse.bass_utils import run_bass_kernel_spmd

F32 = mybir.dt.float32
BF16 = mybir.dt.bfloat16
AF = mybir.ActivationFunctionType
ALU = mybir.AluOpType
AX = mybir.AxisListType
bf16 = ml_dtypes.bfloat16

NCORES = 8
P = 128
D = 1024
DK = D // P
T = 4096
NT = T // P
SLEN = 2048
NSENT = 2
NQ = 8
QT = T // NQ
QTT = QT // P
E = 8
II = 2816
ITH = 11
IHW = ITH * P
NH = 16
HD = 64
LN8 = float(np.log(8.0))

SPARSE_MOE = True

MOE_SEMS = ["gu", "yr", "mk", "sl", "hs", "yf", "zp", "mke", "xo", "e1", "e2", "e3",
            "w0", "w1", "w2", "w3"]

SEMS = ["dS", "dG", "d0", "d1", "d2", "d3", "d4", "d5", "g0", "g1", "g2", "g3", "g4", "tS", "vS", "sS", "bar"]


class NcProxy:

    def __init__(self, nc):
        self._nc = nc
        self._cnt = 0

    def __getattr__(self, name):
        return getattr(self._nc, name)

    def sbuf_tensor(self, name, shape, dt):
        self._cnt += 1
        return self._nc.sbuf_tensor(f"{name}_u{self._cnt}", shape, dt)

    def psum_tensor(self, name, shape, dt):
        self._cnt += 1
        return self._nc.psum_tensor(f"{name}_u{self._cnt}", shape, dt)


class Ctx:
    def __init__(self, nc, stack):
        self.nc = nc
        self.stack = stack
        self.sems = {}
        self.n = {}
        self.inputs = {}

    def sem_init(self):
        for name in SEMS:
            self.sems[name] = self.stack.enter_context(self.nc.semaphore(name))
            self.n[name] = 0

    def inc(self, instr, name, k=1):
        instr.then_inc(self.sems[name], k)
        self.n[name] += k
        return self.n[name]

    def wait(self, eng, name, tok):
        eng.wait_ge(self.sems[name], tok)

    def sb(self, name, shape, dt=F32):
        return self.stack.enter_context(self.nc.sbuf_tensor(name, shape, dt))

    def ps(self, name, shape, dt=F32):
        return self.stack.enter_context(self.nc.psum_tensor(name, shape, dt))


def barrier(ctx, block):
    bar = ctx.sems["bar"]
    tgt = ctx.n["bar"] + 5
    ctx.n["bar"] = tgt
    dtoks = {k: ctx.n[k] for k in ["dS", "d0", "d1", "d2", "d3", "d4", "d5"]}
    gtoks = {k: ctx.n[k] for k in ["dG", "g0", "g1", "g2", "g3", "g4"]}

    @block.sync
    def _(s):
        for k, v in dtoks.items():
            if v:
                s.wait_ge(ctx.sems[k], v)
        s.nop().then_inc(bar, 1)
        s.wait_ge(bar, tgt)

    @block.gpsimd
    def _(g):
        for k, v in gtoks.items():
            if v:
                g.wait_ge(ctx.sems[k], v)
        g.drain()
        g.nop().then_inc(bar, 1)
        g.wait_ge(bar, tgt)

    @block.tensor
    def _(t):
        t.drain()
        t.nop().then_inc(bar, 1)
        t.wait_ge(bar, tgt)

    @block.scalar
    def _(s):
        s.drain()
        s.nop().then_inc(bar, 1)
        s.wait_ge(bar, tgt)

    @block.vector
    def _(v):
        v.drain()
        v.nop().then_inc(bar, 1)
        v.wait_ge(bar, tgt)


def r_pkm(ap2d, p=P):
    return ap2d.rearrange("(k p) m -> p k m", p=p)


def r_pk(ap1d, p=P):
    return ap1d.rearrange("(k p) -> p k", p=p)


def emit_prologue(ctx, block, scr, need, stage="all"):
    inp = ctx.inputs

    casts = []
    if "moe" in need:
        casts.append((scr["rw_bf"], inp["L0_ffn_router_w"]))
    if "perf" in need:
        casts.append((scr["qkv_bf"], inp["L0_attn_qkv_w"]))
        casts.append((scr["proj_bf"], inp["L0_attn_proj_w"]))
        casts.append((scr["Wbig"][1024:1025, 0:D], inp["L0_attn_qkv_b"][2048:3072].unsqueeze(0)))
        casts.append((scr["projT_aug"][1024:1025, :], inp["L0_attn_proj_b"].unsqueeze(0)))
    if "l1att" in need:
        casts.append((scr["Wq_bf"], inp["L1_attn_Wq"]))
        casts.append((scr["Wk_bf"], inp["L1_attn_Wk"]))
        casts.append((scr["Wv_bf"], inp["L1_attn_Wv"]))
        casts.append((scr["Wg1_bf"], inp["L1_attn_Wg1"]))
        casts.append((scr["Wg2_bf"], inp["L1_attn_Wg2"]))
        casts.append((scr["Wo_bf"], inp["L1_attn_Wo"]))
    if "l1ffn" in need:
        casts.append((scr["Wup_bf"], inp["L1_ffn_W_up"]))
        casts.append((scr["bup_bf"], inp["L1_ffn_b_up"]))
        casts.append((scr["br_bf"], inp["L1_ffn_br"].unsqueeze(0)))
        casts.append((scr["bd_bf"], inp["L1_ffn_b_down"]))
    ctok = {}

    if stage in ("all", "casts"):
        @block.gpsimd
        def _(g):
            for j, (dst, src) in enumerate(casts):
                ctok[j] = ctx.inc(g.dma_start(out=dst, in_=src), "dG", 16)
        ctx.casts_done = ctx.n["dG"]
    if stage == "casts":
        return

    if "moe" in need:
        with ExitStack() as ls:
            en = ls.enter_context
            swg = [en(ctx.nc.sbuf_tensor(f"prg{i}", [P, DK, IHW], BF16)) for i in range(2)]
            swu = [en(ctx.nc.sbuf_tensor(f"pru{i}", [P, DK, IHW], BF16)) for i in range(2)]
            swd = [en(ctx.nc.sbuf_tensor(f"prd{i}", [P, ITH, D], BF16)) for i in range(2)]
            sin = [en(ctx.nc.sbuf_tensor(f"prf{i}", [P, D], F32)) for i in range(4)]
            sbin = [en(ctx.nc.sbuf_tensor(f"prc{i}", [P, IHW], BF16)) for i in range(4)]
            pwt = [en(ctx.nc.psum_tensor(f"prp{i}", [P, P], BF16)) for i in range(8)]
            tb = {k: ctx.n[k] for k in (SEMS + MOE_SEMS)}

            LDSL = ["d0", "d1", "e1", "e3"]
            tiles_w = [("g", it) for it in range(ITH)] + [("u", it) for it in range(ITH)] + [("d", kd) for kd in range(DK)]
            NTIL = len(tiles_w)
            sched = []
            for w in range(2 * E):
                e, ih = divmod(w, 2)
                for (kind, idx) in tiles_w:
                    sched.append((w, e, ih, kind, idx))
            G = len(sched)
            ntr_of = lambda kind: DK if kind in ("g", "u") else ITH
            cum_tr = [0] * (G + 1)
            SYNCK = ("g",)
            cast_pre = [0] * (G + 1)
            for g in range(G):
                cum_tr[g + 1] = cum_tr[g] + ntr_of(sched[g][3])
                cast_pre[g + 1] = cast_pre[g] + (1 if sched[g][3] in SYNCK else 0)
            GSL = ["w0", "w1", "w2", "w3"]
            ld_cnt = {}
            gld_cnt = {}
            _lc = {0: 0, 1: 0, 2: 0, 3: 0}
            _gc = {0: 0, 1: 0, 2: 0, 3: 0}
            for g in range(G):
                if sched[g][3] in SYNCK:
                    _lc[g % 4] += 1
                    ld_cnt[g] = _lc[g % 4]
                else:
                    _gc[g % 4] += 1
                    gld_cnt[g] = _gc[g % 4]

            def src_ap(g):
                w, e, ih, kind, idx = sched[g]
                ihs = ih * IHW
                if kind == "g":
                    return inp["L0_ffn_gate_w"][e][ihs + idx * P:ihs + (idx + 1) * P, :]
                if kind == "u":
                    return inp["L0_ffn_up_w"][e][ihs + idx * P:ihs + (idx + 1) * P, :]
                return inp["L0_ffn_down_w"][e][idx * P:(idx + 1) * P, ihs:ihs + IHW]

            def wid(g):
                return sched[g][0]

            WSEMS = {"g": ["xo", "e2"], "u": ["mke", "zp"], "d": ["yf", "gu"]}
            wsem = lambda w, kind: WSEMS[kind][w % 2]

            def wr_piece(s, w, kind):
                e_, ih_ = divmod(w, 2)
                bb = w % 2
                if kind == "g":
                    evtgt = cum_tr[NTIL * w + ITH]
                    buf = swg[bb]
                    dstap = scr["gwT"][e_][ih_]
                elif kind == "u":
                    evtgt = cum_tr[NTIL * w + 2 * ITH]
                    buf = swu[bb]
                    dstap = scr["uwT"][e_][ih_]
                else:
                    evtgt = cum_tr[NTIL * (w + 1)]
                    buf = swd[bb]
                    dstap = scr["dwT"][e_][ih_]
                s.wait_ge(ctx.sems["vS"], tb["vS"] + evtgt)
                ctx.inc(s.dma_start(out=dstap, in_=buf[:]), wsem(w, kind), 16)

            @block.sync
            def _(s):
                s.wait_ge(ctx.sems["dG"], ctx.casts_done)
                for g in range(G):
                    w = wid(g)
                    t = g % NTIL
                    if t == 0 and w >= 1:
                        wr_piece(s, w - 1, "d")
                    if sched[g][3] not in SYNCK:
                        continue
                    if g >= 4:
                        s.wait_ge(ctx.sems["sS"], tb["sS"] + cast_pre[g - 3])
                    ctx.inc(s.dma_start(out=sin[g % 4][:, 0:D], in_=src_ap(g)), LDSL[g % 4], 16)
                    if t == ITH - 1:
                        wr_piece(s, w, "g")
                        wr_piece(s, w, "u")
                wr_piece(s, 2 * E - 1, "d")
                for sm in ("xo", "e2", "mke", "zp", "yf", "gu"):
                    s.wait_ge(ctx.sems[sm], ctx.n[sm])

            @block.scalar
            def _(s):
                for g in range(G):
                    if sched[g][3] not in SYNCK:
                        continue
                    s.wait_ge(ctx.sems[LDSL[g % 4]], tb[LDSL[g % 4]] + 16 * ld_cnt[g])
                    if g >= 4:
                        s.wait_ge(ctx.sems["tS"], tb["tS"] + cum_tr[g - 3])
                    ctx.inc(s.activation(sbin[g % 4][:, 0:D], sin[g % 4][:, 0:D], AF.Copy), "sS", 1)

            @block.gpsimd
            def _(gp):
                for g in range(G):
                    if sched[g][3] in SYNCK:
                        continue
                    gp.wait_ge(ctx.sems["tS"], tb["tS"] + cum_tr[g - 3])
                    cols = D if sched[g][3] == "u" else IHW
                    ctx.inc(gp.dma_start(out=sbin[g % 4][:, 0:cols], in_=src_ap(g)), GSL[g % 4], 16)

            @block.tensor
            def _(t):
                for g in range(G):
                    if sched[g][3] in SYNCK:
                        ctx.wait(t, "sS", tb["sS"] + cast_pre[g + 1])
                    else:
                        t.wait_ge(ctx.sems[GSL[g % 4]], tb[GSL[g % 4]] + 16 * gld_cnt[g])
                    for k in range(ntr_of(sched[g][3])):
                        trn = cum_tr[g] + k
                        if trn >= 8:
                            t.wait_ge(ctx.sems["vS"], tb["vS"] + trn - 7)
                        ctx.inc(t.transpose(pwt[trn % 8][:], sbin[g % 4][:, k * P:(k + 1) * P], ctx.ident[:]), "tS", 1)

            @block.vector
            def _(v):
                lastblk = None
                for g in range(G):
                    w, e, ih, kind, idx = sched[g]
                    if (w, kind) != lastblk and w >= 2:
                        sm = wsem(w, kind)
                        v.wait_ge(ctx.sems[sm], tb[sm] + 16 * (w // 2))
                    lastblk = (w, kind)
                    for k in range(ntr_of(kind)):
                        trn = cum_tr[g] + k
                        ctx.wait(v, "tS", tb["tS"] + trn + 1)
                        src = pwt[trn % 8][:]
                        if kind == "g":
                            dst = swg[w % 2][:, k, idx * P:(idx + 1) * P]
                        elif kind == "u":
                            dst = swu[w % 2][:, k, idx * P:(idx + 1) * P]
                        else:
                            dst = swd[w % 2][:, k, idx * P:(idx + 1) * P]
                        ctx.inc(v.tensor_copy(dst, src), "vS", 1)

        with ExitStack() as ls:
            st8 = ls.enter_context(ctx.nc.sbuf_tensor("prr", [P, DK, 8], BF16))
            srwi = ls.enter_context(ctx.nc.sbuf_tensor("prri", [8, D], BF16))
            prw = ls.enter_context(ctx.nc.psum_tensor("prwp", [P, 8], BF16))
            tb = {k: ctx.n[k] for k in SEMS}

            @block.sync
            def _(s):
                s.wait_ge(ctx.sems["dG"], ctx.casts_done)
                ctx.inc(s.dma_start(out=srwi[:], in_=scr["rw_bf"][:]), "dS", 16)
                s.wait_ge(ctx.sems["dS"], ctx.n["dS"])

            @block.tensor
            def _(t):
                ctx.wait(t, "dS", tb["dS"] + 16)
                for k in range(DK):
                    if k > 0:
                        ctx.wait(t, "vS", tb["vS"] + k)
                    ctx.inc(t.transpose(prw[:], srwi[:, k * P:(k + 1) * P], ctx.ident[0:8, 0:8]), "tS", 1)

            @block.vector
            def _(v):
                for k in range(DK):
                    ctx.wait(v, "tS", tb["tS"] + k + 1)
                    ctx.inc(v.tensor_copy(st8[:, k, :], prw[:]), "vS", 1)

            @block.sync
            def _(s):
                ctx.wait(s, "vS", tb["vS"] + DK)
                ctx.inc(s.dma_start(out=r_pkm(scr["rwT"]), in_=st8[:]), "dS", 16)

    if "perf" in need:
        with ExitStack() as ls:
            stg = ls.enter_context(ctx.nc.sbuf_tensor("pstg", [P, D], BF16))
            @block.sync
            def _(s):
                s.wait_ge(ctx.sems["dG"], ctx.casts_done)
                for k in range(DK):
                    s.wait_ge(ctx.sems["dS"], ctx.n["dS"])
                    ctx.inc(s.dma_start(out=stg[:], in_=scr["qkv_bf"][2048:3072, k * P:(k + 1) * P], transpose=True), "dS", 16)
                    s.wait_ge(ctx.sems["dS"], ctx.n["dS"])
                    ctx.inc(s.dma_start(out=scr["Wbig"][k * P:(k + 1) * P, 0:D], in_=stg[:]), "dS", 16)
                for k in range(DK):
                    s.wait_ge(ctx.sems["dS"], ctx.n["dS"])
                    ctx.inc(s.dma_start(out=stg[:], in_=scr["proj_bf"][:, k * P:(k + 1) * P], transpose=True), "dS", 16)
                    s.wait_ge(ctx.sems["dS"], ctx.n["dS"])
                    ctx.inc(s.dma_start(out=scr["projT_aug"][k * P:(k + 1) * P, 0:D], in_=stg[:]), "dS", 16)
                s.wait_ge(ctx.sems["dS"], ctx.n["dS"])

        with ExitStack() as ls:
            spm = ls.enter_context(ctx.nc.sbuf_tensor("ppm", [64, 64], F32))
            spmb = ls.enter_context(ctx.nc.sbuf_tensor("ppmb", [64, 64], BF16))
            spmT = ls.enter_context(ctx.nc.sbuf_tensor("ppmT", [64, 64], BF16))
            sqh = ls.enter_context(ctx.nc.sbuf_tensor("pqh", [64, D], BF16))
            sphi = ls.enter_context(ctx.nc.sbuf_tensor("pphi", [64, D], BF16))
            sbia = ls.enter_context(ctx.nc.sbuf_tensor("pbia", [64, 1], BF16))
            sbiaf = ls.enter_context(ctx.nc.sbuf_tensor("pbiaf", [64, 1], F32))
            sbrow = ls.enter_context(ctx.nc.sbuf_tensor("pbrow", [1, 64], BF16))
            stg2 = ls.enter_context(ctx.nc.sbuf_tensor("pstg2", [P, DK, 64], BF16))
            pph = ls.enter_context(ctx.nc.psum_tensor("ppph", [64, D], F32))
            ppT = ls.enter_context(ctx.nc.psum_tensor("pppT", [64, 64], BF16))
            ppT2 = ls.enter_context(ctx.nc.psum_tensor("pppT2", [P, 64], BF16))
            pbr = ls.enter_context(ctx.nc.psum_tensor("ppbr", [1, 64], F32))

            @block.sync
            def _(s):
                ctx.inc(s.dma_start(out=spm[:], in_=inp["L0_attn_projection_matrix"][:]), "dS", 16)
                s.wait_ge(ctx.sems["dS"], ctx.n["dS"])
            t_pm = ctx.n["dS"]

            @block.vector
            def _(v):
                ctx.wait(v, "dS", t_pm)
                ctx.inc(v.tensor_copy(spmb[:], spm[:]), "vS", 1)
            v_pm = ctx.n["vS"]

            @block.tensor
            def _(t):
                ctx.wait(t, "vS", v_pm)
                ctx.inc(t.transpose(ppT[:], spmb[:], ctx.ident[0:64, 0:64]), "tS", 1)
            t_pmT = ctx.n["tS"]

            @block.vector
            def _(v):
                ctx.wait(v, "tS", t_pmT)
                ctx.inc(v.tensor_copy(spmT[:], ppT[:]), "vS", 1)
            v_pmT = ctx.n["vS"]

            for hp in range(2 * NH):
                part, h = divmod(hp, NH)
                row0 = part * D + h * HD
                col0 = D + part * D + h * HD
                tb = {k: ctx.n[k] for k in SEMS}

                @block.sync
                def _(s):
                    ctx.wait(s, "tS", tb["tS"])
                    ctx.wait(s, "vS", tb["vS"])
                    s.wait_ge(ctx.sems["dS"], tb["dS"])
                    ctx.inc(s.dma_start(out=sqh[:], in_=scr["qkv_bf"][row0:row0 + HD, :]), "dS", 16)
                    ctx.inc(s.dma_start(out=sbiaf[:], in_=inp["L0_attn_qkv_b"][row0:row0 + HD].unsqueeze(1)), "dS", 16)
                    s.wait_ge(ctx.sems["dS"], ctx.n["dS"])

                @block.vector
                def _(v):
                    ctx.wait(v, "dS", tb["dS"] + 32)
                    ctx.inc(v.tensor_copy(sbia[:], sbiaf[:]), "vS", 1)

                @block.tensor
                def _(t):
                    ctx.wait(t, "dS", tb["dS"] + 32)
                    ctx.wait(t, "vS", max(v_pmT, tb["vS"] + 1))
                    ctx.wait(t, "sS", tb["sS"])
                    t.matmul(pph[:, 0:512], spmT[:], sqh[:, 0:512], start=True, stop=True)
                    ctx.inc(t.matmul(pph[:, 512:1024], spmT[:], sqh[:, 512:1024], start=True, stop=True), "tS", 1)
                    ctx.inc(t.matmul(pbr[:], sbia[:], spmT[:], start=True, stop=True), "tS", 1)

                @block.scalar
                def _(sc):
                    ctx.wait(sc, "tS", tb["tS"] + 2)
                    ctx.inc(sc.activation(sphi[:], pph[:], AF.Copy, scale=0.125), "sS", 1)
                    ctx.inc(sc.activation(sbrow[:], pbr[:], AF.Copy, scale=0.125), "sS", 1)

                @block.tensor
                def _(t):
                    ctx.wait(t, "sS", tb["sS"] + 2)
                    for k in range(DK):
                        if k > 0:
                            ctx.wait(t, "vS", tb["vS"] + 1 + k)
                        ctx.inc(t.transpose(ppT2[:], sphi[:, k * P:(k + 1) * P], ctx.ident[0:64, 0:64]), "tS", 1)

                @block.vector
                def _(v):
                    for k in range(DK):
                        ctx.wait(v, "tS", tb["tS"] + 2 + k + 1)
                        ctx.inc(v.tensor_copy(stg2[:, k, :], ppT2[:]), "vS", 1)

                @block.sync
                def _(s):
                    ctx.wait(s, "vS", tb["vS"] + 1 + DK)
                    ctx.inc(s.dma_start(out=r_pkm(scr["Wbig"][0:D, col0:col0 + HD]), in_=stg2[:]), "dS", 16)
                    ctx.wait(s, "sS", tb["sS"] + 2)
                    ctx.inc(s.dma_start(out=scr["Wbig"][1024:1025, col0:col0 + HD], in_=sbrow[:]), "dS", 16)

    if "l1att" in need:
        _emit_l1att_prep(ctx, block, scr)
    if "l1ffn" in need:
        _emit_l1ffn_prep(ctx, block, scr)


def _emit_l1att_prep(ctx, block, scr):
    inp = ctx.inputs
    with ExitStack() as ls:
        swo = ls.enter_context(ctx.nc.sbuf_tensor("lwo", [P, D], F32))
        sgam = ls.enter_context(ctx.nc.sbuf_tensor("lgam", [P, DK], F32))
        sbet = ls.enter_context(ctx.nc.sbuf_tensor("lbet", [P, DK], BF16))
        sbetf = ls.enter_context(ctx.nc.sbuf_tensor("lbetf", [P, DK], F32))
        swof = ls.enter_context(ctx.nc.sbuf_tensor("lwof", [P, D], BF16))
        swob = ls.enter_context(ctx.nc.sbuf_tensor("lwob", [P, DK, D], BF16))
        sbrow = ls.enter_context(ctx.nc.sbuf_tensor("lbrow", [1, D], BF16))
        pbr = ls.enter_context(ctx.nc.psum_tensor("lpbr", [1, D], F32))
        tb = {k: ctx.n[k] for k in SEMS}

        @block.sync
        def _(s):
            s.wait_ge(ctx.sems["dS"], tb["dS"])
            ctx.inc(s.dma_start(out=sgam[:], in_=r_pk(inp["L1_attn_gamma"]), allow_slow_non_contiguous=True), "dS", 16)
            ctx.inc(s.dma_start(out=sbetf[:], in_=r_pk(inp["L1_attn_beta"]), allow_slow_non_contiguous=True), "dS", 16)
            s.wait_ge(ctx.sems["dS"], ctx.n["dS"])

        @block.vector
        def _(v):
            ctx.wait(v, "dS", tb["dS"] + 32)
            ctx.inc(v.tensor_copy(sbet[:], sbetf[:]), "vS", 1)

        for k in range(DK):
            kb = {kk: ctx.n[kk] for kk in SEMS}

            @block.sync
            def _(s):
                ctx.wait(s, "vS", kb["vS"])
                s.wait_ge(ctx.sems["dS"], kb["dS"])
                ctx.inc(s.dma_start(out=swo[:], in_=inp["L1_attn_Wo"][k * P:(k + 1) * P, :]), "dS", 16)
                s.wait_ge(ctx.sems["dS"], ctx.n["dS"])

            @block.vector
            def _(v):
                ctx.wait(v, "dS", kb["dS"] + 16)
                ctx.inc(v.tensor_scalar(swof[:], swo[:], sgam[:, k:k + 1], None, ALU.mult), "vS", 1)

            @block.sync
            def _(s):
                ctx.wait(s, "vS", kb["vS"] + 1)
                ctx.inc(s.dma_start(out=scr["WoP_aug"][k * P:(k + 1) * P, :], in_=swof[:]), "dS", 16)

        kb = {kk: ctx.n[kk] for kk in SEMS}

        @block.sync
        def _(s):
            ctx.wait(s, "dG", ctx.casts_done)
            s.wait_ge(ctx.sems["dS"], kb["dS"])
            ctx.inc(s.dma_start(out=swob[:], in_=r_pkm(scr["Wo_bf"])), "dS", 16)
            s.wait_ge(ctx.sems["dS"], ctx.n["dS"])

        @block.tensor
        def _(t):
            ctx.wait(t, "dS", kb["dS"] + 16)
            ctx.wait(t, "vS", kb["vS"])
            for k in range(DK):
                t.matmul(pbr[:, 0:512], sbet[:, k:k + 1], swob[:, k, 0:512], start=(k == 0), stop=(k == DK - 1))
            for k in range(DK):
                mm = t.matmul(pbr[:, 512:1024], sbet[:, k:k + 1], swob[:, k, 512:1024], start=(k == 0), stop=(k == DK - 1))
            ctx.inc(mm, "tS", 1)

        @block.vector
        def _(v):
            ctx.wait(v, "tS", kb["tS"] + 1)
            ctx.inc(v.tensor_copy(sbrow[:], pbr[:]), "vS", 1)

        @block.sync
        def _(s):
            ctx.wait(s, "vS", kb["vS"] + 1)
            ctx.inc(s.dma_start(out=scr["WoP_aug"][1024:1025, :], in_=sbrow[:]), "dS", 16)


def _emit_l1ffn_prep(ctx, block, scr):
    inp = ctx.inputs
    with ExitStack() as ls:
        swd = ls.enter_context(ctx.nc.sbuf_tensor("fwd", [P, DK, HD], F32))
        swdb = ls.enter_context(ctx.nc.sbuf_tensor("fwdb", [P, DK, HD], BF16))
        swdf = ls.enter_context(ctx.nc.sbuf_tensor("fwdf", [P, DK, HD], BF16))
        sg = ls.enter_context(ctx.nc.sbuf_tensor("fg", [P, DK], F32))
        sbf = ls.enter_context(ctx.nc.sbuf_tensor("fbf", [P, DK], F32))
        sb16 = ls.enter_context(ctx.nc.sbuf_tensor("fb16", [P, DK], BF16))
        sbd = ls.enter_context(ctx.nc.sbuf_tensor("fbd", [1, HD], BF16))
        srow = ls.enter_context(ctx.nc.sbuf_tensor("frow", [1, HD], BF16))
        pbr = ls.enter_context(ctx.nc.psum_tensor("fpbr", [1, HD], F32))
        tb0 = {k: ctx.n[k] for k in SEMS}

        @block.sync
        def _(s):
            s.wait_ge(ctx.sems["dS"], tb0["dS"])
            ctx.inc(s.dma_start(out=sg[:], in_=r_pk(inp["L1_ffn_ln_g"]), allow_slow_non_contiguous=True), "dS", 16)
            ctx.inc(s.dma_start(out=sbf[:], in_=r_pk(inp["L1_ffn_ln_b"]), allow_slow_non_contiguous=True), "dS", 16)
            s.wait_ge(ctx.sems["dS"], ctx.n["dS"])

        @block.vector
        def _(v):
            ctx.wait(v, "dS", tb0["dS"] + 32)
            ctx.inc(v.tensor_copy(sb16[:], sbf[:]), "vS", 1)

        for e in range(E):
            kb = {kk: ctx.n[kk] for kk in SEMS}

            @block.sync
            def _(s):
                ctx.wait(s, "vS", kb["vS"])
                ctx.wait(s, "tS", kb["tS"])
                s.wait_ge(ctx.sems["dS"], kb["dS"])
                ctx.inc(s.dma_start(out=swd[:], in_=r_pkm(inp["L1_ffn_W_down"][e])), "dS", 16)
                ctx.wait(s, "dG", ctx.casts_done)
                ctx.inc(s.dma_start(out=sbd[:], in_=scr["bd_bf"][e].unsqueeze(0)), "dS", 16)
                s.wait_ge(ctx.sems["dS"], ctx.n["dS"])

            @block.vector
            def _(v):
                ctx.wait(v, "dS", kb["dS"] + 32)
                for k in range(DK):
                    v.tensor_scalar(swdf[:, k, :], swd[:, k, :], sg[:, k:k + 1], None, ALU.mult)
                ctx.inc(v.tensor_copy(swdb[:], swd[:]), "vS", 1)

            @block.tensor
            def _(t):
                ctx.wait(t, "vS", kb["vS"] + 1)
                for k in range(DK):
                    t.matmul(pbr[:], sb16[:, k:k + 1], swdb[:, k, :], start=(k == 0), stop=False)
                ctx.inc(t.matmul(pbr[:], ctx.cones[0:1, 0:1], sbd[:], start=False, stop=True), "tS", 1)

            @block.vector
            def _(v):
                ctx.wait(v, "tS", kb["tS"] + 1)
                ctx.inc(v.tensor_copy(srow[:], pbr[:]), "vS", 1)

            @block.sync
            def _(s):
                ctx.wait(s, "vS", kb["vS"] + 2)
                ctx.inc(s.dma_start(out=r_pkm(scr["WdP"][e]), in_=swdf[:]), "dS", 16)
                ctx.inc(s.dma_start(out=scr["bdP"][e].unsqueeze(0), in_=srow[:]), "dS", 16)


def emit_zt(ctx, block, scr, szst, zw_fin):
    ztb = {k: ctx.n[k] for k in ["d2", "d3", "d4", "d5"]}

    @block.sync
    def _(s):
        s.wait_ge(ctx.sems["g3"], zw_fin[0])
        s.wait_ge(ctx.sems["g4"], zw_fin[1])
        for k in range(DK):
            ts_ = f"d{2 + k % 2}"
            ws_ = f"d{4 + k % 2}"
            if k >= 2:
                ctx.wait(s, ws_, ztb[ws_] + 16 * (k // 2))
            ctx.inc(s.dma_start(out=szst[k % 2][:], in_=scr["z"][:, k * P:(k + 1) * P], transpose=True), ts_, 16)
            ctx.wait(s, ts_, ztb[ts_] + 16 * (k // 2 + 1))
            ctx.inc(s.dma_start(out=scr["zT"][k * P:(k + 1) * P, :], in_=szst[k % 2][:]), ws_, 16)
        s.wait_ge(ctx.sems["d4"], ctx.n["d4"])
        s.wait_ge(ctx.sems["d5"], ctx.n["d5"])


def emit_ln(ctx, block, scr, src_ap, hold=None, defer_zt=False, xtok=None):
    ls = hold if hold is not None else ExitStack()
    try:
        en = ls.enter_context
        sx = [en(ctx.nc.sbuf_tensor(f"lnx{i}", [P, D], F32)) for i in range(3)]
        sz = [en(ctx.nc.sbuf_tensor(f"lnz{i}", [P, D], BF16)) for i in range(2)]
        st12 = [en(ctx.nc.sbuf_tensor(f"lnst{i}", [P, 12], F32)) for i in range(2)]
        smv = [en(ctx.nc.sbuf_tensor(f"lnmv{i}", [P, 2], F32)) for i in range(2)]
        ssd = [en(ctx.nc.sbuf_tensor(f"lnsd{i}", [P, 1], F32)) for i in range(2)]
        srv = [en(ctx.nc.sbuf_tensor(f"lnrv{i}", [P, 1], F32)) for i in range(2)]
        szst = [en(ctx.nc.sbuf_tensor(f"lnzt{i}", [P, T], BF16)) for i in range(2)]
        tb = {k: ctx.n[k] for k in SEMS}
        tok_x = {}
        tok_zw = {}

        @block.gpsimd
        def _(g):
            for i in range(NT):
                if xtok is not None:
                    ctx.wait(g, xtok[i][0], xtok[i][1])
                if i >= 3:
                    ctx.wait(g, "vS", tb["vS"] + 4 * (i - 2))
                sx_sem = f"g{i % 3}"
                tok_x[i] = (sx_sem, ctx.inc(g.dma_start(out=sx[i % 3][:], in_=src_ap[i * P:(i + 1) * P, :]), sx_sem, 16))
                if i >= 1:
                    ctx.wait(g, "vS", tb["vS"] + 4 * i)
                    zw_sem = f"g{3 + (i - 1) % 2}"
                    tok_zw[i - 1] = (zw_sem, ctx.inc(g.dma_start(out=scr["z"][(i - 1) * P:i * P, :], in_=sz[(i - 1) % 2][:]), zw_sem, 16))
            ctx.wait(g, "vS", tb["vS"] + 4 * NT)
            zw_sem = f"g{3 + (NT - 1) % 2}"
            tok_zw[NT - 1] = (zw_sem, ctx.inc(g.dma_start(out=scr["z"][(NT - 1) * P:NT * P, :], in_=sz[(NT - 1) % 2][:]), zw_sem, 16))

        @block.vector
        def _(v):
            for i in range(NT):
                j = i % 2
                ctx.wait(v, tok_x[i][0], tok_x[i][1])
                if i >= 2:
                    ctx.wait(v, "vS", tb["vS"] + 4 * (i - 2) + 2)
                v.bn_stats(st12[j][:, 0:6], sx[i % 3][:, 0:512])
                ctx.inc(v.bn_stats(st12[j][:, 6:12], sx[i % 3][:, 512:1024]), "vS", 1)
                ctx.wait(v, "vS", tb["vS"] + 4 * i + 1)
                if i >= 2:
                    ctx.wait(v, "sS", tb["sS"] + i - 1)
                ctx.inc(v.bn_aggr(smv[j][:], st12[j][:]), "vS", 1)
                ctx.wait(v, "sS", tb["sS"] + i + 1)
                ctx.inc(v.reciprocal(srv[j][:], ssd[j][:]), "vS", 1)
                ctx.wait(v, "vS", tb["vS"] + 4 * i + 3)
                if i >= 2:
                    ctx.wait(v, tok_zw[i - 2][0], tok_zw[i - 2][1])
                ctx.inc(v.tensor_scalar(sz[j][:], sx[i % 3][:], smv[j][:, 0:1], srv[j][:], ALU.subtract, ALU.mult), "vS", 1)

        @block.scalar
        def _(s):
            for i in range(NT):
                j = i % 2
                ctx.wait(s, "vS", tb["vS"] + 4 * i + 2)
                ctx.inc(s.activation(ssd[j][:], smv[j][:, 1:2], AF.Sqrt, bias=ctx.ceps[:, 0:1]), "sS", 1)

        zw_fin = [ctx.n["g3"], ctx.n["g4"]]
        if defer_zt:
            return (szst, zw_fin)
        emit_zt(ctx, block, scr, szst, zw_fin)
    finally:
        if hold is None:
            ls.close()
    return None


def emit_router(ctx, block, scr):
    with ExitStack() as ls:
        srw = ls.enter_context(ctx.nc.sbuf_tensor("rt_rw", [P, DK, 8], BF16))
        rtb = [ls.enter_context(ctx.nc.sbuf_tensor(f"rt_b{i}", [P, DK, 512], BF16)) for i in range(2)]
        slg = ls.enter_context(ctx.nc.sbuf_tensor("rt_lg", [32, 512], F32))
        sh0 = ls.enter_context(ctx.nc.sbuf_tensor("rt_s0", [32, 512], F32))
        sh1 = ls.enter_context(ctx.nc.sbuf_tensor("rt_s1", [32, 512], F32))
        smk = [ls.enter_context(ctx.nc.sbuf_tensor(f"rt_mk{i}", [8, 512], BF16)) for i in range(2)]
        plg = [ls.enter_context(ctx.nc.psum_tensor(f"rt_p{i}", [8, 512], F32)) for i in range(2)]
        tb = {k: ctx.n[k] for k in SEMS}
        NCH = T // 512

        tok_rtb = {}

        @block.sync
        def _(s):
            s.wait_ge(ctx.sems["dS"], tb["dS"])
            ctx.inc(s.dma_start(out=srw[:], in_=r_pkm(scr["rwT"])), "dS", 16)
            s.wait_ge(ctx.sems["dS"], ctx.n["dS"])
            for c in range(NCH):
                if c >= 2:
                    ctx.wait(s, "tS", tb["tS"] + c - 1)
                sem = f"d{c % 2}"
                tok_rtb[c] = (sem, ctx.inc(s.dma_start(out=rtb[c % 2][:], in_=r_pkm(scr["zT"][:, c * 512:(c + 1) * 512])), sem, 16))

        @block.tensor
        def _(t):
            for c in range(NCH):
                ctx.wait(t, "dS", tb["dS"] + 16)
                ctx.wait(t, tok_rtb[c][0], tok_rtb[c][1])
                if c >= 2:
                    ctx.wait(t, "vS", tb["vS"] + 1 + 9 * (c - 1) + 1)
                for k in range(DK):
                    mm = t.matmul(plg[c % 2][:], srw[:, k, :], rtb[c % 2][:, k, :], start=(k == 0), stop=(k == DK - 1))
                ctx.inc(mm, "tS", 1)

        import os
        rdbg = int(os.environ.get("RDBG", "0"))
        tok_smk = {c: (f"g{c % 2}", tb[f"g{c % 2}"] + 16 * (c // 2 + 1)) for c in range(NCH)}

        @block.vector
        def _(v):
            m1 = [(i ^ 4) if i < 8 else i for i in range(32)]
            m2 = [(i ^ 2) if i < 8 else i for i in range(32)]
            m3 = [(i ^ 1) if i < 8 else i for i in range(32)]
            v.memset(slg[:], 0.0)
            v.memset(sh0[:], 0.0)
            ctx.inc(v.memset(sh1[:], 0.0), "vS", 1)
            v.wait_ge(ctx.sems["vS"], tb["vS"] + 1)
            for c in range(NCH):
                vb = tb["vS"] + 1 + 9 * c
                ctx.wait(v, "tS", tb["tS"] + c + 1)
                if c >= 1:
                    ctx.wait(v, "vS", vb - 1)
                ctx.inc(v.tensor_copy(slg[0:8, :], plg[c % 2][:]), "vS", 1)
                ctx.wait(v, "vS", vb + 1)
                ctx.inc(v.stream_shuffle(sh0[:], slg[:], m1), "vS", 1)
                ctx.wait(v, "vS", vb + 2)
                ctx.inc(v.tensor_tensor(sh1[:], slg[:], sh0[:], ALU.max), "vS", 1)
                ctx.wait(v, "vS", vb + 3)
                ctx.inc(v.stream_shuffle(sh0[:], sh1[:], m2), "vS", 1)
                ctx.wait(v, "vS", vb + 4)
                ctx.inc(v.tensor_tensor(sh1[:], sh1[:], sh0[:], ALU.max), "vS", 1)
                ctx.wait(v, "vS", vb + 5)
                ctx.inc(v.stream_shuffle(sh0[:], sh1[:], m3), "vS", 1)
                ctx.wait(v, "vS", vb + 6)
                ctx.inc(v.tensor_tensor(sh1[:], sh1[:], sh0[:], ALU.max), "vS", 1)
                ctx.wait(v, "vS", vb + 7)
                if c >= 2:
                    ctx.wait(v, tok_smk[c - 2][0], tok_smk[c - 2][1])
                ctx.inc(v.tensor_tensor(smk[c % 2][:], slg[0:8, :], sh1[0:8, :], ALU.is_ge), "vS", 1)
                ctx.inc(v.tensor_copy(sh0[0:1, 0:1], sh1[0:1, 0:1]), "vS", 1)

        @block.gpsimd
        def _(g):
            for c in range(NCH):
                ctx.wait(g, "vS", tb["vS"] + 1 + 9 * c + 8)
                sem = f"g{c % 2}"
                assert tok_smk[c] == (sem, ctx.inc(g.dma_start(out=scr["msk"][:, c * 512:(c + 1) * 512], in_=smk[c % 2][:]), sem, 16))
            g.wait_ge(ctx.sems["g0"], ctx.n["g0"])
            g.wait_ge(ctx.sems["g1"], ctx.n["g1"])


MOE_SPARSE_MOE = True



def emit_moe_sweeps(ctx, block, scr, src_ap, dst_ap, half):
    NW = NQ * 2 * E
    b = {k: ctx.n[k] for k in (SEMS + MOE_SEMS)}

    with ExitStack() as ls:
        en = ls.enter_context
        swg = [en(ctx.nc.sbuf_tensor(f"mwg{i}", [P, DK, IHW], BF16)) for i in range(2)]
        swu = [en(ctx.nc.sbuf_tensor(f"mwu{i}", [P, DK, IHW], BF16)) for i in range(2)]
        swd = [en(ctx.nc.sbuf_tensor(f"mwd{i}", [P, ITH, D], BF16)) for i in range(2)]
        zq = en(ctx.nc.sbuf_tensor("mzq", [P, DK, QT], BF16))
        zpr = en(ctx.nc.sbuf_tensor("mzp", [P, DK, QT], BF16))
        mrq = en(ctx.nc.sbuf_tensor("mmrq", [1, E, QT], BF16))
        smkr = en(ctx.nc.sbuf_tensor("mmkr", [P, QT], BF16))
        sh = en(ctx.nc.sbuf_tensor("msh", [P, ITH, QT], BF16))
        sil = [en(ctx.nc.sbuf_tensor(f"msil{i}", [P, QT], F32)) for i in range(2)]
        sy = en(ctx.nc.sbuf_tensor("msy", [P, QTT, D], F32))
        sxq = [en(ctx.nc.sbuf_tensor(f"msxq{i}", [P, D], F32)) for i in range(2)]
        pg = [en(ctx.nc.psum_tensor(f"mpg{i}", [P, QT], F32)) for i in range(2)]
        pu = [en(ctx.nc.psum_tensor(f"mpu{i}", [P, QT], F32)) for i in range(2)]
        py = [en(ctx.nc.psum_tensor(f"mpy{i}", [P, D], F32)) for i in range(2)]

        sync_ops = []
        for w in range(NW):
            if w % (2 * E) == 0:
                sync_ops.append(("mr", w // (2 * E)))
                sync_ops.append(("ze", w // (2 * E)))
            sync_ops.append(("w", w))
        tok_ze = {}
        tok_mr = {}
        tok_w = {}

        @block.sync
        def _(s):
            for op in sync_ops:
                if op[0] == "mr":
                    q = op[1]
                    if q > 0:
                        s.wait_ge(ctx.sems["mk"], b["mk"] + 8 * q)
                    tok_mr[q] = ("d3", ctx.inc(s.dma_start(out=mrq[:], in_=scr["msk"][:, q * QT:(q + 1) * QT].unsqueeze(0)), "d3", 16))
                elif op[0] == "ze":
                    q = op[1]
                    if q > 0:
                        s.wait_ge(ctx.sems["zp"], b["zp"] + 8 * q)
                    tok_ze[q] = ("d2", ctx.inc(s.dma_start(out=zq[:], in_=r_pkm(scr["zT"][:, q * QT:(q + 1) * QT])), "d2", 16))
                else:
                    w = op[1]
                    q, r = divmod(w, 2 * E)
                    e, ih = divmod(r, 2)
                    bb = w % 2
                    sem = f"d{bb}"
                    if w >= 2:
                        s.wait_ge(ctx.sems["yr"], b["yr"] + 4 * (w - 1))
                    ihs = ih * IHW
                    ctx.inc(s.dma_start(out=swg[bb][:], in_=scr["gwT"][e][ih]), sem, 16)
                    ctx.inc(s.dma_start(out=swu[bb][:], in_=scr["uwT"][e][ih]), sem, 16)
                    tok_w[w] = (sem, ctx.inc(s.dma_start(out=swd[bb][:], in_=scr["dwT"][e][ih]), sem, 16))

        tok_xl = {}
        tok_st = {}

        @block.gpsimd
        def _(g):
            for i in range(NQ * QTT):
                j = i % 2
                if i >= 2:
                    ctx.wait(g, tok_st[i - 2][0], tok_st[i - 2][1])
                sem = f"g{2 + j}"
                tok_xl[i] = (sem, ctx.inc(g.dma_start(out=sxq[j][:], in_=src_ap[i * P:(i + 1) * P, :]), sem, 16))
                ctx.wait(g, "xo", b["xo"] + i + 1)
                tok_st[i] = (f"g{j}", ctx.inc(g.dma_start(out=dst_ap[i * P:(i + 1) * P, :], in_=sxq[j][:]), f"g{j}", 16))
            g.wait_ge(ctx.sems["g0"], ctx.n["g0"])
            g.wait_ge(ctx.sems["g1"], ctx.n["g1"])

        @block.tensor
        def _(t):
            n = 0
            m = 0
            for w in range(NW):
                q, r = divmod(w, 2 * E)
                e, ih = divmod(r, 2)
                bb = w % 2
                zidx = q * E + e
                ctx.wait(t, tok_w[w][0], tok_w[w][1])
                if ih == 0:
                    if zidx == 0:
                        ctx.wait(t, tok_mr[0][0], tok_mr[0][1])
                        ctx.inc(t.matmul(pg[0][:], ctx.cones[0:1, 0:P], mrq[0:1, 0, :], start=True, stop=True), "mk", 1)
                    ctx.wait(t, "zp", b["zp"] + zidx + 1)
                for it in range(ITH):
                    par = n % 2
                    if n >= 2:
                        ctx.wait(t, "hs", b["hs"] + n - 1)
                    for k in range(DK):
                        t.matmul(pg[par][:], swg[bb][:, k, it * P:(it + 1) * P], zpr[:, k, :], start=(k == 0), stop=(k == DK - 1))
                    for k in range(DK):
                        mm = t.matmul(pu[par][:], swu[bb][:, k, it * P:(it + 1) * P], zpr[:, k, :], start=(k == 0), stop=(k == DK - 1))
                    ctx.inc(mm, "gu", 1)
                    n += 1
                ctx.wait(t, "hs", b["hs"] + n)
                if ih == 1 and zidx + 1 < NQ * E:
                    zn = zidx + 1
                    qn, en_ = divmod(zn, E)
                    ctx.wait(t, tok_mr[qn][0], tok_mr[qn][1])
                    ctx.inc(t.matmul(pg[0][:], ctx.cones[0:1, 0:P], mrq[0:1, en_, :], start=True, stop=True), "mk", 1)
                for ts in range(QTT):
                    pyb = py[m % 2]
                    if m >= 2:
                        ctx.wait(t, "yf", b["yf"] + m - 1)
                    for it in range(ITH):
                        t.matmul(pyb[:, 0:512], sh[:, it, ts * P:(ts + 1) * P], swd[bb][:, it, 0:512], start=(it == 0), stop=(it == ITH - 1))
                        mm = t.matmul(pyb[:, 512:1024], sh[:, it, ts * P:(ts + 1) * P], swd[bb][:, it, 512:1024], start=(it == 0), stop=(it == ITH - 1))
                    ctx.inc(mm, "yr", 1)
                    m += 1

        @block.scalar
        def _(s):
            for n in range(NW * ITH):
                par = n % 2
                ctx.wait(s, "gu", b["gu"] + n + 1)
                if n >= 2:
                    ctx.wait(s, "hs", b["hs"] + n - 1)
                ctx.inc(s.activation(sil[par][:], pg[par][:], AF.Silu), "sl", 1)

        def zbuild(v, zidx):
            q, e = divmod(zidx, E)
            ctx.wait(v, "mk", b["mk"] + zidx + 1)
            if zidx >= 1:
                ctx.wait(v, "zp", b["zp"] + zidx)
            ctx.inc(v.tensor_copy(smkr[:], pg[0][:]), "mke", 1)
            ctx.wait(v, "mke", b["mke"] + zidx + 1)
            ctx.wait(v, tok_ze[q][0], tok_ze[q][1])
            ctx.wait(v, "gu", b["gu"] + ITH * 2 * zidx)
            for k in range(DK):
                ins = v.tensor_tensor(zpr[:, k, :], zq[:, k, :], smkr[:], ALU.mult)
            ctx.inc(ins, "zp", 1)

        @block.vector
        def _(v):
            n = 0
            m = 0
            zbuild(v, 0)
            for w in range(NW):
                q, r = divmod(w, 2 * E)
                e, ih = divmod(r, 2)
                zidx = q * E + e
                for it in range(ITH):
                    par = n % 2
                    ctx.wait(v, "sl", b["sl"] + n + 1)
                    ctx.inc(v.tensor_tensor(sh[:, it, :], sil[par][:], pu[par][:], ALU.mult), "hs", 1)
                    n += 1
                if ih == 1 and zidx + 1 < NQ * E:
                    zbuild(v, zidx + 1)
                for ts in range(QTT):
                    pyb = py[m % 2]
                    ctx.wait(v, "yr", b["yr"] + m + 1)
                    if w % (2 * E) == 0 and q > 0:
                        ctx.wait(v, "xo", b["xo"] + 4 * q)
                    if w % (2 * E) == 0:
                        ctx.inc(v.tensor_copy(sy[:, ts, :], pyb[:]), "yf", 1)
                    else:
                        ctx.inc(v.tensor_tensor(sy[:, ts, :], sy[:, ts, :], pyb[:], ALU.add), "yf", 1)
                    m += 1
                if w % (2 * E) == 2 * E - 1:
                    for ts in range(QTT):
                        i = q * QTT + ts
                        ctx.wait(v, "yf", b["yf"] + 64 * (q + 1))
                        ctx.wait(v, tok_xl[i][0], tok_xl[i][1])
                        ctx.inc(v.scalar_tensor_tensor(sxq[i % 2][:], sy[:, ts, :], half, sxq[i % 2][:], ALU.mult, ALU.add), "xo", 1)


PHASE_NEED = {"p1": "moe", "p2": "perf", "p3": "moe", "p4": "l1ffn", "p5": "l1att", "p6": "l1ffn"}


def make_scratch(ctx, need):
    nc = ctx.nc
    scr = {}

    def dram(name, shape, dt):
        return nc.dram_tensor(name, shape, dt, kind="Internal").ap()

    scr["x"] = dram("x_scr", [T, D], F32)
    scr["idx"] = dram("idx_scr", [T], mybir.dt.uint32)
    scr["zc"] = dram("zc_scr", [E * 768, D], BF16)
    scr["yc"] = dram("yc_scr", [E * 768, D], BF16)
    scr["z"] = dram("z_scr", [T, D], BF16)
    scr["zT"] = dram("zT_scr", [D, T], BF16)
    if "moe" in need:
        scr["msk"] = dram("msk_scr", [E, T], BF16)
        scr["gwT"] = [dram(f"gwT{e}", [2, P, DK, IHW], BF16) for e in range(E)]
        scr["uwT"] = [dram(f"uwT{e}", [2, P, DK, IHW], BF16) for e in range(E)]
        scr["dwT"] = [dram(f"dwT{e}", [2, P, ITH, D], BF16) for e in range(E)]
        scr["rw_bf"] = dram("rwbf", [E, D], BF16)
        scr["rwT"] = dram("rwT", [D, E], BF16)
    if "perf" in need:
        scr["qkv_bf"] = dram("qkvbf", [3 * D, D], BF16)
        scr["proj_bf"] = dram("projbf", [D, D], BF16)
        scr["Wbig"] = dram("Wbig", [D + 1, 3 * D], BF16)
        scr["projT_aug"] = dram("projTa", [D + 1, D], BF16)
    if "l1att" in need:
        scr["Wq_bf"] = dram("Wqbf", [D, D], BF16)
        scr["Wk_bf"] = dram("Wkbf", [D, D], BF16)
        scr["Wv_bf"] = dram("Wvbf", [D, D], BF16)
        scr["Wg1_bf"] = dram("Wg1bf", [D, HD], BF16)
        scr["Wg2_bf"] = dram("Wg2bf", [HD, D], BF16)
        scr["Wo_bf"] = dram("Wobf", [D, D], BF16)
        scr["WoP_aug"] = dram("WoPa", [D + 1, D], BF16)
    if "l1ffn" in need:
        scr["Wup_bf"] = dram("Wupbf", [E, HD, D], BF16)
        scr["bup_bf"] = dram("bupbf", [E, D], BF16)
        scr["br_bf"] = dram("brbf", [1, E], BF16)
        scr["bd_bf"] = dram("bdbf", [E, HD], BF16)
        scr["WdP"] = [dram(f"WdP{e}", [D, HD], BF16) for e in range(E)]
        scr["bdP"] = dram("bdP", [E, HD], BF16)
    return scr


INPUT_NAMES = [
    "L0_attn_qkv_w", "L0_attn_qkv_b", "L0_attn_proj_w", "L0_attn_proj_b",
    "L0_attn_projection_matrix", "L0_ffn_router_w", "L0_ffn_gate_w",
    "L0_ffn_up_w", "L0_ffn_down_w", "L1_attn_Wq", "L1_attn_Wk", "L1_attn_Wv",
    "L1_attn_Wo", "L1_attn_Wg1", "L1_attn_Wg2", "L1_attn_gamma",
    "L1_attn_beta", "L1_ffn_ln_g", "L1_ffn_ln_b", "L1_ffn_bn_g",
    "L1_ffn_bn_b", "L1_ffn_bn_mean", "L1_ffn_bn_var", "L1_ffn_Wr",
    "L1_ffn_br", "L1_ffn_W_down", "L1_ffn_b_down", "L1_ffn_W_up",
    "L1_ffn_b_up",
]
INPUT_SHAPES = {
    "L0_attn_qkv_w": [3 * D, D], "L0_attn_qkv_b": [3 * D],
    "L0_attn_proj_w": [D, D], "L0_attn_proj_b": [D],
    "L0_attn_projection_matrix": [64, 64], "L0_ffn_router_w": [E, D],
    "L0_ffn_gate_w": [E, II, D], "L0_ffn_up_w": [E, II, D],
    "L0_ffn_down_w": [E, D, II], "L1_attn_Wq": [D, D], "L1_attn_Wk": [D, D],
    "L1_attn_Wv": [D, D], "L1_attn_Wo": [D, D], "L1_attn_Wg1": [D, HD],
    "L1_attn_Wg2": [HD, D], "L1_attn_gamma": [D], "L1_attn_beta": [D],
    "L1_ffn_ln_g": [D], "L1_ffn_ln_b": [D], "L1_ffn_bn_g": [D],
    "L1_ffn_bn_b": [D], "L1_ffn_bn_mean": [D], "L1_ffn_bn_var": [D],
    "L1_ffn_Wr": [D, E], "L1_ffn_br": [E], "L1_ffn_W_down": [E, D, HD],
    "L1_ffn_b_down": [E, HD], "L1_ffn_W_up": [E, HD, D], "L1_ffn_b_up": [E, D],
}


def build_nc(phases, parts="all", dbg=False):
    nc = bass.Bass("TRN2", target_bir_lowering=False, debug=False)
    stack = ExitStack()
    ctx = Ctx(NcProxy(nc), stack)
    need = sorted({PHASE_NEED[p] for p in phases})
    with stack:
        ctx.sem_init()
        for s in MOE_SEMS:
            ctx.sems[s] = stack.enter_context(nc.semaphore(s))
            ctx.n[s] = 0
        for nm in INPUT_NAMES:
            ctx.inputs[nm] = nc.dram_tensor(nm, INPUT_SHAPES[nm], F32, kind="ExternalInput").ap()
        xin = nc.dram_tensor("xin", [T, D], F32, kind="ExternalInput").ap()
        xout = nc.dram_tensor("xout", [T, D], F32, kind="ExternalOutput").ap()
        c_id = nc.dram_tensor("c_ident", [P, P], BF16, kind="ExternalInput").ap()
        c_on = nc.dram_tensor("c_ones", [1, 512], BF16, kind="ExternalInput").ap()
        c_onf = nc.dram_tensor("c_onesf", [1, 512], F32, kind="ExternalInput").ap()
        c_onc = nc.dram_tensor("c_onec", [P, 1], BF16, kind="ExternalInput").ap()
        c_cm = nc.dram_tensor("c_cmask", [4 * P, 512], BF16, kind="ExternalInput").ap()
        ctx.c_e8 = nc.dram_tensor("c_e8", [8, 2], F32, kind="ExternalInput").ap()
        ctx.c_ecap = nc.dram_tensor("c_ecap", [1, 256], F32, kind="ExternalInput").ap()
        c_tri = nc.dram_tensor("c_tri", [P, P], BF16, kind="ExternalInput").ap()
        c_rc = nc.dram_tensor("c_rotc", [SLEN, HD], F32, kind="ExternalInput").ap()
        c_rs = nc.dram_tensor("c_rots", [SLEN, HD], F32, kind="ExternalInput").ap()

        scr = make_scratch(ctx, need)
        ctx.ident = ctx.sb("c_ident_sb", [P, P], BF16)
        ctx.cones = ctx.sb("c_ones_sb", [1, 512], BF16)
        ctx.conesf = ctx.sb("c_onesf_sb", [1, 512], F32)
        ctx.conec = ctx.sb("c_onec_sb", [P, 1], BF16)
        ctx.cmask = ctx.sb("c_cmask_sb", [P, 4, 512], BF16)
        ctx.ctri = ctx.sb("c_tri_sb", [P, P], BF16)
        ctx.ceps = ctx.sb("c_eps_sb", [P, 2], F32)

        block = stack.enter_context(nc.Block())

        @block.sync
        def _(s):
            ctx.inc(s.dma_start(out=ctx.ident[:], in_=c_id[:]), "dS", 16)
            ctx.inc(s.dma_start(out=ctx.cones[:], in_=c_on[:]), "dS", 16)
            ctx.inc(s.dma_start(out=ctx.conesf[:], in_=c_onf[:]), "dS", 16)
            ctx.inc(s.dma_start(out=ctx.conec[:], in_=c_onc[:]), "dS", 16)
            ctx.inc(s.dma_start(out=ctx.cmask[:], in_=c_cm.rearrange("(d p) m -> p d m", p=P)), "dS", 16)
            ctx.inc(s.dma_start(out=ctx.ctri[:], in_=c_tri[:]), "dS", 16)
            s.wait_ge(ctx.sems["dS"], ctx.n["dS"])
        cload = ctx.n["dS"]

        @block.tensor
        def _(t):
            ctx.wait(t, "dS", cload)

        @block.vector
        def _(v):
            ctx.wait(v, "dS", cload)
            v.memset(ctx.ceps[:, 0:1], 1e-5)
            ctx.inc(v.memset(ctx.ceps[:, 1:2], 1e-6), "vS", 1)

        emit_prologue(ctx, block, scr, need, stage="casts")
        first_ln_done = False
        if phases and phases[0] in ("p1", "p3") and parts in ("all", "ln", "router", "sweeps"):
            with ExitStack() as lnscope:
                ztinfo = emit_ln(ctx, block, scr, xin, hold=lnscope, defer_zt=True)
                emit_prologue(ctx, block, scr, need, stage="rest")
                emit_zt(ctx, block, scr, *ztinfo)
                barrier(ctx, block)
            first_ln_done = True
        else:
            emit_prologue(ctx, block, scr, need, stage="rest")
            barrier(ctx, block)

        for pi, ph in enumerate(phases):
            src = xin if pi == 0 else scr["x"]
            dst = xout if pi == len(phases) - 1 else scr["x"]
            xtk = getattr(ctx, "last_xtok", None) if pi > 0 else None
            ctx.last_xtok = None

            def pre_ln():
                if xtk is None:
                    barrier(ctx, block)
                    emit_ln(ctx, block, scr, src)
                else:
                    emit_ln(ctx, block, scr, src, xtok=xtk)
                barrier(ctx, block)

            if ph in ("p1", "p3"):
                if parts in ("all", "ln", "router", "sweeps"):
                    if not (pi == 0 and first_ln_done):
                        pre_ln()
                if parts in ("all", "router", "sweeps"):
                    emit_router(ctx, block, scr)
                    barrier(ctx, block)
                if parts in ("all", "sweeps"):
                    if SPARSE_MOE:
                        with ExitStack() as ms:
                            sidx = ms.enter_context(nc.sbuf_tensor(f"sidx{pi}", [P, NT], U32))
                            emit_moe_indices(ctx, block, scr, sidx)
                            barrier(ctx, block)
                            emit_moe_sparse(ctx, block, scr, src, dst, 0.5, sidx)
                    else:
                        emit_moe_sweeps(ctx, block, scr, src, dst, 0.5)
            elif ph == "p2":
                pre_ln()
                emit_perf(ctx, block, scr, src, dst)
            elif ph in ("p4", "p6"):
                pre_ln()
                emit_l1ffn(ctx, block, scr, src, dst)
            elif ph == "p5":
                pre_ln()
                emit_l1att(ctx, block, scr, src, dst, (c_rc, c_rs))
        barrier(ctx, block)

        if dbg:
            dz = nc.dram_tensor("dbg_z", [T, D], BF16, kind="ExternalOutput").ap()
            dzT = nc.dram_tensor("dbg_zT", [D, T], BF16, kind="ExternalOutput").ap()
            dmsk = nc.dram_tensor("dbg_msk", [E, T], BF16, kind="ExternalOutput").ap()

            @block.sync
            def _(s):
                ctx.inc(s.dma_start(out=dz, in_=scr["z"]), "dS", 16)
                ctx.inc(s.dma_start(out=dzT, in_=scr["zT"]), "dS", 16)
                if "msk" in scr:
                    ctx.inc(s.dma_start(out=dmsk, in_=scr["msk"]), "dS", 16)

        @block.sync
        def _(s):
            ctx.wait(s, "dS", ctx.n["dS"])

        @block.gpsimd
        def _(g):
            ctx.wait(g, "dG", ctx.n["dG"])
        barrier(ctx, block)
    return nc


def _host_consts():
    cs = {}
    cs["c_ident"] = np.eye(P).astype(bf16)
    cs["c_ones"] = np.ones((1, 512), dtype=bf16)
    cs["c_onesf"] = np.ones((1, 512), dtype=np.float32)
    cs["c_onec"] = np.ones((P, 1), dtype=bf16)
    cm = np.zeros((4, P, 512), dtype=np.float32)
    for d in range(4):
        for sl in range(P):
            cm[d, sl, d * P + sl:] = 1.0
    cs["c_cmask"] = cm.reshape(4 * P, 512).astype(bf16)
    cs["c_e8"] = np.stack([np.ones(8), np.arange(8)], axis=1).astype(np.float32)
    cs["c_ecap"] = np.tile(np.arange(8) * CAP - 1.0, NT).reshape(1, 256).astype(np.float32)
    cs["c_tri"] = np.triu(np.ones((P, P))).astype(bf16)
    inv_freq = (10000.0 ** (-(np.arange(HD // 2, dtype=np.float64) * 2.0) / HD))
    pos = np.arange(SLEN, dtype=np.float64)
    ang = pos[:, None] * inv_freq[None, :]
    cosv = np.cos(ang).astype(np.float32)
    sinv = np.sin(ang).astype(np.float32)
    cs["c_rotc"] = np.concatenate([cosv, cosv], axis=1)
    cs["c_rots"] = np.concatenate([-sinv, sinv], axis=1)
    return cs


def run_phases(phases, x_full, weights):
    nc = build_nc(phases)
    cs = _host_consts()
    in_maps = []
    for c in range(NCORES):
        m = {nm: weights[nm] for nm in INPUT_NAMES}
        m.update(cs)
        m["xin"] = np.ascontiguousarray(x_full[2 * c:2 * c + 2].reshape(T, D))
        in_maps.append(m)
    res = run_bass_kernel_spmd(nc, in_maps, core_ids=list(range(NCORES)))
    out = np.stack([res.results[c]["xout"].reshape(2, SLEN, D) for c in range(NCORES)])
    return out.reshape(16, SLEN, D)


def kernel(**inputs):
    w = {nm: np.ascontiguousarray(np.asarray(inputs[nm], dtype=np.float32)) for nm in INPUT_NAMES}
    x = np.asarray(inputs["L0_attn_x"], dtype=np.float32)
    x = run_phases(["p1", "p2", "p3", "p4", "p5", "p6"], x, w)
    return x.astype(np.float32)


def emit_l1ffn(ctx, block, scr, src_ap, dst_ap):
    inp = ctx.inputs
    NCH = T // 512
    CPS = NCH // NSENT

    with ExitStack() as outer:
        eno = outer.enter_context
        sscl = eno(ctx.nc.sbuf_tensor("f4scl", [1, E, NSENT], F32))
        srep = eno(ctx.nc.sbuf_tensor("f4rep", [64, E, NSENT], F32))
        srows = eno(ctx.nc.sbuf_tensor("f4rows", [1, NSENT, E, P], BF16))

        with ExitStack() as ls:
            en = ls.enter_context
            rtb = [en(ctx.nc.sbuf_tensor(f"f4b{i}", [P, DK, 512], BF16)) for i in range(2)]
            szs = en(ctx.nc.sbuf_tensor("f4zs", [P, DK, NSENT], F32))
            sr8 = en(ctx.nc.sbuf_tensor("f4r8", [P, DK], F32))
            sWr = en(ctx.nc.sbuf_tensor("f4wr", [P, DK, E], F32))
            sbr = en(ctx.nc.sbuf_tensor("f4br", [1, E], F32))
            scols = en(ctx.nc.sbuf_tensor("f4cols", [P, DK, 6], F32))
            sbmm = en(ctx.nc.sbuf_tensor("f4bmm", [P, DK], F32))
            srs = en(ctx.nc.sbuf_tensor("f4rs", [P, DK], F32))
            ssd2 = en(ctx.nc.sbuf_tensor("f4sd", [P, DK], F32))
            szbn = en(ctx.nc.sbuf_tensor("f4zbn", [P, DK, NSENT], F32))
            slog = en(ctx.nc.sbuf_tensor("f4log", [1, E], F32))
            smx = en(ctx.nc.sbuf_tensor("f4mx", [1, 1], F32))
            sex = en(ctx.nc.sbuf_tensor("f4ex", [1, E], F32))
            ssum = en(ctx.nc.sbuf_tensor("f4sum", [1, 1], F32))
            sprob = en(ctx.nc.sbuf_tensor("f4prob", [1, E], F32))
            sgate = en(ctx.nc.sbuf_tensor("f4gate", [1, 1], F32))
            smsk1 = en(ctx.nc.sbuf_tensor("f4msk", [1, E], F32))
            plog = en(ctx.nc.psum_tensor("f4plog", [1, E], F32))
            prep = en(ctx.nc.psum_tensor("f4prep", [64, E], F32))
            prow = en(ctx.nc.psum_tensor("f4prow", [1, P], F32))
            tb = {k: ctx.n[k] for k in SEMS}
            tok_c = {}

            @block.sync
            def _(s):
                s.wait_ge(ctx.sems["dS"], tb["dS"])
                ctx.inc(s.dma_start(out=sWr[:], in_=r_pkm(inp["L1_ffn_Wr"])), "dS", 16)
                ctx.inc(s.dma_start(out=sbr[:], in_=inp["L1_ffn_br"].unsqueeze(0)), "dS", 16)
                for j, nm in enumerate(["ln_g", "ln_b", "bn_g", "bn_b", "bn_mean", "bn_var"]):
                    ctx.inc(s.dma_start(out=scols[:, :, j], in_=r_pk(inp[f"L1_ffn_{nm}"]), allow_slow_non_contiguous=True), "dS", 16)
                s.wait_ge(ctx.sems["dS"], ctx.n["dS"])
                for c in range(NCH):
                    if c >= 2:
                        ctx.wait(s, "vS", tb["vS"] + 2 * (c - 1))
                    sem = f"d{c % 2}"
                    tok_c[c] = (sem, ctx.inc(s.dma_start(out=rtb[c % 2][:], in_=r_pkm(scr["zT"][:, c * 512:(c + 1) * 512])), sem, 16))
            ldtok = tb["dS"] + 16 * 8

            @block.vector
            def _(v):
                ctx.wait(v, "dS", ldtok)
                for c in range(NCH):
                    s_ = c // CPS
                    ctx.wait(v, tok_c[c][0], tok_c[c][1])
                    ctx.inc(v.tensor_reduce(sr8[:].unsqueeze(2), rtb[c % 2][:], axis=AX.X, op=ALU.add), "vS", 1)
                    ctx.wait(v, "vS", ctx.n["vS"])
                    if c % CPS == 0:
                        ctx.inc(v.tensor_copy(szs[:, :, s_], sr8[:]), "vS", 1)
                    else:
                        ctx.inc(v.tensor_tensor(szs[:, :, s_], szs[:, :, s_], sr8[:], ALU.add), "vS", 1)
                    ctx.wait(v, "vS", ctx.n["vS"])
                ctx.inc(v.tensor_tensor(sbmm[:], scols[:, :, 1], scols[:, :, 4], ALU.subtract), "vS", 1)
                ctx.wait(v, "sS", tb["sS"] + 1)
                ctx.inc(v.reciprocal(srs[:], ssd2[:]), "vS", 1)
                ctx.wait(v, "vS", ctx.n["vS"])
                for s_ in range(NSENT):
                    ctx.inc(v.tensor_scalar(szbn[:, :, s_], szs[:, :, s_], 1.0 / SLEN, None, ALU.mult), "vS", 1)
                    ctx.wait(v, "vS", ctx.n["vS"])
                    for op, other in [(ALU.mult, scols[:, :, 0]), (ALU.add, sbmm[:]), (ALU.mult, srs[:]),
                                      (ALU.mult, scols[:, :, 2]), (ALU.add, scols[:, :, 3])]:
                        ctx.inc(v.tensor_tensor(szbn[:, :, s_], szbn[:, :, s_], other, op), "vS", 1)
                        ctx.wait(v, "vS", ctx.n["vS"])
            vA = ctx.n["vS"]

            @block.scalar
            def _(s):
                ctx.wait(s, "dS", ldtok)
                ctx.wait(s, "vS", tb["vS"] + 2 * NCH + 1)
                ctx.inc(s.activation(ssd2[:], scols[:, :, 5], AF.Sqrt, bias=ctx.ceps[:, 0:1]), "sS", 1)

            for s_ in range(NSENT):
                kb = {k: ctx.n[k] for k in SEMS}

                @block.tensor
                def _(t):
                    ctx.wait(t, "vS", vA if s_ == 0 else kb["vS"])
                    ctx.wait(t, "dS", ldtok)
                    for k in range(DK):
                        t.matmul(plog[:], szbn[:, k:k + 1, s_], sWr[:, k, :], start=(k == 0), stop=False)
                    ctx.inc(t.matmul(plog[:], ctx.conesf[0:1, 0:1], sbr[:], start=False, stop=True), "tS", 1)

                @block.vector
                def _(v):
                    ctx.wait(v, "tS", kb["tS"] + 1)
                    for fn in [lambda: v.tensor_copy(slog[:], plog[:]),
                               lambda: v.reduce_max(smx[:], slog[:], axis=AX.X),
                               lambda: v.tensor_scalar(sex[:], slog[:], smx[:], None, ALU.subtract)]:
                        ctx.inc(fn(), "vS", 1)
                        ctx.wait(v, "vS", ctx.n["vS"])

                @block.scalar
                def _(s):
                    ctx.wait(s, "vS", kb["vS"] + 3)
                    ctx.inc(s.activation(sex[:], sex[:], AF.Exp), "sS", 1)

                @block.vector
                def _(v):
                    ctx.wait(v, "sS", kb["sS"] + 1)
                    for fn in [lambda: v.reduce_sum(ssum[:], sex[:], axis=AX.X),
                               lambda: v.reciprocal(ssum[:], ssum[:]),
                               lambda: v.tensor_scalar(sprob[:], sex[:], ssum[:], None, ALU.mult),
                               lambda: v.reduce_max(sgate[:], sprob[:], axis=AX.X),
                               lambda: v.tensor_scalar(smsk1[:], sprob[:], sgate[:], None, ALU.is_ge),
                               lambda: v.tensor_scalar(sscl[:, :, s_], smsk1[:], sgate[:], 0.5, ALU.mult, ALU.mult)]:
                        ctx.inc(fn(), "vS", 1)
                        ctx.wait(v, "vS", ctx.n["vS"])

                @block.tensor
                def _(t):
                    ctx.wait(t, "vS", kb["vS"] + 9)
                    ctx.inc(t.matmul(prep[:], ctx.conesf[0:1, 0:64], sscl[0:1, :, s_], start=True, stop=True), "tS", 1)
                    for e in range(E):
                        ctx.wait(t, "vS", kb["vS"] + 10 + e)
                        ctx.inc(t.matmul(prow[:], sscl[0:1, e:e + 1, s_], ctx.conesf[0:1, 0:P], start=True, stop=True), "tS", 1)

                @block.vector
                def _(v):
                    ctx.wait(v, "tS", kb["tS"] + 2)
                    ctx.inc(v.tensor_copy(srep[:, :, s_], prep[:]), "vS", 1)
                    for e in range(E):
                        ctx.wait(v, "tS", kb["tS"] + 2 + e + 1)
                        ctx.inc(v.tensor_copy(srows[:, s_, e, :], prow[:]), "vS", 1)

        barrier(ctx, block)

        with ExitStack() as ls:
            en = ls.enter_context
            rtb = [en(ctx.nc.sbuf_tensor(f"f4c{i}", [P, DK, 512], BF16)) for i in range(2)]
            sWd = en(ctx.nc.sbuf_tensor("f4wd", [P, DK, E, HD], BF16))
            sbdr = en(ctx.nc.sbuf_tensor("f4bdr", [1, E, HD], BF16))
            sWu = en(ctx.nc.sbuf_tensor("f4wu", [64, E, D], BF16))
            sbur = en(ctx.nc.sbuf_tensor("f4bur", [1, E, D], BF16))
            shtf = en(ctx.nc.sbuf_tensor("f4htf", [64, 512], F32))
            sht = en(ctx.nc.sbuf_tensor("f4ht", [64, E, 512], BF16))
            sxf = [en(ctx.nc.sbuf_tensor(f"f4x{i}", [P, D], F32)) for i in range(2)]
            phT = [en(ctx.nc.psum_tensor(f"f4ph{i}", [64, 512], F32)) for i in range(2)]
            py = [en(ctx.nc.psum_tensor(f"f4py{i}", [P, D], F32)) for i in range(2)]
            b = {k: ctx.n[k] for k in (SEMS + MOE_SEMS)}
            tok_c = {}
            tok_xg = {}
            tok_st = {}

            @block.sync
            def _(s):
                s.wait_ge(ctx.sems["dS"], b["dS"])
                for e in range(E):
                    ctx.inc(s.dma_start(out=sWd[:, :, e, :], in_=r_pkm(scr["WdP"][e])), "dS", 16)
                    ctx.inc(s.dma_start(out=sbdr[:, e, :], in_=scr["bdP"][e].unsqueeze(0)), "dS", 16)
                ctx.inc(s.dma_start(out=sWu[:], in_=scr["Wup_bf"].rearrange("e p m -> p e m")), "dS", 16)
                ctx.inc(s.dma_start(out=sbur[:], in_=scr["bup_bf"].unsqueeze(0)), "dS", 16)
                s.wait_ge(ctx.sems["dS"], ctx.n["dS"])
                for gc in range(NCH):
                    if gc >= 2:
                        ctx.wait(s, "gu", b["gu"] + E * (gc - 1))
                    sem = f"d{gc % 2}"
                    tok_c[gc] = (sem, ctx.inc(s.dma_start(out=rtb[gc % 2][:], in_=r_pkm(scr["zT"][:, gc * 512:(gc + 1) * 512])), sem, 16))
            wtok = b["dS"] + 16 * 18

            @block.gpsimd
            def _(g):
                for i in range(NT):
                    j = i % 2
                    if i >= 2:
                        ctx.wait(g, tok_st[i - 2][0], tok_st[i - 2][1])
                    sem = f"g{2 + j}"
                    tok_xg[i] = (sem, ctx.inc(g.dma_start(out=sxf[j][:], in_=src_ap[i * P:(i + 1) * P, :]), sem, 16))
                    ctx.wait(g, "yf", b["yf"] + i + 1)
                    tok_st[i] = (f"g{j}", ctx.inc(g.dma_start(out=dst_ap[i * P:(i + 1) * P, :], in_=sxf[j][:]), f"g{j}", 16))

            @block.tensor
            def _(t):
                n = 0
                m = 0
                ctx.wait(t, "dS", wtok)
                for gc in range(NCH):
                    s_ = gc // CPS
                    ctx.wait(t, tok_c[gc][0], tok_c[gc][1])
                    for e in range(E):
                        par = n % 2
                        if n >= 2:
                            ctx.wait(t, "sl", b["sl"] + n - 1)
                        for k in range(DK):
                            t.matmul(phT[par][:], sWd[:, k, e, :], rtb[gc % 2][:, k, :], start=(k == 0), stop=False)
                        mm = t.matmul(phT[par][:], sbdr[:, e, :], ctx.cones[0:1, 0:512], start=False, stop=True)
                        ctx.inc(mm, "gu", 1)
                        n += 1
                    ctx.wait(t, "hs", b["hs"] + E * (gc + 1))
                    for ts in range(4):
                        pyb = py[m % 2]
                        if m >= 2:
                            ctx.wait(t, "yf", b["yf"] + m - 1)
                        for e in range(E):
                            t.matmul(pyb[:, 0:512], sht[:, e, ts * P:(ts + 1) * P], sWu[:, e, 0:512], start=(e == 0), stop=False)
                            t.matmul(pyb[:, 512:1024], sht[:, e, ts * P:(ts + 1) * P], sWu[:, e, 512:1024], start=(e == 0), stop=False)
                            t.matmul(pyb[:, 0:512], srows[:, s_, e, :], sbur[:, e, 0:512], start=False, stop=(e == E - 1))
                            mm = t.matmul(pyb[:, 512:1024], srows[:, s_, e, :], sbur[:, e, 512:1024], start=False, stop=(e == E - 1))
                        ctx.inc(mm, "yr", 1)
                        m += 1

            @block.scalar
            def _(s):
                for n in range(NCH * E):
                    par = n % 2
                    ctx.wait(s, "gu", b["gu"] + n + 1)
                    if n >= 1:
                        ctx.wait(s, "hs", b["hs"] + n)
                    ctx.inc(s.activation(shtf[:], phT[par][:], AF.Relu), "sl", 1)

            @block.vector
            def _(v):
                n = 0
                m = 0
                for gc in range(NCH):
                    s_ = gc // CPS
                    for e in range(E):
                        ctx.wait(v, "sl", b["sl"] + n + 1)
                        if gc >= 1 and e == 0:
                            ctx.wait(v, "yr", b["yr"] + 4 * gc)
                        ctx.inc(v.tensor_scalar(sht[:, e, :], shtf[:], srep[:, e:e + 1, s_], None, ALU.mult), "hs", 1)
                        n += 1
                    for ts in range(4):
                        i = gc * 4 + ts
                        ctx.wait(v, "yr", b["yr"] + m + 1)
                        ctx.wait(v, tok_xg[i][0], tok_xg[i][1])
                        ctx.inc(v.tensor_tensor(sxf[i % 2][:], sxf[i % 2][:], py[m % 2][:], ALU.add), "yf", 1)
                        m += 1

            ctx.last_xtok = dict(tok_st)


def _tok24(nh):
    return 24 * (nh // NH) + nh % NH


def _l5_e2(nh):
    return 96 * (nh // NH) + 4 * (nh % NH)


def emit_l1att(ctx, block, scr, src_ap, dst_ap, rot):
    c_rc, c_rs = rot
    NTS = SLEN // P

    with ExitStack() as outer:
        eno = outer.enter_context
        sv = eno(ctx.nc.sbuf_tensor("a5v", [P, NTS, D], BF16))
        sqT = eno(ctx.nc.sbuf_tensor("a5qT", [P, NH // 2, SLEN], BF16))
        skT = eno(ctx.nc.sbuf_tensor("a5kT", [P, NH // 2, SLEN], BF16))
        sg1T = eno(ctx.nc.sbuf_tensor("a5g1", [64, 2 * 4, 512], BF16))
        srotc = [eno(ctx.nc.sbuf_tensor(f"a5rc{i}", [P, HD], F32)) for i in range(2)]
        srots = [eno(ctx.nc.sbuf_tensor(f"a5rs{i}", [P, HD], F32)) for i in range(2)]
        sWg2 = eno(ctx.nc.sbuf_tensor("a5wg2", [64, D], BF16))
        sWoP = eno(ctx.nc.sbuf_tensor("a5wop", [P, DK, D], BF16))
        sWoB = eno(ctx.nc.sbuf_tensor("a5wob", [1, D], BF16))

        for s_ in range(NSENT):
            with ExitStack() as ls:
                en = ls.enter_context
                sWq = en(ctx.nc.sbuf_tensor(f"s{s_}a5wq", [P, DK, D], BF16))
                sWk = en(ctx.nc.sbuf_tensor(f"s{s_}a5wk", [P, DK, D], BF16))
                sWv = en(ctx.nc.sbuf_tensor(f"s{s_}a5wv", [P, DK, D], BF16))
                sWg1 = en(ctx.nc.sbuf_tensor(f"s{s_}a5wg1", [P, DK, HD], BF16))
                szt = [en(ctx.nc.sbuf_tensor(f"s{s_}a5zt{i}", [P, DK, P], BF16)) for i in range(2)]
                rtb = [en(ctx.nc.sbuf_tensor(f"s{s_}a5rt0", [P, DK, 512], BF16))] * 2
                stmpf = en(ctx.nc.sbuf_tensor(f"s{s_}a5tmf", [P, D], F32))
                sswap = en(ctx.nc.sbuf_tensor(f"s{s_}a5swp", [P, D], F32))
                srotq = en(ctx.nc.sbuf_tensor(f"s{s_}a5rq", [P, D], BF16))
                pqkv = [en(ctx.nc.psum_tensor(f"s{s_}a5pq{i}", [P, D], F32)) for i in range(2)]
                ptr = [en(ctx.nc.psum_tensor(f"s{s_}a5ptr{i}", [64, P], BF16)) for i in range(2)]
                pg1 = en(ctx.nc.psum_tensor(f"s{s_}a5pg1", [64, 512], F32))
                b = {k: ctx.n[k] for k in (SEMS + MOE_SEMS)}
                tok_z = {}
                tok_g = {}

                def qkv_free(n):
                    m = n - 2
                    i2, p2 = divmod(m, 3)
                    if p2 <= 1:
                        return ("sl", b["sl"] + 2 * i2 + p2 + 1)
                    return ("hs", b["hs"] + i2 + 1)

                @block.sync
                def _(s):
                    s.wait_ge(ctx.sems["dS"], b["dS"])
                    if s_ == 0:
                        ctx.inc(s.dma_start(out=sWg2[:], in_=scr["Wg2_bf"][:]), "dS", 16)
                        ctx.inc(s.dma_start(out=sWoP[:], in_=r_pkm(scr["WoP_aug"][0:D, :])), "dS", 16)
                        ctx.inc(s.dma_start(out=sWoB[:], in_=scr["WoP_aug"][1024:1025, :]), "dS", 16)
                    ctx.inc(s.dma_start(out=sWq[:], in_=r_pkm(scr["Wq_bf"])), "dS", 16)
                    ctx.inc(s.dma_start(out=sWk[:], in_=r_pkm(scr["Wk_bf"])), "dS", 16)
                    ctx.inc(s.dma_start(out=sWv[:], in_=r_pkm(scr["Wv_bf"])), "dS", 16)
                    ctx.inc(s.dma_start(out=sWg1[:], in_=r_pkm(scr["Wg1_bf"])), "dS", 16)
                    s.wait_ge(ctx.sems["dS"], ctx.n["dS"])
                    for i in range(NTS):
                        if i >= 2:
                            ctx.wait(s, "gu", b["gu"] + 3 * (i - 1))
                        sem = f"d{i % 2}"
                        tok_z[i] = (sem, ctx.inc(s.dma_start(out=szt[i % 2][:], in_=r_pkm(scr["zT"][:, (s_ * SLEN + i * P):(s_ * SLEN + (i + 1) * P)])), sem, 16))
                        rsem = f"d{4 + i % 2}"
                        if i >= 2:
                            ctx.wait(s, "zp", b["zp"] + 2 * i - 2)
                        s.dma_start(out=srotc[i % 2][:], in_=c_rc[i * P:(i + 1) * P, :]).then_inc(ctx.sems[rsem], 16)
                        ctx.n[rsem] += 16
                        tok_r = ctx.inc(s.dma_start(out=srots[i % 2][:], in_=c_rs[i * P:(i + 1) * P, :]), rsem, 16)
                        tok_z[(i, "rot")] = (rsem, tok_r)
                    for c in range(4):
                        if c >= 1:
                            ctx.wait(s, "yr", b["yr"] + c)
                        sem = "d2"
                        tok_g[c] = (sem, ctx.inc(s.dma_start(out=rtb[0][:], in_=r_pkm(scr["zT"][:, (s_ * SLEN + c * 512):(s_ * SLEN + (c + 1) * 512)])), sem, 16))
                wtok = ctx.n["dS"]

                @block.tensor
                def _(t):
                    ctx.wait(t, "dS", wtok)
                    ntr = 0
                    for i in range(NTS):
                        ctx.wait(t, tok_z[i][0], tok_z[i][1])
                        for part, W in [(0, sWq), (1, sWk), (2, sWv)]:
                            n = 3 * i + part
                            par = n % 2
                            if n >= 2:
                                fs, ft = qkv_free(n)
                                ctx.wait(t, fs, ft)
                            for half in range(2):
                                for k in range(DK):
                                    mm = t.matmul(pqkv[par][:, half * 512:(half + 1) * 512], szt[i % 2][:, k, :],
                                                  W[:, k, half * 512:(half + 1) * 512], start=(k == 0), stop=(k == DK - 1))
                            ctx.inc(mm, "gu", 1)
                        for qk in range(2):
                            ctx.wait(t, "zp", b["zp"] + 2 * i + qk + 1)
                            for h in range(NH):
                                if ntr >= 2:
                                    ctx.wait(t, "mke", b["mke"] + ntr - 1)
                                ctx.inc(t.transpose(ptr[ntr % 2][:], srotq[:, h * HD:(h + 1) * HD], ctx.ident[:]), "e1", 1)
                                ntr += 1
                    for c in range(4):
                        ctx.wait(t, tok_g[c][0], tok_g[c][1])
                        if c >= 1:
                            ctx.wait(t, "yf", b["yf"] + c)
                        for k in range(DK):
                            mm = t.matmul(pg1[:], sWg1[:, k, :], rtb[0][:, k, :], start=(k == 0), stop=(k == DK - 1))
                        ctx.inc(mm, "yr", 1)

                @block.scalar
                def _(s):
                    for i in range(NTS):
                        for qk in range(2):
                            n = 3 * i + qk
                            nz = 2 * i + qk
                            ctx.wait(s, "gu", b["gu"] + n + 1)
                            if nz >= 1:
                                ctx.wait(s, "zp", b["zp"] + nz)
                            ctx.inc(s.activation(stmpf[:], pqkv[n % 2][:], AF.Silu), "sl", 1)

                @block.vector
                def _(v):
                    ntr = 0
                    for i in range(NTS):
                        c1b = srotc[i % 2][:].unsqueeze(1).broadcast_to((P, NH, HD))
                        s1b = srots[i % 2][:].unsqueeze(1).broadcast_to((P, NH, HD))
                        for qk in range(2):
                            nz = 2 * i + qk
                            if qk == 0:
                                ctx.wait(v, tok_z[(i, "rot")][0], tok_z[(i, "rot")][1])
                            ctx.wait(v, "sl", b["sl"] + nz + 1)
                            if nz >= 1:
                                ctx.wait(v, "zp", b["zp"] + nz)
                            swv = sswap[:].rearrange("p (h two f) -> p h two f", two=2, f=HD // 2)
                            inv = stmpf[:].rearrange("p (h two f) -> p h two f", two=2, f=HD // 2)
                            v.tensor_copy(swv[:, :, 0, :], inv[:, :, 1, :])
                            ctx.inc(v.tensor_copy(swv[:, :, 1, :], inv[:, :, 0, :]), "vS", 1)
                            ctx.wait(v, "vS", ctx.n["vS"])
                            v.tensor_tensor(stmpf[:].rearrange("p (h f) -> p h f", f=HD),
                                            stmpf[:].rearrange("p (h f) -> p h f", f=HD), c1b, ALU.mult)
                            ctx.inc(v.tensor_tensor(sswap[:].rearrange("p (h f) -> p h f", f=HD),
                                                    sswap[:].rearrange("p (h f) -> p h f", f=HD), s1b, ALU.mult), "vS", 1)
                            ctx.wait(v, "vS", ctx.n["vS"])
                            if nz >= 1:
                                ctx.wait(v, "e1", b["e1"] + 16 * nz)
                            ctx.inc(v.tensor_tensor(srotq[:], stmpf[:], sswap[:], ALU.add), "zp", 1)
                            dstT = sqT if qk == 0 else skT
                            for h in range(NH):
                                ctx.wait(v, "e1", b["e1"] + ntr + 1)
                                ctx.inc(v.tensor_copy(dstT[64 * (h // 8):64 * (h // 8) + 64, h % 8, i * P:(i + 1) * P], ptr[ntr % 2][:]), "mke", 1)
                                ntr += 1
                        nv = 3 * i + 2
                        ctx.wait(v, "gu", b["gu"] + nv + 1)
                        ctx.inc(v.tensor_copy(sv[:, i, :], pqkv[nv % 2][:]), "hs", 1)
                    for c in range(4):
                        ctx.wait(v, "yr", b["yr"] + c + 1)
                        ctx.inc(v.tensor_copy(sg1T[:, s_ * 4 + c, :], pg1[:]), "yf", 1)

            barrier(ctx, block)

            with ExitStack() as ls:
                en = ls.enter_context
                sE = [en(ctx.nc.sbuf_tensor(f"s{s_}a5e{i}", [P, 512], BF16)) for i in range(2)]
                st64w = [en(ctx.nc.sbuf_tensor(f"s{s_}a5t64{i}", [64, 512], BF16)) for i in range(2)]
                sattn4 = en(ctx.nc.sbuf_tensor(f"s{s_}a5at", [P, 4, D], BF16))
                sgt = en(ctx.nc.sbuf_tensor(f"s{s_}a5gt", [P, D], BF16))
                sato = en(ctx.nc.sbuf_tensor(f"s{s_}a5ato", [P, D], F32))
                snorm = en(ctx.nc.sbuf_tensor(f"s{s_}a5nrm", [P, D], BF16))
                snormT = en(ctx.nc.sbuf_tensor(f"s{s_}a5nrT", [P, DK, P], BF16))
                st12 = en(ctx.nc.sbuf_tensor(f"s{s_}a5st", [P, 12], F32))
                smv = en(ctx.nc.sbuf_tensor(f"s{s_}a5mv", [P, 2], F32))
                ssd = en(ctx.nc.sbuf_tensor(f"s{s_}a5sd", [P, 1], F32))
                srv = en(ctx.nc.sbuf_tensor(f"s{s_}a5rv", [P, 1], F32))
                sxf = [en(ctx.nc.sbuf_tensor(f"s{s_}a5x{i}", [P, D], F32)) for i in range(2)]
                pE = [en(ctx.nc.psum_tensor(f"s{s_}a5pE{i}", [P, 512], F32)) for i in range(2)]
                pav = en(ctx.nc.psum_tensor(f"s{s_}a5pav", [64, 512], F32))
                ptn = en(ctx.nc.psum_tensor(f"s{s_}a5ptn", [P, P], BF16))
                pgt = en(ctx.nc.psum_tensor(f"s{s_}a5pgt", [P, D], F32))
                pxo = en(ctx.nc.psum_tensor(f"s{s_}a5pxo", [P, D], F32))
                b3 = {k: ctx.n[k] for k in (SEMS + MOE_SEMS)}
                tok_xg = {}
                tok_st = {}
                NTC4 = NTS // 4

                def cum_pairs(tc):
                    return 32 * tc * (tc + 1)


                @block.gpsimd
                def _(g):
                    for i in range(NTS):
                        gt = s_ * NTS + i
                        j = i % 2
                        if i >= 2:
                            ctx.wait(g, tok_st[i - 2][0], tok_st[i - 2][1])
                        sem = f"g{2 + j}"
                        tok_xg[i] = (sem, ctx.inc(g.dma_start(out=sxf[j][:], in_=src_ap[gt * P:(gt + 1) * P, :]), sem, 16))
                        ctx.wait(g, "yf", b3["yf"] + i + 1)
                        tok_st[i] = (f"g{j}", ctx.inc(g.dma_start(out=dst_ap[gt * P:(gt + 1) * P, :], in_=sxf[j][:]), f"g{j}", 16))

                if s_ == 0:
                    ctx.last_xtok = {}
                for i_ in range(NTS):
                    ctx.last_xtok[s_ * NTS + i_] = tok_st[i_]

                @block.tensor
                def _(t):
                    ne = 0
                    for tc in range(NTC4):
                        jmax = 4 * (tc + 1)
                        for h in range(NH):
                            pb = 64 * (h // 8)
                            hr = h % 8
                            nh = tc * NH + h
                            for j in range(jmax):
                                par = ne % 2
                                if ne >= 2:
                                    ctx.wait(t, "hs", b3["hs"] + ne - 1)
                                ctx.inc(t.matmul(pE[par][:], skT[pb:pb + 64, hr, j * P:(j + 1) * P],
                                                 sqT[pb:pb + 64, hr, tc * 512:(tc + 1) * 512], start=True, stop=True), "gu", 1)
                                if j >= 1:
                                    ctx.wait(t, "hs", b3["hs"] + ne)
                                    ctx.inc(t.matmul(pav[:], sv[:, j - 1, h * HD:(h + 1) * HD], sE[(ne - 1) % 2][:],
                                                     start=(j - 1 == 0), stop=False), "yr", 1)
                                ne += 1
                            ctx.wait(t, "hs", b3["hs"] + ne)
                            ctx.inc(t.matmul(pav[:], sv[:, jmax - 1, h * HD:(h + 1) * HD], sE[(ne - 1) % 2][:],
                                             start=(jmax - 1 == 0), stop=True), "yr", 1)
                            ctx.wait(t, "mke", b3["mke"] + 48 * tc + h + 1)
                            for q in range(4):
                                if h == 0 and q == 0:
                                    ctx.wait(t, "mke", b3["mke"] + 48 * tc)
                                if nh > 0 or q > 0:
                                    ctx.wait(t, "zp", b3["zp"] + 68 * tc + 4 * h + q)
                                ctx.inc(t.transpose(ptn[:, 0:64], st64w[nh % 2][:, q * P:(q + 1) * P], ctx.ident[0:64, 0:64]), "e2", 1)
                        for q in range(4):
                            i = tc * 4 + q
                            if i >= 1:
                                ctx.wait(t, "sl", b3["sl"] + i)
                            t.matmul(pgt[:, 0:512], sg1T[:, s_ * 4 + tc, q * P:(q + 1) * P], sWg2[:, 0:512], start=True, stop=True)
                            ctx.inc(t.matmul(pgt[:, 512:1024], sg1T[:, s_ * 4 + tc, q * P:(q + 1) * P], sWg2[:, 512:1024], start=True, stop=True), "e3", 1)
                            ctx.wait(t, "zp", b3["zp"] + 68 * tc + 64 + q + 1)
                            for k in range(DK):
                                if k == 0:
                                    ctx.wait(t, "zp", b3["zp"] + 68 * tc + 64)
                                    ctx.wait(t, "mke", b3["mke"] + 48 * tc + 16 + 8 * q)
                                else:
                                    ctx.wait(t, "mke", b3["mke"] + 48 * tc + 16 + 8 * q + k)
                                ctx.inc(t.transpose(ptn[:], snorm[:, k * P:(k + 1) * P], ctx.ident[:]), "e2", 1)
                            ctx.wait(t, "mke", b3["mke"] + 48 * tc + 16 + 8 * (q + 1))
                            if i >= 1:
                                ctx.wait(t, "yf", b3["yf"] + i)
                            for half in range(2):
                                for k in range(DK):
                                    t.matmul(pxo[:, half * 512:(half + 1) * 512], snormT[:, k, :], sWoP[:, k, half * 512:(half + 1) * 512], start=(k == 0), stop=False)
                                mm = t.matmul(pxo[:, half * 512:(half + 1) * 512], ctx.cones[0:1, 0:P], sWoB[:, half * 512:(half + 1) * 512], start=False, stop=True)
                            ctx.inc(mm, "e1", 1)

                @block.scalar
                def _(s):
                    for i in range(NTS):
                        ctx.wait(s, "e3", b3["e3"] + i + 1)
                        if i >= 1:
                            ctx.wait(s, "vS", b3["vS"] + 4 * (i - 1) + 1)
                        ctx.inc(s.activation(sgt[:], pgt[:], AF.Sigmoid), "sl", 1)
                        ctx.wait(s, "vS", b3["vS"] + 4 * i + 3)
                        ctx.inc(s.activation(ssd[:], smv[:, 1:2], AF.Sqrt, bias=ctx.ceps[:, 0:1]), "sS", 1)

                @block.vector
                def _(v):
                    ne = 0
                    for tc in range(NTC4):
                        jmax = 4 * (tc + 1)
                        for h in range(NH):
                            nh = tc * NH + h
                            if h == 0 and tc >= 1:
                                ctx.wait(v, "vS", b3["vS"] + 16 * tc)
                            for j in range(jmax):
                                par = ne % 2
                                ctx.wait(v, "gu", b3["gu"] + ne + 1)
                                if ne >= 2:
                                    ctx.wait(v, "yr", b3["yr"] + ne - 1)
                                if j >= 4 * tc:
                                    ctx.inc(v.tensor_tensor(sE[par][:], pE[par][:], ctx.cmask[:, j - 4 * tc, :], ALU.mult), "hs", 1)
                                else:
                                    ctx.inc(v.tensor_copy(sE[par][:], pE[par][:]), "hs", 1)
                                ne += 1
                            ctx.wait(v, "yr", b3["yr"] + ne)
                            if nh >= 2:
                                ctx.wait(v, "e2", b3["e2"] + _l5_e2(nh - 2) + 4)
                            ctx.inc(v.tensor_copy(st64w[nh % 2][:], pav[:]), "mke", 1)
                            for q in range(4):
                                ctx.wait(v, "e2", b3["e2"] + _l5_e2(nh) + q + 1)
                                ctx.inc(v.tensor_copy(sattn4[:, q, h * HD:(h + 1) * HD], ptn[:, 0:64]), "zp", 1)
                        for q in range(4):
                            i = tc * 4 + q
                            vb = b3["vS"] + 4 * i
                            ctx.wait(v, "sl", b3["sl"] + i + 1)
                            ctx.wait(v, "zp", b3["zp"] + 68 * tc + 64)
                            ctx.inc(v.tensor_tensor(sato[:], sattn4[:, q, :], sgt[:], ALU.mult), "vS", 1)
                            ctx.wait(v, "vS", vb + 1)
                            v.bn_stats(st12[:, 0:6], sato[:, 0:512])
                            ctx.inc(v.bn_stats(st12[:, 6:12], sato[:, 512:1024]), "vS", 1)
                            ctx.wait(v, "vS", vb + 2)
                            ctx.inc(v.bn_aggr(smv[:], st12[:]), "vS", 1)
                            ctx.wait(v, "sS", b3["sS"] + i + 1)
                            ctx.inc(v.reciprocal(srv[:], ssd[:]), "vS", 1)
                            ctx.wait(v, "vS", vb + 4)
                            ctx.inc(v.tensor_scalar(snorm[:], sato[:], smv[:, 0:1], srv[:], ALU.subtract, ALU.mult), "zp", 1)
                            for k in range(DK):
                                ctx.wait(v, "e2", b3["e2"] + 96 * tc + 64 + 8 * q + k + 1)
                                ctx.inc(v.tensor_copy(snormT[:, k, :], ptn[:]), "mke", 1)
                            ctx.wait(v, "e1", b3["e1"] + i + 1)
                            ctx.wait(v, tok_xg[i][0], tok_xg[i][1])
                            ctx.inc(v.tensor_tensor(sxf[i % 2][:], sxf[i % 2][:], pxo[:], ALU.add), "yf", 1)

            barrier(ctx, block)


def emit_perf(ctx, block, scr, src_ap, dst_ap):
    NTS = SLEN // P
    NCB = 6

    def nphi(nc_):
        return 4 * (nc_ // NCB) + max(0, nc_ % NCB - 2)

    with ExitStack() as outer:
        eno = outer.enter_context
        swb = eno(ctx.nc.sbuf_tensor("p2wb", [P, DK, 3 * D], BF16))
        sbb = eno(ctx.nc.sbuf_tensor("p2bb", [1, 3 * D], BF16))
        spj = eno(ctx.nc.sbuf_tensor("p2pj", [P, DK, D], BF16))
        spjB = eno(ctx.nc.sbuf_tensor("p2pjB", [1, D], BF16))
        sqp = eno(ctx.nc.sbuf_tensor("p2qp", [P, NTS, D], BF16))
        skp = eno(ctx.nc.sbuf_tensor("p2kp", [P, NTS, D], BF16))
        svp = eno(ctx.nc.sbuf_tensor("p2vp", [P, NTS, D], BF16))
        skv = eno(ctx.nc.sbuf_tensor("p2kv", [64, NH * (HD + 1)], BF16))
        tb0 = {k: ctx.n[k] for k in SEMS}

        @block.sync
        def _(s):
            s.wait_ge(ctx.sems["dS"], tb0["dS"])
            ctx.inc(s.dma_start(out=swb[:], in_=r_pkm(scr["Wbig"][0:D, :])), "dS", 16)
            ctx.inc(s.dma_start(out=sbb[:], in_=scr["Wbig"][1024:1025, :]), "dS", 16)
            ctx.inc(s.dma_start(out=spj[:], in_=r_pkm(scr["projT_aug"][0:D, :])), "dS", 16)
            ctx.inc(s.dma_start(out=spjB[:], in_=scr["projT_aug"][1024:1025, :]), "dS", 16)
            s.wait_ge(ctx.sems["dS"], ctx.n["dS"])
        wtok0 = ctx.n["dS"]

        for s_ in range(NSENT):
            with ExitStack() as ls:
                en = ls.enter_context
                szt = [en(ctx.nc.sbuf_tensor(f"x{s_}p2z{i}", [P, DK, P], BF16)) for i in range(2)]
                sneg = [en(ctx.nc.sbuf_tensor(f"x{s_}p2n{i}", [P, 8], F32)) for i in range(2)]
                pbig = [en(ctx.nc.psum_tensor(f"x{s_}p2b{i}", [P, 512], F32)) for i in range(2)]
                b = {k: ctx.n[k] for k in (SEMS + MOE_SEMS)}
                tok_z = {}

                @block.sync
                def _(s):
                    for i in range(NTS):
                        if i >= 2:
                            ctx.wait(s, "gu", b["gu"] + NCB * (i - 1))
                        sem = f"d{i % 2}"
                        tok_z[i] = (sem, ctx.inc(s.dma_start(out=szt[i % 2][:], in_=r_pkm(scr["zT"][:, (s_ * SLEN + i * P):(s_ * SLEN + (i + 1) * P)])), sem, 16))

                @block.tensor
                def _(t):
                    ctx.wait(t, "dS", wtok0)
                    nc_ = 0
                    for i in range(NTS):
                        ctx.wait(t, tok_z[i][0], tok_z[i][1])
                        for cb in range(NCB):
                            par = nc_ % 2
                            if nc_ >= 2:
                                ctx.wait(t, "hs", b["hs"] + nc_ - 1)
                            for k in range(DK):
                                t.matmul(pbig[par][:], szt[i % 2][:, k, :], swb[:, k, cb * 512:(cb + 1) * 512], start=(k == 0), stop=False)
                            mm = t.matmul(pbig[par][:], ctx.cones[0:1, 0:P], sbb[0:1, cb * 512:(cb + 1) * 512], start=False, stop=True)
                            ctx.inc(mm, "gu", 1)
                            nc_ += 1

                @block.scalar
                def _(s):
                    nc_ = 0
                    for i in range(NTS):
                        for cb in range(NCB):
                            par = nc_ % 2
                            if cb >= 2:
                                dst = sqp if cb < 4 else skp
                                col = (cb - 2) % 2
                                ctx.wait(s, "gu", b["gu"] + nc_ + 1)
                                ctx.wait(s, "mk", b["mk"] + 2 * (nphi(nc_) + 1))
                                for hseg in range(8):
                                    ctx.inc(s.activation(dst[:, i, col * 512 + hseg * HD: col * 512 + (hseg + 1) * HD],
                                                         pbig[par][:, hseg * HD:(hseg + 1) * HD], AF.Exp,
                                                         bias=sneg[nphi(nc_) % 2][:, hseg:hseg + 1]), "sl", 1)
                            nc_ += 1

                @block.vector
                def _(v):
                    nc_ = 0
                    for i in range(NTS):
                        for cb in range(NCB):
                            par = nc_ % 2
                            if cb < 2:
                                ctx.wait(v, "gu", b["gu"] + nc_ + 1)
                                ctx.inc(v.tensor_copy(svp[:, i, cb * 512:(cb + 1) * 512], pbig[par][:]), "hs", 1)
                            else:
                                dst = sqp if cb < 4 else skp
                                col = (cb - 2) % 2
                                np_ = nphi(nc_)
                                ctx.wait(v, "gu", b["gu"] + nc_ + 1)
                                if np_ >= 2:
                                    ctx.wait(v, "sl", b["sl"] + 8 * (np_ - 1))
                                ctx.inc(v.reduce_max(sneg[np_ % 2][:].unsqueeze(2),
                                                     pbig[par][:].rearrange("p (h f) -> p h f", f=HD),
                                                     axis=AX.X, negate=True), "mk", 1)
                                ctx.wait(v, "mk", ctx.n["mk"])
                                ctx.inc(v.tensor_scalar(sneg[np_ % 2][:], sneg[np_ % 2][:], -LN8, None, ALU.add), "mk", 1)
                                ctx.wait(v, "sl", b["sl"] + 8 * (np_ + 1))
                                ctx.inc(v.tensor_scalar(dst[:, i, col * 512:(col + 1) * 512],
                                                        dst[:, i, col * 512:(col + 1) * 512], 1e-6, None, ALU.add), "hs", 1)
                            nc_ += 1

            barrier(ctx, block)

            with ExitStack() as ls:
                en = ls.enter_context
                pkv = en(ctx.nc.psum_tensor(f"x{s_}p2kv", [64, NH * P], F32))
                b = {k: ctx.n[k] for k in (SEMS + MOE_SEMS)}

                @block.tensor
                def _(t):
                    for i in range(NTS):
                        for h in range(NH):
                            st = (i == 0 and h % 4 == 0)
                            sp = (i == NTS - 1 and h % 4 == 3)
                            t.matmul(pkv[:, h * P:h * P + HD], skp[:, i, h * HD:(h + 1) * HD],
                                     svp[:, i, h * HD:(h + 1) * HD], start=st, stop=False)
                            mm = t.matmul(pkv[:, h * P + HD:h * P + HD + 1], skp[:, i, h * HD:(h + 1) * HD],
                                          ctx.conec[0:P, 0:1], start=False, stop=sp)
                    ctx.inc(mm, "e1", 1)

                @block.vector
                def _(v):
                    ctx.wait(v, "e1", b["e1"] + 1)
                    for h in range(NH):
                        ctx.inc(v.tensor_copy(skv[:, h * (HD + 1):(h + 1) * (HD + 1)], pkv[:, h * P:h * P + HD + 1]), "e2", 1)

            barrier(ctx, block)

            with ExitStack() as ls:
                en = ls.enter_context
                sqpt = [en(ctx.nc.sbuf_tensor(f"x{s_}p2qt{i}", [64, P], BF16)) for i in range(2)]
                sattn = en(ctx.nc.sbuf_tensor(f"x{s_}p2at", [P, D], BF16))
                satT = en(ctx.nc.sbuf_tensor(f"x{s_}p2aT", [P, DK, P], BF16))
                sdiv = en(ctx.nc.sbuf_tensor(f"x{s_}p2dv", [P, 1], F32))
                sxf = [en(ctx.nc.sbuf_tensor(f"x{s_}p2x{i}", [P, D], F32)) for i in range(2)]
                ptr = [en(ctx.nc.psum_tensor(f"x{s_}p2tr{i}", [64, P], BF16)) for i in range(2)]
                paug = [en(ctx.nc.psum_tensor(f"x{s_}p2au{i}", [P, HD + 1], F32)) for i in range(2)]
                ptn = en(ctx.nc.psum_tensor(f"x{s_}p2tn", [P, P], BF16))
                pxo = en(ctx.nc.psum_tensor(f"x{s_}p2xo", [P, D], F32))
                b = {k: ctx.n[k] for k in (SEMS + MOE_SEMS)}
                tok_xg = {}
                tok_st = {}

                @block.gpsimd
                def _(g):
                    for i in range(NTS):
                        gt = s_ * NTS + i
                        j = i % 2
                        if i >= 2:
                            ctx.wait(g, tok_st[i - 2][0], tok_st[i - 2][1])
                        sem = f"g{2 + j}"
                        tok_xg[i] = (sem, ctx.inc(g.dma_start(out=sxf[j][:], in_=src_ap[gt * P:(gt + 1) * P, :]), sem, 16))
                        ctx.wait(g, "yf", b["yf"] + i + 1)
                        tok_st[i] = (f"g{j}", ctx.inc(g.dma_start(out=dst_ap[gt * P:(gt + 1) * P, :], in_=sxf[j][:]), f"g{j}", 16))

                if s_ == 0:
                    ctx.last_xtok = {}
                for i_ in range(NTS):
                    ctx.last_xtok[s_ * NTS + i_] = tok_st[i_]

                @block.tensor
                def _(t):
                    nh = 0
                    for i in range(NTS):
                        for h in range(NH):
                            if nh >= 2:
                                ctx.wait(t, "mke", b["mke"] + nh - 1)
                            ctx.inc(t.transpose(ptr[nh % 2][:], sqp[:, i, h * HD:(h + 1) * HD], ctx.ident[:]), "e3", 1)
                            ctx.wait(t, "mke", b["mke"] + nh + 1)
                            if nh >= 2:
                                ctx.wait(t, "yr", b["yr"] + nh - 1)
                            ctx.inc(t.matmul(paug[nh % 2][:], sqpt[nh % 2][:], skv[:, h * (HD + 1):(h + 1) * (HD + 1)],
                                             start=True, stop=True), "gu", 1)
                            nh += 1
                        ctx.wait(t, "yr", b["yr"] + 16 * (i + 1))
                        for k in range(DK):
                            if i > 0 or k > 0:
                                ctx.wait(t, "zp", b["zp"] + i * DK + k)
                            ctx.inc(t.transpose(ptn[:], sattn[:, k * P:(k + 1) * P], ctx.ident[:]), "e2", 1)
                        ctx.wait(t, "zp", b["zp"] + (i + 1) * DK)
                        if i >= 1:
                            ctx.wait(t, "yf", b["yf"] + i)
                        for half in range(2):
                            for k in range(DK):
                                t.matmul(pxo[:, half * 512:(half + 1) * 512], satT[:, k, :], spj[:, k, half * 512:(half + 1) * 512], start=(k == 0), stop=False)
                            mm = t.matmul(pxo[:, half * 512:(half + 1) * 512], ctx.cones[0:1, 0:P], spjB[0:1, half * 512:(half + 1) * 512], start=False, stop=True)
                        ctx.inc(mm, "mk", 1)

                @block.vector
                def _(v):
                    nh = 0
                    for i in range(NTS):
                        for h in range(NH):
                            if h == 0 and i >= 1:
                                ctx.wait(v, "e2", b["e2"] + i * DK)
                            ctx.wait(v, "e3", b["e3"] + nh + 1)
                            ctx.inc(v.tensor_copy(sqpt[nh % 2][:], ptr[nh % 2][:]), "mke", 1)
                            ctx.wait(v, "gu", b["gu"] + nh + 1)
                            ctx.inc(v.tensor_scalar(sdiv[:], paug[nh % 2][:, HD:HD + 1], 1e-6, None, ALU.add), "vS", 1)
                            ctx.wait(v, "vS", ctx.n["vS"])
                            ctx.inc(v.reciprocal(sdiv[:], sdiv[:]), "vS", 1)
                            ctx.wait(v, "vS", ctx.n["vS"])
                            ctx.inc(v.tensor_scalar(sattn[:, h * HD:(h + 1) * HD], paug[nh % 2][:, 0:HD], sdiv[:], None, ALU.mult), "yr", 1)
                            nh += 1
                        for k in range(DK):
                            ctx.wait(v, "e2", b["e2"] + i * DK + k + 1)
                            ctx.inc(v.tensor_copy(satT[:, k, :], ptn[:]), "zp", 1)
                        ctx.wait(v, "mk", b["mk"] + i + 1)
                        ctx.wait(v, tok_xg[i][0], tok_xg[i][1])
                        ctx.inc(v.tensor_tensor(sxf[i % 2][:], sxf[i % 2][:], pxo[:], ALU.add), "yf", 1)

            barrier(ctx, block)


CAP = 768
NTS6 = CAP // P
U32 = mybir.dt.uint32


def emit_moe_indices(ctx, block, scr, sidx):
    import bass_rust
    with ExitStack() as ls:
        en = ls.enter_context
        mb = en(ctx.nc.sbuf_tensor("ixmb", [8, T], BF16))
        m32 = en(ctx.nc.sbuf_tensor("ixm", [32, T], F32))
        c32 = en(ctx.nc.sbuf_tensor("ixc", [32, T], F32))
        t32 = en(ctx.nc.sbuf_tensor("ixt", [32, T], F32))
        smT = en(ctx.nc.sbuf_tensor("ixsmt", [P, NT, 8], BF16))
        s8 = en(ctx.nc.sbuf_tensor("ixs8", [P, NT, 8], F32))
        slot = en(ctx.nc.sbuf_tensor("ixslt", [P, NT], F32))
        ctmp = en(ctx.nc.sbuf_tensor("ixct", [1, 256], F32))
        c2 = en(ctx.nc.sbuf_tensor("ixc2", [1, 256], F32))
        c2b = en(ctx.nc.sbuf_tensor("ixc2b", [1, 256], F32))
        scap = en(ctx.nc.sbuf_tensor("ixcap", [1, 256], F32))
        szt = [en(ctx.nc.sbuf_tensor(f"ixz{i}", [P, D], BF16)) for i in range(2)]
        pmT = [en(ctx.nc.psum_tensor(f"ixpt{i}", [P, 8], BF16)) for i in range(2)]
        pall = en(ctx.nc.psum_tensor("ixpa", [P, NT * 8], F32))
        prow = en(ctx.nc.psum_tensor("ixpr", [1, 256], F32))
        pbc = en(ctx.nc.psum_tensor("ixpb", [P, NT * 8], F32))
        b = {k: ctx.n[k] for k in (SEMS + MOE_SEMS)}
        first = not getattr(ctx, "zc_zeroed", False)
        ctx.zc_zeroed = True

        @block.sync
        def _(s):
            s.wait_ge(ctx.sems["dS"], b["dS"])
            ctx.inc(s.dma_start(out=mb[:], in_=scr["msk"][:]), "dS", 16)
            ctx.inc(s.dma_start(out=scap[:], in_=ctx.c_ecap[:]), "dS", 16)
            s.wait_ge(ctx.sems["dS"], ctx.n["dS"])
        ldtok = ctx.n["dS"]

        zctok = None
        if first:
            szero = en(ctx.nc.sbuf_tensor("ixzr", [P, D], BF16))

            @block.vector
            def _(v):
                ctx.inc(v.memset(szero[:], 0.0), "vS", 1)
            zrtok = ctx.n["vS"]

            @block.sync
            def _(s):
                ctx.wait(s, "vS", zrtok)
                for r in range(E * CAP // P):
                    ctx.inc(s.dma_start(out=scr["zc"][r * P:(r + 1) * P, :], in_=szero[:]), "dS", 16)
                s.wait_ge(ctx.sems["dS"], ctx.n["dS"])
            zctok = ctx.n["dS"]

        def vstep(v, ins):
            ctx.inc(ins, "vS", 1)
            ctx.wait(v, "vS", ctx.n["vS"])

        @block.vector
        def _(v):
            ctx.wait(v, "dS", ldtok)
            v.memset(m32[:], 0.0)
            vstep(v, v.memset(t32[:], 0.0))
            vstep(v, v.tensor_copy(m32[0:8, :], mb[:]))
            for sh in (1, 2, 4):
                mp = [(i - sh) if (8 > i >= sh) else 31 for i in range(32)]
                srcb = m32 if sh == 1 else c32
                vstep(v, v.stream_shuffle(t32[:], srcb[:], mp))
                vstep(v, v.tensor_tensor(c32[:], srcb[:], t32[:], ALU.add))
            vstep(v, v.tensor_scalar(t32[:], c32[:], 1.0, None, ALU.is_equal))
            vstep(v, v.tensor_tensor(m32[:], m32[:], t32[:], ALU.mult))
            ctx.inc(v.tensor_copy(mb[:], m32[0:8, :]), "mk", 1)
        vtf = ctx.n["mk"]

        @block.tensor
        def _(t):
            ctx.wait(t, "mk", vtf)
            for i in range(NT):
                if i >= 2:
                    ctx.wait(t, "yr", b["yr"] + i - 1)
                ctx.inc(t.transpose(pmT[i % 2][:], mb[:, i * P:(i + 1) * P], ctx.ident[0:8, 0:8]), "gu", 1)
            ctx.wait(t, "yr", b["yr"] + NT)
            ctx.inc(t.matmul(pall[:], ctx.ctri[:], smT[:].rearrange("p t e -> p (t e)"), start=True, stop=True), "sl", 1)
            ctx.inc(t.matmul(prow[:], ctx.conec[:], smT[:].rearrange("p t e -> p (t e)"), start=True, stop=True), "sl", 1)
            ctx.wait(t, "hs", b["hs"] + 1)
            ctx.inc(t.matmul(pbc[:], ctx.conesf[0:1, 0:P], c2b[:], start=True, stop=True), "sl", 1)

        @block.vector
        def _(v):
            for i in range(NT):
                ctx.wait(v, "gu", b["gu"] + i + 1)
                ctx.inc(v.tensor_copy(smT[:, i, :], pmT[i % 2][:]), "yr", 1)
            ctx.wait(v, "sl", b["sl"] + 2)
            vstep(v, v.tensor_copy(ctmp[:], prow[:]))
            vstep(v, v.memset(c2[:, 0:8], 0.0))
            vstep(v, v.tensor_copy(c2[:, 8:256], ctmp[:, 0:248]))
            a, bb = c2, c2b
            for s_ in (8, 16, 32, 64, 128):
                v.tensor_copy(bb[:, 0:s_], a[:, 0:s_])
                vstep(v, v.tensor_tensor(bb[:, s_:256], a[:, s_:256], a[:, 0:256 - s_], ALU.add))
                a, bb = bb, a
            assert a is c2b
            ctx.inc(v.tensor_tensor(c2b[:], a[:], scap[:], ALU.add), "hs", 1)
            ctx.wait(v, "sl", b["sl"] + 3)
            vstep(v, v.tensor_copy(s8[:].rearrange("p t e -> p (t e)"), pbc[:]))
            vstep(v, v.tensor_tensor(s8[:].rearrange("p t e -> p (t e)"), s8[:].rearrange("p t e -> p (t e)"), pall[:], ALU.add))
            vstep(v, v.tensor_tensor(s8[:].rearrange("p t e -> p (t e)"), s8[:].rearrange("p t e -> p (t e)"), smT[:].rearrange("p t e -> p (t e)"), ALU.mult))
            vstep(v, v.tensor_reduce(slot[:].unsqueeze(2), s8[:], axis=AX.X, op=ALU.add))
            vstep(v, v.tensor_scalar(slot[:], slot[:], float(E * CAP - 1), None, ALU.min))
            ctx.inc(v.tensor_copy(sidx[:], slot[:]), "zp", 1)
        slottok = ctx.n["zp"]

        tok_zl = {}

        @block.sync
        def _(s):
            for i in range(NT):
                sem = f"d{4 + i % 2}"
                if i >= 2:
                    ctx.wait(s, f"g{i % 2}", b[f"g{i % 2}"] + 16 * (i // 2))
                tok_zl[i] = (sem, ctx.inc(s.dma_start(out=szt[i % 2][:], in_=scr["z"][i * P:(i + 1) * P, :]), sem, 16))

        @block.gpsimd
        def _(g):
            ctx.wait(g, "zp", slottok)
            if zctok is not None:
                ctx.wait(g, "dS", zctok)
            for i in range(NT):
                ctx.wait(g, tok_zl[i][0], tok_zl[i][1])
                ctx.inc(g.indirect_dma_start(out=scr["zc"][:], out_offset=bass_rust.IndirectOffsetOnAxis(axis=0, ap=sidx[:, i:i + 1]),
                                             in_=szt[i % 2][:], in_offset=None), f"g{i % 2}", 16)
            g.wait_ge(ctx.sems["g0"], ctx.n["g0"])
            g.wait_ge(ctx.sems["g1"], ctx.n["g1"])


def emit_moe_sparse(ctx, block, scr, src_ap, dst_ap, half, sidx):
    import bass_rust
    b = {k: ctx.n[k] for k in (SEMS + MOE_SEMS)}
    with ExitStack() as ls:
        en = ls.enter_context
        swg = [en(ctx.nc.sbuf_tensor(f"swg{i}", [P, DK, IHW], BF16)) for i in range(2)]
        swu = [en(ctx.nc.sbuf_tensor(f"swu{i}", [P, DK, IHW], BF16)) for i in range(2)]
        swd = [en(ctx.nc.sbuf_tensor(f"swd{i}", [P, ITH, D], BF16)) for i in range(2)]
        szc = [en(ctx.nc.sbuf_tensor(f"szc{i}", [P, DK, CAP], BF16)) for i in range(2)]
        sh = en(ctx.nc.sbuf_tensor("ssh", [P, ITH, CAP], BF16))
        sil = [en(ctx.nc.sbuf_tensor(f"ssil{i}", [P, 512], BF16)) for i in range(2)]
        syb = en(ctx.nc.sbuf_tensor("ssyb", [P, NTS6, D], BF16))
        sxf = [en(ctx.nc.sbuf_tensor(f"ssx{i}", [P, D], F32)) for i in range(2)]
        syg = [en(ctx.nc.sbuf_tensor(f"ssyg{i}", [P, D], BF16)) for i in range(2)]
        pg = [en(ctx.nc.psum_tensor(f"spg{i}", [P, 512], F32)) for i in range(2)]
        pu = [en(ctx.nc.psum_tensor(f"spu{i}", [P, 512], F32)) for i in range(2)]
        py = [en(ctx.nc.psum_tensor(f"spy{i}", [P, D], F32)) for i in range(2)]
        TCS = [(0, 512), (512, 256)]
        tok_yw = {e: ("g3", b["g3"] + 16 * (e + 1)) for e in range(E)}

        tok_zc = {}
        tok_w = {}

        @block.sync
        def _(s):
            s.wait_ge(ctx.sems["g0"], b["g0"])
            s.wait_ge(ctx.sems["g1"], b["g1"])
            for e in range(E):
                sem = f"d{2 + e % 2}"
                if e >= 2:
                    ctx.wait(s, "gu", b["gu"] + 44 * (e - 1))
                for k in range(DK):
                    ctx.inc(s.dma_start(out=szc[e % 2][:, k, :], in_=scr["zc"][e * CAP:(e + 1) * CAP, k * P:(k + 1) * P], transpose=True), sem, 16)
                tok_zc[e] = (sem, ctx.n[sem])
                for ih in range(2):
                    w = 2 * e + ih
                    wsem = f"d{w % 2}"
                    if w >= 2:
                        ctx.wait(s, "yr", b["yr"] + 6 * (w - 1))
                    ihs = ih * IHW
                    ctx.inc(s.dma_start(out=swg[w % 2][:], in_=scr["gwT"][e][ih]), wsem, 16)
                    ctx.inc(s.dma_start(out=swu[w % 2][:], in_=scr["uwT"][e][ih]), wsem, 16)
                    ctx.inc(s.dma_start(out=swd[w % 2][:], in_=scr["dwT"][e][ih]), wsem, 16)
                    tok_w[w] = (wsem, ctx.n[wsem])

        @block.tensor
        def _(t):
            n = 0
            m = 0
            for e in range(E):
                ctx.wait(t, tok_zc[e][0], tok_zc[e][1])
                for ih in range(2):
                    w = 2 * e + ih
                    ctx.wait(t, tok_w[w][0], tok_w[w][1])
                    for (tco, tcn) in TCS:
                        for it in range(ITH):
                            par = n % 2
                            if n >= 2:
                                ctx.wait(t, "hs", b["hs"] + n - 1)
                            for k in range(DK):
                                t.matmul(pg[par][:, 0:tcn], swg[w % 2][:, k, it * P:(it + 1) * P], szc[e % 2][:, k, tco:tco + tcn], start=(k == 0), stop=(k == DK - 1))
                            for k in range(DK):
                                mm = t.matmul(pu[par][:, 0:tcn], swu[w % 2][:, k, it * P:(it + 1) * P], szc[e % 2][:, k, tco:tco + tcn], start=(k == 0), stop=(k == DK - 1))
                            ctx.inc(mm, "gu", 1)
                            n += 1
                    ctx.wait(t, "hs", b["hs"] + n)
                    for ts in range(NTS6):
                        pyb = py[m % 2]
                        if m >= 2:
                            ctx.wait(t, "yf", b["yf"] + m - 1)
                        for it in range(ITH):
                            t.matmul(pyb[:, 0:512], sh[:, it, ts * P:(ts + 1) * P], swd[w % 2][:, it, 0:512], start=(it == 0), stop=(it == ITH - 1))
                            mm = t.matmul(pyb[:, 512:1024], sh[:, it, ts * P:(ts + 1) * P], swd[w % 2][:, it, 512:1024], start=(it == 0), stop=(it == ITH - 1))
                        ctx.inc(mm, "yr", 1)
                        m += 1

        @block.scalar
        def _(s):
            n = 0
            for e in range(E):
                for ih in range(2):
                    for (tco, tcn) in TCS:
                        for it in range(ITH):
                            par = n % 2
                            ctx.wait(s, "gu", b["gu"] + n + 1)
                            if n >= 2:
                                ctx.wait(s, "hs", b["hs"] + n - 1)
                            ctx.inc(s.activation(sil[par][:, 0:tcn], pg[par][:, 0:tcn], AF.Silu), "sl", 1)
                            n += 1


        @block.vector
        def _(v):
            n = 0
            m = 0
            for e in range(E):
                for ih in range(2):
                    for (tco, tcn) in TCS:
                        for it in range(ITH):
                            par = n % 2
                            ctx.wait(v, "sl", b["sl"] + n + 1)
                            ctx.inc(v.tensor_tensor(sh[:, it, tco:tco + tcn], sil[par][:, 0:tcn], pu[par][:, 0:tcn], ALU.mult), "hs", 1)
                            n += 1
                    for ts in range(NTS6):
                        ctx.wait(v, "yr", b["yr"] + m + 1)
                        if ih == 0 and e >= 1:
                            ctx.wait(v, "g3", tok_yw[e - 1][1])
                        if ih == 0:
                            ctx.inc(v.tensor_copy(syb[:, ts, :], py[m % 2][:]), "yf", 1)
                        else:
                            ctx.inc(v.tensor_tensor(syb[:, ts, :], syb[:, ts, :], py[m % 2][:], ALU.add), "yf", 1)
                        m += 1
        @block.gpsimd
        def _(g):
            for e in range(E):
                ctx.wait(g, "yf", b["yf"] + 12 * (e + 1))
                tok = ctx.inc(g.dma_start(out=r_pkm(scr["yc"][e * CAP:(e + 1) * CAP, :]), in_=syb[:]), "g3", 16)
                assert tok == tok_yw[e][1]
            g.wait_ge(ctx.sems["g3"], ctx.n["g3"])
        ycdone = ctx.n["g3"]

        tok_xl = {}
        tok_gg = {}
        st_slot = lambda i: ("g2" if i % 2 == 0 else "g4")
        tok_st = {i: (st_slot(i), b[st_slot(i)] + 16 * (i // 2 + 1)) for i in range(NT)}

        @block.sync
        def _(s):
            for i in range(NT):
                j = i % 2
                sem = f"d{4 + j}"
                if i >= 2:
                    ctx.wait(s, "xo", b["xo"] + i - 1)
                    ctx.wait(s, tok_st[i - 2][0], tok_st[i - 2][1])
                tok_xl[i] = (sem, ctx.inc(s.dma_start(out=sxf[j][:], in_=src_ap[i * P:(i + 1) * P, :]), sem, 16))

        @block.gpsimd
        def _(g):
            g.wait_ge(ctx.sems["g3"], ycdone)
            for i in range(NT):
                j = i % 2
                sem = f"g{j}"
                if i >= 2:
                    ctx.wait(g, "xo", b["xo"] + i - 1)
                tok_gg[i] = (sem, ctx.inc(g.indirect_dma_start(out=syg[j][:], out_offset=None, in_=scr["yc"][:],
                                                               in_offset=bass_rust.IndirectOffsetOnAxis(axis=0, ap=sidx[:, i:i + 1])), sem, 16))
                ctx.wait(g, "xo", b["xo"] + i + 1)
                tok = ctx.inc(g.dma_start(out=dst_ap[i * P:(i + 1) * P, :], in_=sxf[j][:]), tok_st[i][0], 16)
                assert tok == tok_st[i][1]

        @block.vector
        def _(v):
            for i in range(NT):
                j = i % 2
                ctx.wait(v, tok_xl[i][0], tok_xl[i][1])
                ctx.wait(v, tok_gg[i][0], tok_gg[i][1])
                ctx.inc(v.scalar_tensor_tensor(sxf[j][:], syg[j][:], half, sxf[j][:], ALU.mult, ALU.add), "xo", 1)

        ctx.last_xtok = dict(tok_st)


# version 16 (final)
# speedup vs baseline: 1.0204x; 1.0127x over previous
import numpy as np
import ml_dtypes
from contextlib import ExitStack

import concourse.bass as bass
import concourse.mybir as mybir
from concourse.bass_utils import run_bass_kernel_spmd

F32 = mybir.dt.float32
BF16 = mybir.dt.bfloat16
AF = mybir.ActivationFunctionType
ALU = mybir.AluOpType
AX = mybir.AxisListType
bf16 = ml_dtypes.bfloat16

NCORES = 8
P = 128
D = 1024
DK = D // P
T = 4096
NT = T // P
SLEN = 2048
NSENT = 2
NQ = 8
QT = T // NQ
QTT = QT // P
E = 8
II = 2816
ITH = 11
IHW = ITH * P
NH = 16
HD = 64
LN8 = float(np.log(8.0))

SPARSE_MOE = True

MOE_SEMS = ["gu", "yr", "mk", "sl", "hs", "yf", "zp", "mke", "xo", "e1", "e2", "e3",
            "w0", "w1", "w2", "w3"]

SEMS = ["dS", "dG", "d0", "d1", "d2", "d3", "d4", "d5", "g0", "g1", "g2", "g3", "g4", "tS", "vS", "sS", "bar"]


class NcProxy:

    def __init__(self, nc):
        self._nc = nc
        self._cnt = 0

    def __getattr__(self, name):
        return getattr(self._nc, name)

    def sbuf_tensor(self, name, shape, dt):
        self._cnt += 1
        return self._nc.sbuf_tensor(f"{name}_u{self._cnt}", shape, dt)

    def psum_tensor(self, name, shape, dt):
        self._cnt += 1
        return self._nc.psum_tensor(f"{name}_u{self._cnt}", shape, dt)


class Ctx:
    def __init__(self, nc, stack):
        self.nc = nc
        self.stack = stack
        self.sems = {}
        self.n = {}
        self.inputs = {}

    def sem_init(self):
        for name in SEMS:
            self.sems[name] = self.stack.enter_context(self.nc.semaphore(name))
            self.n[name] = 0

    def inc(self, instr, name, k=1):
        instr.then_inc(self.sems[name], k)
        self.n[name] += k
        return self.n[name]

    def wait(self, eng, name, tok):
        eng.wait_ge(self.sems[name], tok)

    def sb(self, name, shape, dt=F32):
        return self.stack.enter_context(self.nc.sbuf_tensor(name, shape, dt))

    def ps(self, name, shape, dt=F32):
        return self.stack.enter_context(self.nc.psum_tensor(name, shape, dt))


def barrier(ctx, block):
    bar = ctx.sems["bar"]
    tgt = ctx.n["bar"] + 5
    ctx.n["bar"] = tgt
    dtoks = {k: ctx.n[k] for k in ["dS", "d0", "d1", "d2", "d3", "d4", "d5"]}
    gtoks = {k: ctx.n[k] for k in ["dG", "g0", "g1", "g2", "g3", "g4"]}

    @block.sync
    def _(s):
        for k, v in dtoks.items():
            if v:
                s.wait_ge(ctx.sems[k], v)
        s.nop().then_inc(bar, 1)
        s.wait_ge(bar, tgt)

    @block.gpsimd
    def _(g):
        for k, v in gtoks.items():
            if v:
                g.wait_ge(ctx.sems[k], v)
        g.drain()
        g.nop().then_inc(bar, 1)
        g.wait_ge(bar, tgt)

    @block.tensor
    def _(t):
        t.drain()
        t.nop().then_inc(bar, 1)
        t.wait_ge(bar, tgt)

    @block.scalar
    def _(s):
        s.drain()
        s.nop().then_inc(bar, 1)
        s.wait_ge(bar, tgt)

    @block.vector
    def _(v):
        v.drain()
        v.nop().then_inc(bar, 1)
        v.wait_ge(bar, tgt)


def r_pkm(ap2d, p=P):
    return ap2d.rearrange("(k p) m -> p k m", p=p)


def r_pk(ap1d, p=P):
    return ap1d.rearrange("(k p) -> p k", p=p)


def emit_prologue(ctx, block, scr, need, stage="all"):
    inp = ctx.inputs

    casts = []
    if "moe" in need:
        casts.append((scr["rw_bf"], inp["L0_ffn_router_w"]))
    if "perf" in need:
        casts.append((scr["qkv_bf"], inp["L0_attn_qkv_w"]))
        casts.append((scr["proj_bf"], inp["L0_attn_proj_w"]))
        casts.append((scr["Wbig"][1024:1025, 0:D], inp["L0_attn_qkv_b"][2048:3072].unsqueeze(0)))
        casts.append((scr["projT_aug"][1024:1025, :], inp["L0_attn_proj_b"].unsqueeze(0)))
    if "l1att" in need:
        casts.append((scr["Wq_bf"], inp["L1_attn_Wq"]))
        casts.append((scr["Wk_bf"], inp["L1_attn_Wk"]))
        casts.append((scr["Wv_bf"], inp["L1_attn_Wv"]))
        casts.append((scr["Wg1_bf"], inp["L1_attn_Wg1"]))
        casts.append((scr["Wg2_bf"], inp["L1_attn_Wg2"]))
        casts.append((scr["Wo_bf"], inp["L1_attn_Wo"]))
    if "l1ffn" in need:
        casts.append((scr["Wup_bf"], inp["L1_ffn_W_up"]))
        casts.append((scr["bup_bf"], inp["L1_ffn_b_up"]))
        casts.append((scr["br_bf"], inp["L1_ffn_br"].unsqueeze(0)))
        casts.append((scr["bd_bf"], inp["L1_ffn_b_down"]))
    ctok = {}

    if stage in ("all", "casts"):
        @block.gpsimd
        def _(g):
            for j, (dst, src) in enumerate(casts):
                ctok[j] = ctx.inc(g.dma_start(out=dst, in_=src), "dG", 16)
        ctx.casts_done = ctx.n["dG"]
    if stage == "casts":
        return

    if "moe" in need:
        with ExitStack() as ls:
            en = ls.enter_context
            swg = [en(ctx.nc.sbuf_tensor(f"prg{i}", [P, DK, IHW], BF16)) for i in range(2)]
            swu = [en(ctx.nc.sbuf_tensor(f"pru{i}", [P, DK, IHW], BF16)) for i in range(2)]
            swd = [en(ctx.nc.sbuf_tensor(f"prd{i}", [P, ITH, D], BF16)) for i in range(2)]
            sin = [en(ctx.nc.sbuf_tensor(f"prf{i}", [P, D], F32)) for i in range(4)]
            sbin = [en(ctx.nc.sbuf_tensor(f"prc{i}", [P, IHW], BF16)) for i in range(4)]
            pwt = [en(ctx.nc.psum_tensor(f"prp{i}", [P, P], BF16)) for i in range(8)]
            tb = {k: ctx.n[k] for k in (SEMS + MOE_SEMS)}

            LDSL = ["d0", "d1", "e1", "e3"]
            tiles_w = [("g", it) for it in range(ITH)] + [("u", it) for it in range(ITH)] + [("d", kd) for kd in range(DK)]
            NTIL = len(tiles_w)
            sched = []
            for w in range(2 * E):
                e, ih = divmod(w, 2)
                for (kind, idx) in tiles_w:
                    sched.append((w, e, ih, kind, idx))
            G = len(sched)
            ntr_of = lambda kind: DK if kind in ("g", "u") else ITH
            cum_tr = [0] * (G + 1)
            SYNCK = ("g",)
            cast_pre = [0] * (G + 1)
            for g in range(G):
                cum_tr[g + 1] = cum_tr[g] + ntr_of(sched[g][3])
                cast_pre[g + 1] = cast_pre[g] + (1 if sched[g][3] in SYNCK else 0)
            GSL = ["w0", "w1", "w2", "w3"]
            ld_cnt = {}
            gld_cnt = {}
            _lc = {0: 0, 1: 0, 2: 0, 3: 0}
            _gc = {0: 0, 1: 0, 2: 0, 3: 0}
            for g in range(G):
                if sched[g][3] in SYNCK:
                    _lc[g % 4] += 1
                    ld_cnt[g] = _lc[g % 4]
                else:
                    _gc[g % 4] += 1
                    gld_cnt[g] = _gc[g % 4]

            def src_ap(g):
                w, e, ih, kind, idx = sched[g]
                ihs = ih * IHW
                if kind == "g":
                    return inp["L0_ffn_gate_w"][e][ihs + idx * P:ihs + (idx + 1) * P, :]
                if kind == "u":
                    return inp["L0_ffn_up_w"][e][ihs + idx * P:ihs + (idx + 1) * P, :]
                return inp["L0_ffn_down_w"][e][idx * P:(idx + 1) * P, ihs:ihs + IHW]

            def wid(g):
                return sched[g][0]

            WSEMS = {"g": ["xo", "e2"], "u": ["mke", "zp"], "d": ["yf", "gu"]}
            wsem = lambda w, kind: WSEMS[kind][w % 2]

            def wr_piece(s, w, kind):
                e_, ih_ = divmod(w, 2)
                bb = w % 2
                if kind == "g":
                    evtgt = cum_tr[NTIL * w + ITH]
                    buf = swg[bb]
                    dstap = scr["gwT"][e_][ih_]
                elif kind == "u":
                    evtgt = cum_tr[NTIL * w + 2 * ITH]
                    buf = swu[bb]
                    dstap = scr["uwT"][e_][ih_]
                else:
                    evtgt = cum_tr[NTIL * (w + 1)]
                    buf = swd[bb]
                    dstap = scr["dwT"][e_][ih_]
                s.wait_ge(ctx.sems["vS"], tb["vS"] + evtgt)
                ctx.inc(s.dma_start(out=dstap, in_=buf[:]), wsem(w, kind), 16)

            @block.sync
            def _(s):
                s.wait_ge(ctx.sems["dG"], ctx.casts_done)
                for g in range(G):
                    w = wid(g)
                    t = g % NTIL
                    if t == 0 and w >= 1:
                        wr_piece(s, w - 1, "g")
                        wr_piece(s, w - 1, "u")
                        wr_piece(s, w - 1, "d")
                    if sched[g][3] not in SYNCK:
                        continue
                    if g >= 4:
                        s.wait_ge(ctx.sems["sS"], tb["sS"] + cast_pre[g - 3])
                    ctx.inc(s.dma_start(out=sin[g % 4][:, 0:D], in_=src_ap(g)), LDSL[g % 4], 16)
                wr_piece(s, 2 * E - 1, "g")
                wr_piece(s, 2 * E - 1, "u")
                wr_piece(s, 2 * E - 1, "d")
                for sm in ("xo", "e2", "mke", "zp", "yf", "gu"):
                    s.wait_ge(ctx.sems[sm], ctx.n[sm])

            @block.scalar
            def _(s):
                for g in range(G):
                    if sched[g][3] not in SYNCK:
                        continue
                    s.wait_ge(ctx.sems[LDSL[g % 4]], tb[LDSL[g % 4]] + 16 * ld_cnt[g])
                    if g >= 4:
                        s.wait_ge(ctx.sems["tS"], tb["tS"] + cum_tr[g - 3])
                    ctx.inc(s.activation(sbin[g % 4][:, 0:D], sin[g % 4][:, 0:D], AF.Copy), "sS", 1)

            @block.gpsimd
            def _(gp):
                for g in range(G):
                    if sched[g][3] in SYNCK:
                        continue
                    gp.wait_ge(ctx.sems["tS"], tb["tS"] + cum_tr[g - 3])
                    cols = D if sched[g][3] == "u" else IHW
                    ctx.inc(gp.dma_start(out=sbin[g % 4][:, 0:cols], in_=src_ap(g)), GSL[g % 4], 16)

            @block.tensor
            def _(t):
                for g in range(G):
                    if sched[g][3] in SYNCK:
                        ctx.wait(t, "sS", tb["sS"] + cast_pre[g + 1])
                    else:
                        t.wait_ge(ctx.sems[GSL[g % 4]], tb[GSL[g % 4]] + 16 * gld_cnt[g])
                    for k in range(ntr_of(sched[g][3])):
                        trn = cum_tr[g] + k
                        if trn >= 8:
                            t.wait_ge(ctx.sems["vS"], tb["vS"] + trn - 7)
                        ctx.inc(t.transpose(pwt[trn % 8][:], sbin[g % 4][:, k * P:(k + 1) * P], ctx.ident[:]), "tS", 1)

            @block.vector
            def _(v):
                lastblk = None
                for g in range(G):
                    w, e, ih, kind, idx = sched[g]
                    if (w, kind) != lastblk and w >= 2:
                        sm = wsem(w, kind)
                        v.wait_ge(ctx.sems[sm], tb[sm] + 16 * (w // 2))
                    lastblk = (w, kind)
                    for k in range(ntr_of(kind)):
                        trn = cum_tr[g] + k
                        ctx.wait(v, "tS", tb["tS"] + trn + 1)
                        src = pwt[trn % 8][:]
                        if kind == "g":
                            dst = swg[w % 2][:, k, idx * P:(idx + 1) * P]
                        elif kind == "u":
                            dst = swu[w % 2][:, k, idx * P:(idx + 1) * P]
                        else:
                            dst = swd[w % 2][:, k, idx * P:(idx + 1) * P]
                        ctx.inc(v.tensor_copy(dst, src), "vS", 1)

        with ExitStack() as ls:
            st8 = ls.enter_context(ctx.nc.sbuf_tensor("prr", [P, DK, 8], BF16))
            srwi = ls.enter_context(ctx.nc.sbuf_tensor("prri", [8, D], BF16))
            prw = ls.enter_context(ctx.nc.psum_tensor("prwp", [P, 8], BF16))
            tb = {k: ctx.n[k] for k in SEMS}

            @block.sync
            def _(s):
                s.wait_ge(ctx.sems["dG"], ctx.casts_done)
                ctx.inc(s.dma_start(out=srwi[:], in_=scr["rw_bf"][:]), "dS", 16)
                s.wait_ge(ctx.sems["dS"], ctx.n["dS"])

            @block.tensor
            def _(t):
                ctx.wait(t, "dS", tb["dS"] + 16)
                for k in range(DK):
                    if k > 0:
                        ctx.wait(t, "vS", tb["vS"] + k)
                    ctx.inc(t.transpose(prw[:], srwi[:, k * P:(k + 1) * P], ctx.ident[0:8, 0:8]), "tS", 1)

            @block.vector
            def _(v):
                for k in range(DK):
                    ctx.wait(v, "tS", tb["tS"] + k + 1)
                    ctx.inc(v.tensor_copy(st8[:, k, :], prw[:]), "vS", 1)

            @block.sync
            def _(s):
                ctx.wait(s, "vS", tb["vS"] + DK)
                ctx.inc(s.dma_start(out=r_pkm(scr["rwT"]), in_=st8[:]), "dS", 16)

    if "perf" in need:
        with ExitStack() as ls:
            stg = ls.enter_context(ctx.nc.sbuf_tensor("pstg", [P, D], BF16))
            @block.sync
            def _(s):
                s.wait_ge(ctx.sems["dG"], ctx.casts_done)
                for k in range(DK):
                    s.wait_ge(ctx.sems["dS"], ctx.n["dS"])
                    ctx.inc(s.dma_start(out=stg[:], in_=scr["qkv_bf"][2048:3072, k * P:(k + 1) * P], transpose=True), "dS", 16)
                    s.wait_ge(ctx.sems["dS"], ctx.n["dS"])
                    ctx.inc(s.dma_start(out=scr["Wbig"][k * P:(k + 1) * P, 0:D], in_=stg[:]), "dS", 16)
                for k in range(DK):
                    s.wait_ge(ctx.sems["dS"], ctx.n["dS"])
                    ctx.inc(s.dma_start(out=stg[:], in_=scr["proj_bf"][:, k * P:(k + 1) * P], transpose=True), "dS", 16)
                    s.wait_ge(ctx.sems["dS"], ctx.n["dS"])
                    ctx.inc(s.dma_start(out=scr["projT_aug"][k * P:(k + 1) * P, 0:D], in_=stg[:]), "dS", 16)
                s.wait_ge(ctx.sems["dS"], ctx.n["dS"])

        with ExitStack() as ls:
            spm = ls.enter_context(ctx.nc.sbuf_tensor("ppm", [64, 64], F32))
            spmb = ls.enter_context(ctx.nc.sbuf_tensor("ppmb", [64, 64], BF16))
            spmT = ls.enter_context(ctx.nc.sbuf_tensor("ppmT", [64, 64], BF16))
            sqh = ls.enter_context(ctx.nc.sbuf_tensor("pqh", [64, D], BF16))
            sphi = ls.enter_context(ctx.nc.sbuf_tensor("pphi", [64, D], BF16))
            sbia = ls.enter_context(ctx.nc.sbuf_tensor("pbia", [64, 1], BF16))
            sbiaf = ls.enter_context(ctx.nc.sbuf_tensor("pbiaf", [64, 1], F32))
            sbrow = ls.enter_context(ctx.nc.sbuf_tensor("pbrow", [1, 64], BF16))
            stg2 = ls.enter_context(ctx.nc.sbuf_tensor("pstg2", [P, DK, 64], BF16))
            pph = ls.enter_context(ctx.nc.psum_tensor("ppph", [64, D], F32))
            ppT = ls.enter_context(ctx.nc.psum_tensor("pppT", [64, 64], BF16))
            ppT2 = ls.enter_context(ctx.nc.psum_tensor("pppT2", [P, 64], BF16))
            pbr = ls.enter_context(ctx.nc.psum_tensor("ppbr", [1, 64], F32))

            @block.sync
            def _(s):
                ctx.inc(s.dma_start(out=spm[:], in_=inp["L0_attn_projection_matrix"][:]), "dS", 16)
                s.wait_ge(ctx.sems["dS"], ctx.n["dS"])
            t_pm = ctx.n["dS"]

            @block.vector
            def _(v):
                ctx.wait(v, "dS", t_pm)
                ctx.inc(v.tensor_copy(spmb[:], spm[:]), "vS", 1)
            v_pm = ctx.n["vS"]

            @block.tensor
            def _(t):
                ctx.wait(t, "vS", v_pm)
                ctx.inc(t.transpose(ppT[:], spmb[:], ctx.ident[0:64, 0:64]), "tS", 1)
            t_pmT = ctx.n["tS"]

            @block.vector
            def _(v):
                ctx.wait(v, "tS", t_pmT)
                ctx.inc(v.tensor_copy(spmT[:], ppT[:]), "vS", 1)
            v_pmT = ctx.n["vS"]

            for hp in range(2 * NH):
                part, h = divmod(hp, NH)
                row0 = part * D + h * HD
                col0 = D + part * D + h * HD
                tb = {k: ctx.n[k] for k in SEMS}

                @block.sync
                def _(s):
                    ctx.wait(s, "tS", tb["tS"])
                    ctx.wait(s, "vS", tb["vS"])
                    s.wait_ge(ctx.sems["dS"], tb["dS"])
                    ctx.inc(s.dma_start(out=sqh[:], in_=scr["qkv_bf"][row0:row0 + HD, :]), "dS", 16)
                    ctx.inc(s.dma_start(out=sbiaf[:], in_=inp["L0_attn_qkv_b"][row0:row0 + HD].unsqueeze(1)), "dS", 16)
                    s.wait_ge(ctx.sems["dS"], ctx.n["dS"])

                @block.vector
                def _(v):
                    ctx.wait(v, "dS", tb["dS"] + 32)
                    ctx.inc(v.tensor_copy(sbia[:], sbiaf[:]), "vS", 1)

                @block.tensor
                def _(t):
                    ctx.wait(t, "dS", tb["dS"] + 32)
                    ctx.wait(t, "vS", max(v_pmT, tb["vS"] + 1))
                    ctx.wait(t, "sS", tb["sS"])
                    t.matmul(pph[:, 0:512], spmT[:], sqh[:, 0:512], start=True, stop=True)
                    ctx.inc(t.matmul(pph[:, 512:1024], spmT[:], sqh[:, 512:1024], start=True, stop=True), "tS", 1)
                    ctx.inc(t.matmul(pbr[:], sbia[:], spmT[:], start=True, stop=True), "tS", 1)

                @block.scalar
                def _(sc):
                    ctx.wait(sc, "tS", tb["tS"] + 2)
                    ctx.inc(sc.activation(sphi[:], pph[:], AF.Copy, scale=0.125), "sS", 1)
                    ctx.inc(sc.activation(sbrow[:], pbr[:], AF.Copy, scale=0.125), "sS", 1)

                @block.tensor
                def _(t):
                    ctx.wait(t, "sS", tb["sS"] + 2)
                    for k in range(DK):
                        if k > 0:
                            ctx.wait(t, "vS", tb["vS"] + 1 + k)
                        ctx.inc(t.transpose(ppT2[:], sphi[:, k * P:(k + 1) * P], ctx.ident[0:64, 0:64]), "tS", 1)

                @block.vector
                def _(v):
                    for k in range(DK):
                        ctx.wait(v, "tS", tb["tS"] + 2 + k + 1)
                        ctx.inc(v.tensor_copy(stg2[:, k, :], ppT2[:]), "vS", 1)

                @block.sync
                def _(s):
                    ctx.wait(s, "vS", tb["vS"] + 1 + DK)
                    ctx.inc(s.dma_start(out=r_pkm(scr["Wbig"][0:D, col0:col0 + HD]), in_=stg2[:]), "dS", 16)
                    ctx.wait(s, "sS", tb["sS"] + 2)
                    ctx.inc(s.dma_start(out=scr["Wbig"][1024:1025, col0:col0 + HD], in_=sbrow[:]), "dS", 16)

    if "l1att" in need:
        _emit_l1att_prep(ctx, block, scr)
    if "l1ffn" in need:
        _emit_l1ffn_prep(ctx, block, scr)


def _emit_l1att_prep(ctx, block, scr):
    inp = ctx.inputs
    with ExitStack() as ls:
        swo = ls.enter_context(ctx.nc.sbuf_tensor("lwo", [P, D], F32))
        sgam = ls.enter_context(ctx.nc.sbuf_tensor("lgam", [P, DK], F32))
        sbet = ls.enter_context(ctx.nc.sbuf_tensor("lbet", [P, DK], BF16))
        sbetf = ls.enter_context(ctx.nc.sbuf_tensor("lbetf", [P, DK], F32))
        swof = ls.enter_context(ctx.nc.sbuf_tensor("lwof", [P, D], BF16))
        swob = ls.enter_context(ctx.nc.sbuf_tensor("lwob", [P, DK, D], BF16))
        sbrow = ls.enter_context(ctx.nc.sbuf_tensor("lbrow", [1, D], BF16))
        pbr = ls.enter_context(ctx.nc.psum_tensor("lpbr", [1, D], F32))
        tb = {k: ctx.n[k] for k in SEMS}

        @block.sync
        def _(s):
            s.wait_ge(ctx.sems["dS"], tb["dS"])
            ctx.inc(s.dma_start(out=sgam[:], in_=r_pk(inp["L1_attn_gamma"]), allow_slow_non_contiguous=True), "dS", 16)
            ctx.inc(s.dma_start(out=sbetf[:], in_=r_pk(inp["L1_attn_beta"]), allow_slow_non_contiguous=True), "dS", 16)
            s.wait_ge(ctx.sems["dS"], ctx.n["dS"])

        @block.vector
        def _(v):
            ctx.wait(v, "dS", tb["dS"] + 32)
            ctx.inc(v.tensor_copy(sbet[:], sbetf[:]), "vS", 1)

        for k in range(DK):
            kb = {kk: ctx.n[kk] for kk in SEMS}

            @block.sync
            def _(s):
                ctx.wait(s, "vS", kb["vS"])
                s.wait_ge(ctx.sems["dS"], kb["dS"])
                ctx.inc(s.dma_start(out=swo[:], in_=inp["L1_attn_Wo"][k * P:(k + 1) * P, :]), "dS", 16)
                s.wait_ge(ctx.sems["dS"], ctx.n["dS"])

            @block.vector
            def _(v):
                ctx.wait(v, "dS", kb["dS"] + 16)
                ctx.inc(v.tensor_scalar(swof[:], swo[:], sgam[:, k:k + 1], None, ALU.mult), "vS", 1)

            @block.sync
            def _(s):
                ctx.wait(s, "vS", kb["vS"] + 1)
                ctx.inc(s.dma_start(out=scr["WoP_aug"][k * P:(k + 1) * P, :], in_=swof[:]), "dS", 16)

        kb = {kk: ctx.n[kk] for kk in SEMS}

        @block.sync
        def _(s):
            ctx.wait(s, "dG", ctx.casts_done)
            s.wait_ge(ctx.sems["dS"], kb["dS"])
            ctx.inc(s.dma_start(out=swob[:], in_=r_pkm(scr["Wo_bf"])), "dS", 16)
            s.wait_ge(ctx.sems["dS"], ctx.n["dS"])

        @block.tensor
        def _(t):
            ctx.wait(t, "dS", kb["dS"] + 16)
            ctx.wait(t, "vS", kb["vS"])
            for k in range(DK):
                t.matmul(pbr[:, 0:512], sbet[:, k:k + 1], swob[:, k, 0:512], start=(k == 0), stop=(k == DK - 1))
            for k in range(DK):
                mm = t.matmul(pbr[:, 512:1024], sbet[:, k:k + 1], swob[:, k, 512:1024], start=(k == 0), stop=(k == DK - 1))
            ctx.inc(mm, "tS", 1)

        @block.vector
        def _(v):
            ctx.wait(v, "tS", kb["tS"] + 1)
            ctx.inc(v.tensor_copy(sbrow[:], pbr[:]), "vS", 1)

        @block.sync
        def _(s):
            ctx.wait(s, "vS", kb["vS"] + 1)
            ctx.inc(s.dma_start(out=scr["WoP_aug"][1024:1025, :], in_=sbrow[:]), "dS", 16)


def _emit_l1ffn_prep(ctx, block, scr):
    inp = ctx.inputs
    with ExitStack() as ls:
        swd = ls.enter_context(ctx.nc.sbuf_tensor("fwd", [P, DK, HD], F32))
        swdb = ls.enter_context(ctx.nc.sbuf_tensor("fwdb", [P, DK, HD], BF16))
        swdf = ls.enter_context(ctx.nc.sbuf_tensor("fwdf", [P, DK, HD], BF16))
        sg = ls.enter_context(ctx.nc.sbuf_tensor("fg", [P, DK], F32))
        sbf = ls.enter_context(ctx.nc.sbuf_tensor("fbf", [P, DK], F32))
        sb16 = ls.enter_context(ctx.nc.sbuf_tensor("fb16", [P, DK], BF16))
        sbd = ls.enter_context(ctx.nc.sbuf_tensor("fbd", [1, HD], BF16))
        srow = ls.enter_context(ctx.nc.sbuf_tensor("frow", [1, HD], BF16))
        pbr = ls.enter_context(ctx.nc.psum_tensor("fpbr", [1, HD], F32))
        tb0 = {k: ctx.n[k] for k in SEMS}

        @block.sync
        def _(s):
            s.wait_ge(ctx.sems["dS"], tb0["dS"])
            ctx.inc(s.dma_start(out=sg[:], in_=r_pk(inp["L1_ffn_ln_g"]), allow_slow_non_contiguous=True), "dS", 16)
            ctx.inc(s.dma_start(out=sbf[:], in_=r_pk(inp["L1_ffn_ln_b"]), allow_slow_non_contiguous=True), "dS", 16)
            s.wait_ge(ctx.sems["dS"], ctx.n["dS"])

        @block.vector
        def _(v):
            ctx.wait(v, "dS", tb0["dS"] + 32)
            ctx.inc(v.tensor_copy(sb16[:], sbf[:]), "vS", 1)

        for e in range(E):
            kb = {kk: ctx.n[kk] for kk in SEMS}

            @block.sync
            def _(s):
                ctx.wait(s, "vS", kb["vS"])
                ctx.wait(s, "tS", kb["tS"])
                s.wait_ge(ctx.sems["dS"], kb["dS"])
                ctx.inc(s.dma_start(out=swd[:], in_=r_pkm(inp["L1_ffn_W_down"][e])), "dS", 16)
                ctx.wait(s, "dG", ctx.casts_done)
                ctx.inc(s.dma_start(out=sbd[:], in_=scr["bd_bf"][e].unsqueeze(0)), "dS", 16)
                s.wait_ge(ctx.sems["dS"], ctx.n["dS"])

            @block.vector
            def _(v):
                ctx.wait(v, "dS", kb["dS"] + 32)
                for k in range(DK):
                    v.tensor_scalar(swdf[:, k, :], swd[:, k, :], sg[:, k:k + 1], None, ALU.mult)
                ctx.inc(v.tensor_copy(swdb[:], swd[:]), "vS", 1)

            @block.tensor
            def _(t):
                ctx.wait(t, "vS", kb["vS"] + 1)
                for k in range(DK):
                    t.matmul(pbr[:], sb16[:, k:k + 1], swdb[:, k, :], start=(k == 0), stop=False)
                ctx.inc(t.matmul(pbr[:], ctx.cones[0:1, 0:1], sbd[:], start=False, stop=True), "tS", 1)

            @block.vector
            def _(v):
                ctx.wait(v, "tS", kb["tS"] + 1)
                ctx.inc(v.tensor_copy(srow[:], pbr[:]), "vS", 1)

            @block.sync
            def _(s):
                ctx.wait(s, "vS", kb["vS"] + 2)
                ctx.inc(s.dma_start(out=r_pkm(scr["WdP"][e]), in_=swdf[:]), "dS", 16)
                ctx.inc(s.dma_start(out=scr["bdP"][e].unsqueeze(0), in_=srow[:]), "dS", 16)


def emit_zt(ctx, block, scr, szst, zw_fin):
    ztb = {k: ctx.n[k] for k in ["d2", "d3", "d4", "d5"]}

    @block.sync
    def _(s):
        s.wait_ge(ctx.sems["g3"], zw_fin[0])
        s.wait_ge(ctx.sems["g4"], zw_fin[1])
        for k in range(DK):
            ts_ = f"d{2 + k % 2}"
            ws_ = f"d{4 + k % 2}"
            if k >= 2:
                ctx.wait(s, ws_, ztb[ws_] + 16 * (k // 2))
            ctx.inc(s.dma_start(out=szst[k % 2][:], in_=scr["z"][:, k * P:(k + 1) * P], transpose=True), ts_, 16)
            ctx.wait(s, ts_, ztb[ts_] + 16 * (k // 2 + 1))
            ctx.inc(s.dma_start(out=scr["zT"][k * P:(k + 1) * P, :], in_=szst[k % 2][:]), ws_, 16)
        s.wait_ge(ctx.sems["d4"], ctx.n["d4"])
        s.wait_ge(ctx.sems["d5"], ctx.n["d5"])


def emit_ln(ctx, block, scr, src_ap, hold=None, defer_zt=False, xtok=None):
    ls = hold if hold is not None else ExitStack()
    try:
        en = ls.enter_context
        sx = [en(ctx.nc.sbuf_tensor(f"lnx{i}", [P, D], F32)) for i in range(3)]
        sz = [en(ctx.nc.sbuf_tensor(f"lnz{i}", [P, D], BF16)) for i in range(2)]
        st12 = [en(ctx.nc.sbuf_tensor(f"lnst{i}", [P, 12], F32)) for i in range(2)]
        smv = [en(ctx.nc.sbuf_tensor(f"lnmv{i}", [P, 2], F32)) for i in range(2)]
        ssd = [en(ctx.nc.sbuf_tensor(f"lnsd{i}", [P, 1], F32)) for i in range(2)]
        srv = [en(ctx.nc.sbuf_tensor(f"lnrv{i}", [P, 1], F32)) for i in range(2)]
        szst = [en(ctx.nc.sbuf_tensor(f"lnzt{i}", [P, T], BF16)) for i in range(2)]
        tb = {k: ctx.n[k] for k in SEMS}
        tok_x = {}
        tok_zw = {}

        @block.gpsimd
        def _(g):
            for i in range(NT):
                if xtok is not None:
                    ctx.wait(g, xtok[i][0], xtok[i][1])
                if i >= 3:
                    ctx.wait(g, "vS", tb["vS"] + 4 * (i - 2))
                sx_sem = f"g{i % 3}"
                tok_x[i] = (sx_sem, ctx.inc(g.dma_start(out=sx[i % 3][:], in_=src_ap[i * P:(i + 1) * P, :]), sx_sem, 16))
                if i >= 1:
                    ctx.wait(g, "vS", tb["vS"] + 4 * i)
                    zw_sem = f"g{3 + (i - 1) % 2}"
                    tok_zw[i - 1] = (zw_sem, ctx.inc(g.dma_start(out=scr["z"][(i - 1) * P:i * P, :], in_=sz[(i - 1) % 2][:]), zw_sem, 16))
            ctx.wait(g, "vS", tb["vS"] + 4 * NT)
            zw_sem = f"g{3 + (NT - 1) % 2}"
            tok_zw[NT - 1] = (zw_sem, ctx.inc(g.dma_start(out=scr["z"][(NT - 1) * P:NT * P, :], in_=sz[(NT - 1) % 2][:]), zw_sem, 16))

        @block.vector
        def _(v):
            for i in range(NT):
                j = i % 2
                ctx.wait(v, tok_x[i][0], tok_x[i][1])
                if i >= 2:
                    ctx.wait(v, "vS", tb["vS"] + 4 * (i - 2) + 2)
                v.bn_stats(st12[j][:, 0:6], sx[i % 3][:, 0:512])
                ctx.inc(v.bn_stats(st12[j][:, 6:12], sx[i % 3][:, 512:1024]), "vS", 1)
                ctx.wait(v, "vS", tb["vS"] + 4 * i + 1)
                if i >= 2:
                    ctx.wait(v, "sS", tb["sS"] + i - 1)
                ctx.inc(v.bn_aggr(smv[j][:], st12[j][:]), "vS", 1)
                ctx.wait(v, "sS", tb["sS"] + i + 1)
                ctx.inc(v.reciprocal(srv[j][:], ssd[j][:]), "vS", 1)
                ctx.wait(v, "vS", tb["vS"] + 4 * i + 3)
                if i >= 2:
                    ctx.wait(v, tok_zw[i - 2][0], tok_zw[i - 2][1])
                ctx.inc(v.tensor_scalar(sz[j][:], sx[i % 3][:], smv[j][:, 0:1], srv[j][:], ALU.subtract, ALU.mult), "vS", 1)

        @block.scalar
        def _(s):
            for i in range(NT):
                j = i % 2
                ctx.wait(s, "vS", tb["vS"] + 4 * i + 2)
                ctx.inc(s.activation(ssd[j][:], smv[j][:, 1:2], AF.Sqrt, bias=ctx.ceps[:, 0:1]), "sS", 1)

        zw_fin = [ctx.n["g3"], ctx.n["g4"]]
        if defer_zt:
            return (szst, zw_fin)
        emit_zt(ctx, block, scr, szst, zw_fin)
    finally:
        if hold is None:
            ls.close()
    return None


def emit_router(ctx, block, scr):
    with ExitStack() as ls:
        srw = ls.enter_context(ctx.nc.sbuf_tensor("rt_rw", [P, DK, 8], BF16))
        rtb = [ls.enter_context(ctx.nc.sbuf_tensor(f"rt_b{i}", [P, DK, 512], BF16)) for i in range(2)]
        slg = ls.enter_context(ctx.nc.sbuf_tensor("rt_lg", [32, 512], F32))
        sh0 = ls.enter_context(ctx.nc.sbuf_tensor("rt_s0", [32, 512], F32))
        sh1 = ls.enter_context(ctx.nc.sbuf_tensor("rt_s1", [32, 512], F32))
        smk = [ls.enter_context(ctx.nc.sbuf_tensor(f"rt_mk{i}", [8, 512], BF16)) for i in range(2)]
        plg = [ls.enter_context(ctx.nc.psum_tensor(f"rt_p{i}", [8, 512], F32)) for i in range(2)]
        tb = {k: ctx.n[k] for k in SEMS}
        NCH = T // 512

        tok_rtb = {}

        @block.sync
        def _(s):
            s.wait_ge(ctx.sems["dS"], tb["dS"])
            ctx.inc(s.dma_start(out=srw[:], in_=r_pkm(scr["rwT"])), "dS", 16)
            s.wait_ge(ctx.sems["dS"], ctx.n["dS"])
            for c in range(NCH):
                if c >= 2:
                    ctx.wait(s, "tS", tb["tS"] + c - 1)
                sem = f"d{c % 2}"
                tok_rtb[c] = (sem, ctx.inc(s.dma_start(out=rtb[c % 2][:], in_=r_pkm(scr["zT"][:, c * 512:(c + 1) * 512])), sem, 16))

        @block.tensor
        def _(t):
            for c in range(NCH):
                ctx.wait(t, "dS", tb["dS"] + 16)
                ctx.wait(t, tok_rtb[c][0], tok_rtb[c][1])
                if c >= 2:
                    ctx.wait(t, "vS", tb["vS"] + 1 + 9 * (c - 1) + 1)
                for k in range(DK):
                    mm = t.matmul(plg[c % 2][:], srw[:, k, :], rtb[c % 2][:, k, :], start=(k == 0), stop=(k == DK - 1))
                ctx.inc(mm, "tS", 1)

        import os
        rdbg = int(os.environ.get("RDBG", "0"))
        tok_smk = {c: (f"g{c % 2}", tb[f"g{c % 2}"] + 16 * (c // 2 + 1)) for c in range(NCH)}

        @block.vector
        def _(v):
            m1 = [(i ^ 4) if i < 8 else i for i in range(32)]
            m2 = [(i ^ 2) if i < 8 else i for i in range(32)]
            m3 = [(i ^ 1) if i < 8 else i for i in range(32)]
            v.memset(slg[:], 0.0)
            v.memset(sh0[:], 0.0)
            ctx.inc(v.memset(sh1[:], 0.0), "vS", 1)
            v.wait_ge(ctx.sems["vS"], tb["vS"] + 1)
            for c in range(NCH):
                vb = tb["vS"] + 1 + 9 * c
                ctx.wait(v, "tS", tb["tS"] + c + 1)
                if c >= 1:
                    ctx.wait(v, "vS", vb - 1)
                ctx.inc(v.tensor_copy(slg[0:8, :], plg[c % 2][:]), "vS", 1)
                ctx.wait(v, "vS", vb + 1)
                ctx.inc(v.stream_shuffle(sh0[:], slg[:], m1), "vS", 1)
                ctx.wait(v, "vS", vb + 2)
                ctx.inc(v.tensor_tensor(sh1[:], slg[:], sh0[:], ALU.max), "vS", 1)
                ctx.wait(v, "vS", vb + 3)
                ctx.inc(v.stream_shuffle(sh0[:], sh1[:], m2), "vS", 1)
                ctx.wait(v, "vS", vb + 4)
                ctx.inc(v.tensor_tensor(sh1[:], sh1[:], sh0[:], ALU.max), "vS", 1)
                ctx.wait(v, "vS", vb + 5)
                ctx.inc(v.stream_shuffle(sh0[:], sh1[:], m3), "vS", 1)
                ctx.wait(v, "vS", vb + 6)
                ctx.inc(v.tensor_tensor(sh1[:], sh1[:], sh0[:], ALU.max), "vS", 1)
                ctx.wait(v, "vS", vb + 7)
                if c >= 2:
                    ctx.wait(v, tok_smk[c - 2][0], tok_smk[c - 2][1])
                ctx.inc(v.tensor_tensor(smk[c % 2][:], slg[0:8, :], sh1[0:8, :], ALU.is_ge), "vS", 1)
                ctx.inc(v.tensor_copy(sh0[0:1, 0:1], sh1[0:1, 0:1]), "vS", 1)

        @block.gpsimd
        def _(g):
            for c in range(NCH):
                ctx.wait(g, "vS", tb["vS"] + 1 + 9 * c + 8)
                sem = f"g{c % 2}"
                assert tok_smk[c] == (sem, ctx.inc(g.dma_start(out=scr["msk"][:, c * 512:(c + 1) * 512], in_=smk[c % 2][:]), sem, 16))
            g.wait_ge(ctx.sems["g0"], ctx.n["g0"])
            g.wait_ge(ctx.sems["g1"], ctx.n["g1"])


MOE_SPARSE_MOE = True



def emit_moe_sweeps(ctx, block, scr, src_ap, dst_ap, half):
    NW = NQ * 2 * E
    b = {k: ctx.n[k] for k in (SEMS + MOE_SEMS)}

    with ExitStack() as ls:
        en = ls.enter_context
        swg = [en(ctx.nc.sbuf_tensor(f"mwg{i}", [P, DK, IHW], BF16)) for i in range(2)]
        swu = [en(ctx.nc.sbuf_tensor(f"mwu{i}", [P, DK, IHW], BF16)) for i in range(2)]
        swd = [en(ctx.nc.sbuf_tensor(f"mwd{i}", [P, ITH, D], BF16)) for i in range(2)]
        zq = en(ctx.nc.sbuf_tensor("mzq", [P, DK, QT], BF16))
        zpr = en(ctx.nc.sbuf_tensor("mzp", [P, DK, QT], BF16))
        mrq = en(ctx.nc.sbuf_tensor("mmrq", [1, E, QT], BF16))
        smkr = en(ctx.nc.sbuf_tensor("mmkr", [P, QT], BF16))
        sh = en(ctx.nc.sbuf_tensor("msh", [P, ITH, QT], BF16))
        sil = [en(ctx.nc.sbuf_tensor(f"msil{i}", [P, QT], F32)) for i in range(2)]
        sy = en(ctx.nc.sbuf_tensor("msy", [P, QTT, D], F32))
        sxq = [en(ctx.nc.sbuf_tensor(f"msxq{i}", [P, D], F32)) for i in range(2)]
        pg = [en(ctx.nc.psum_tensor(f"mpg{i}", [P, QT], F32)) for i in range(2)]
        pu = [en(ctx.nc.psum_tensor(f"mpu{i}", [P, QT], F32)) for i in range(2)]
        py = [en(ctx.nc.psum_tensor(f"mpy{i}", [P, D], F32)) for i in range(2)]

        sync_ops = []
        for w in range(NW):
            if w % (2 * E) == 0:
                sync_ops.append(("mr", w // (2 * E)))
                sync_ops.append(("ze", w // (2 * E)))
            sync_ops.append(("w", w))
        tok_ze = {}
        tok_mr = {}
        tok_w = {}

        @block.sync
        def _(s):
            for op in sync_ops:
                if op[0] == "mr":
                    q = op[1]
                    if q > 0:
                        s.wait_ge(ctx.sems["mk"], b["mk"] + 8 * q)
                    tok_mr[q] = ("d3", ctx.inc(s.dma_start(out=mrq[:], in_=scr["msk"][:, q * QT:(q + 1) * QT].unsqueeze(0)), "d3", 16))
                elif op[0] == "ze":
                    q = op[1]
                    if q > 0:
                        s.wait_ge(ctx.sems["zp"], b["zp"] + 8 * q)
                    tok_ze[q] = ("d2", ctx.inc(s.dma_start(out=zq[:], in_=r_pkm(scr["zT"][:, q * QT:(q + 1) * QT])), "d2", 16))
                else:
                    w = op[1]
                    q, r = divmod(w, 2 * E)
                    e, ih = divmod(r, 2)
                    bb = w % 2
                    sem = f"d{bb}"
                    if w >= 2:
                        s.wait_ge(ctx.sems["yr"], b["yr"] + 4 * (w - 1))
                    ihs = ih * IHW
                    ctx.inc(s.dma_start(out=swg[bb][:], in_=scr["gwT"][e][ih]), sem, 16)
                    ctx.inc(s.dma_start(out=swu[bb][:], in_=scr["uwT"][e][ih]), sem, 16)
                    tok_w[w] = (sem, ctx.inc(s.dma_start(out=swd[bb][:], in_=scr["dwT"][e][ih]), sem, 16))

        tok_xl = {}
        tok_st = {}

        @block.gpsimd
        def _(g):
            for i in range(NQ * QTT):
                j = i % 2
                if i >= 2:
                    ctx.wait(g, tok_st[i - 2][0], tok_st[i - 2][1])
                sem = f"g{2 + j}"
                tok_xl[i] = (sem, ctx.inc(g.dma_start(out=sxq[j][:], in_=src_ap[i * P:(i + 1) * P, :]), sem, 16))
                ctx.wait(g, "xo", b["xo"] + i + 1)
                tok_st[i] = (f"g{j}", ctx.inc(g.dma_start(out=dst_ap[i * P:(i + 1) * P, :], in_=sxq[j][:]), f"g{j}", 16))
            g.wait_ge(ctx.sems["g0"], ctx.n["g0"])
            g.wait_ge(ctx.sems["g1"], ctx.n["g1"])

        @block.tensor
        def _(t):
            n = 0
            m = 0
            for w in range(NW):
                q, r = divmod(w, 2 * E)
                e, ih = divmod(r, 2)
                bb = w % 2
                zidx = q * E + e
                ctx.wait(t, tok_w[w][0], tok_w[w][1])
                if ih == 0:
                    if zidx == 0:
                        ctx.wait(t, tok_mr[0][0], tok_mr[0][1])
                        ctx.inc(t.matmul(pg[0][:], ctx.cones[0:1, 0:P], mrq[0:1, 0, :], start=True, stop=True), "mk", 1)
                    ctx.wait(t, "zp", b["zp"] + zidx + 1)
                for it in range(ITH):
                    par = n % 2
                    if n >= 2:
                        ctx.wait(t, "hs", b["hs"] + n - 1)
                    for k in range(DK):
                        t.matmul(pg[par][:], swg[bb][:, k, it * P:(it + 1) * P], zpr[:, k, :], start=(k == 0), stop=(k == DK - 1))
                    for k in range(DK):
                        mm = t.matmul(pu[par][:], swu[bb][:, k, it * P:(it + 1) * P], zpr[:, k, :], start=(k == 0), stop=(k == DK - 1))
                    ctx.inc(mm, "gu", 1)
                    n += 1
                ctx.wait(t, "hs", b["hs"] + n)
                if ih == 1 and zidx + 1 < NQ * E:
                    zn = zidx + 1
                    qn, en_ = divmod(zn, E)
                    ctx.wait(t, tok_mr[qn][0], tok_mr[qn][1])
                    ctx.inc(t.matmul(pg[0][:], ctx.cones[0:1, 0:P], mrq[0:1, en_, :], start=True, stop=True), "mk", 1)
                for ts in range(QTT):
                    pyb = py[m % 2]
                    if m >= 2:
                        ctx.wait(t, "yf", b["yf"] + m - 1)
                    for it in range(ITH):
                        t.matmul(pyb[:, 0:512], sh[:, it, ts * P:(ts + 1) * P], swd[bb][:, it, 0:512], start=(it == 0), stop=(it == ITH - 1))
                        mm = t.matmul(pyb[:, 512:1024], sh[:, it, ts * P:(ts + 1) * P], swd[bb][:, it, 512:1024], start=(it == 0), stop=(it == ITH - 1))
                    ctx.inc(mm, "yr", 1)
                    m += 1

        @block.scalar
        def _(s):
            for n in range(NW * ITH):
                par = n % 2
                ctx.wait(s, "gu", b["gu"] + n + 1)
                if n >= 2:
                    ctx.wait(s, "hs", b["hs"] + n - 1)
                ctx.inc(s.activation(sil[par][:], pg[par][:], AF.Silu), "sl", 1)

        def zbuild(v, zidx):
            q, e = divmod(zidx, E)
            ctx.wait(v, "mk", b["mk"] + zidx + 1)
            if zidx >= 1:
                ctx.wait(v, "zp", b["zp"] + zidx)
            ctx.inc(v.tensor_copy(smkr[:], pg[0][:]), "mke", 1)
            ctx.wait(v, "mke", b["mke"] + zidx + 1)
            ctx.wait(v, tok_ze[q][0], tok_ze[q][1])
            ctx.wait(v, "gu", b["gu"] + ITH * 2 * zidx)
            for k in range(DK):
                ins = v.tensor_tensor(zpr[:, k, :], zq[:, k, :], smkr[:], ALU.mult)
            ctx.inc(ins, "zp", 1)

        @block.vector
        def _(v):
            n = 0
            m = 0
            zbuild(v, 0)
            for w in range(NW):
                q, r = divmod(w, 2 * E)
                e, ih = divmod(r, 2)
                zidx = q * E + e
                for it in range(ITH):
                    par = n % 2
                    ctx.wait(v, "sl", b["sl"] + n + 1)
                    ctx.inc(v.tensor_tensor(sh[:, it, :], sil[par][:], pu[par][:], ALU.mult), "hs", 1)
                    n += 1
                if ih == 1 and zidx + 1 < NQ * E:
                    zbuild(v, zidx + 1)
                for ts in range(QTT):
                    pyb = py[m % 2]
                    ctx.wait(v, "yr", b["yr"] + m + 1)
                    if w % (2 * E) == 0 and q > 0:
                        ctx.wait(v, "xo", b["xo"] + 4 * q)
                    if w % (2 * E) == 0:
                        ctx.inc(v.tensor_copy(sy[:, ts, :], pyb[:]), "yf", 1)
                    else:
                        ctx.inc(v.tensor_tensor(sy[:, ts, :], sy[:, ts, :], pyb[:], ALU.add), "yf", 1)
                    m += 1
                if w % (2 * E) == 2 * E - 1:
                    for ts in range(QTT):
                        i = q * QTT + ts
                        ctx.wait(v, "yf", b["yf"] + 64 * (q + 1))
                        ctx.wait(v, tok_xl[i][0], tok_xl[i][1])
                        ctx.inc(v.scalar_tensor_tensor(sxq[i % 2][:], sy[:, ts, :], half, sxq[i % 2][:], ALU.mult, ALU.add), "xo", 1)


PHASE_NEED = {"p1": "moe", "p2": "perf", "p3": "moe", "p4": "l1ffn", "p5": "l1att", "p6": "l1ffn"}


def make_scratch(ctx, need):
    nc = ctx.nc
    scr = {}

    def dram(name, shape, dt):
        return nc.dram_tensor(name, shape, dt, kind="Internal").ap()

    scr["x"] = dram("x_scr", [T, D], F32)
    scr["idx"] = dram("idx_scr", [T], mybir.dt.uint32)
    scr["zc"] = dram("zc_scr", [E * 768, D], BF16)
    scr["yc"] = dram("yc_scr", [E * 768, D], BF16)
    scr["z"] = dram("z_scr", [T, D], BF16)
    scr["zT"] = dram("zT_scr", [D, T], BF16)
    if "moe" in need:
        scr["msk"] = dram("msk_scr", [E, T], BF16)
        scr["gwT"] = [dram(f"gwT{e}", [2, P, DK, IHW], BF16) for e in range(E)]
        scr["uwT"] = [dram(f"uwT{e}", [2, P, DK, IHW], BF16) for e in range(E)]
        scr["dwT"] = [dram(f"dwT{e}", [2, P, ITH, D], BF16) for e in range(E)]
        scr["rw_bf"] = dram("rwbf", [E, D], BF16)
        scr["rwT"] = dram("rwT", [D, E], BF16)
    if "perf" in need:
        scr["qkv_bf"] = dram("qkvbf", [3 * D, D], BF16)
        scr["proj_bf"] = dram("projbf", [D, D], BF16)
        scr["Wbig"] = dram("Wbig", [D + 1, 3 * D], BF16)
        scr["projT_aug"] = dram("projTa", [D + 1, D], BF16)
    if "l1att" in need:
        scr["Wq_bf"] = dram("Wqbf", [D, D], BF16)
        scr["Wk_bf"] = dram("Wkbf", [D, D], BF16)
        scr["Wv_bf"] = dram("Wvbf", [D, D], BF16)
        scr["Wg1_bf"] = dram("Wg1bf", [D, HD], BF16)
        scr["Wg2_bf"] = dram("Wg2bf", [HD, D], BF16)
        scr["Wo_bf"] = dram("Wobf", [D, D], BF16)
        scr["WoP_aug"] = dram("WoPa", [D + 1, D], BF16)
    if "l1ffn" in need:
        scr["Wup_bf"] = dram("Wupbf", [E, HD, D], BF16)
        scr["bup_bf"] = dram("bupbf", [E, D], BF16)
        scr["br_bf"] = dram("brbf", [1, E], BF16)
        scr["bd_bf"] = dram("bdbf", [E, HD], BF16)
        scr["WdP"] = [dram(f"WdP{e}", [D, HD], BF16) for e in range(E)]
        scr["bdP"] = dram("bdP", [E, HD], BF16)
    return scr


INPUT_NAMES = [
    "L0_attn_qkv_w", "L0_attn_qkv_b", "L0_attn_proj_w", "L0_attn_proj_b",
    "L0_attn_projection_matrix", "L0_ffn_router_w", "L0_ffn_gate_w",
    "L0_ffn_up_w", "L0_ffn_down_w", "L1_attn_Wq", "L1_attn_Wk", "L1_attn_Wv",
    "L1_attn_Wo", "L1_attn_Wg1", "L1_attn_Wg2", "L1_attn_gamma",
    "L1_attn_beta", "L1_ffn_ln_g", "L1_ffn_ln_b", "L1_ffn_bn_g",
    "L1_ffn_bn_b", "L1_ffn_bn_mean", "L1_ffn_bn_var", "L1_ffn_Wr",
    "L1_ffn_br", "L1_ffn_W_down", "L1_ffn_b_down", "L1_ffn_W_up",
    "L1_ffn_b_up",
]
INPUT_SHAPES = {
    "L0_attn_qkv_w": [3 * D, D], "L0_attn_qkv_b": [3 * D],
    "L0_attn_proj_w": [D, D], "L0_attn_proj_b": [D],
    "L0_attn_projection_matrix": [64, 64], "L0_ffn_router_w": [E, D],
    "L0_ffn_gate_w": [E, II, D], "L0_ffn_up_w": [E, II, D],
    "L0_ffn_down_w": [E, D, II], "L1_attn_Wq": [D, D], "L1_attn_Wk": [D, D],
    "L1_attn_Wv": [D, D], "L1_attn_Wo": [D, D], "L1_attn_Wg1": [D, HD],
    "L1_attn_Wg2": [HD, D], "L1_attn_gamma": [D], "L1_attn_beta": [D],
    "L1_ffn_ln_g": [D], "L1_ffn_ln_b": [D], "L1_ffn_bn_g": [D],
    "L1_ffn_bn_b": [D], "L1_ffn_bn_mean": [D], "L1_ffn_bn_var": [D],
    "L1_ffn_Wr": [D, E], "L1_ffn_br": [E], "L1_ffn_W_down": [E, D, HD],
    "L1_ffn_b_down": [E, HD], "L1_ffn_W_up": [E, HD, D], "L1_ffn_b_up": [E, D],
}


def build_nc(phases, parts="all", dbg=False):
    nc = bass.Bass("TRN2", target_bir_lowering=False, debug=False)
    stack = ExitStack()
    ctx = Ctx(NcProxy(nc), stack)
    need = sorted({PHASE_NEED[p] for p in phases})
    with stack:
        ctx.sem_init()
        for s in MOE_SEMS:
            ctx.sems[s] = stack.enter_context(nc.semaphore(s))
            ctx.n[s] = 0
        for nm in INPUT_NAMES:
            ctx.inputs[nm] = nc.dram_tensor(nm, INPUT_SHAPES[nm], F32, kind="ExternalInput").ap()
        xin = nc.dram_tensor("xin", [T, D], F32, kind="ExternalInput").ap()
        xout = nc.dram_tensor("xout", [T, D], F32, kind="ExternalOutput").ap()
        c_id = nc.dram_tensor("c_ident", [P, P], BF16, kind="ExternalInput").ap()
        c_on = nc.dram_tensor("c_ones", [1, 512], BF16, kind="ExternalInput").ap()
        c_onf = nc.dram_tensor("c_onesf", [1, 512], F32, kind="ExternalInput").ap()
        c_onc = nc.dram_tensor("c_onec", [P, 1], BF16, kind="ExternalInput").ap()
        c_cm = nc.dram_tensor("c_cmask", [4 * P, 512], BF16, kind="ExternalInput").ap()
        ctx.c_e8 = nc.dram_tensor("c_e8", [8, 2], F32, kind="ExternalInput").ap()
        ctx.c_ecap = nc.dram_tensor("c_ecap", [1, 256], F32, kind="ExternalInput").ap()
        c_tri = nc.dram_tensor("c_tri", [P, P], BF16, kind="ExternalInput").ap()
        c_rc = nc.dram_tensor("c_rotc", [SLEN, HD], F32, kind="ExternalInput").ap()
        c_rs = nc.dram_tensor("c_rots", [SLEN, HD], F32, kind="ExternalInput").ap()

        scr = make_scratch(ctx, need)
        ctx.ident = ctx.sb("c_ident_sb", [P, P], BF16)
        ctx.cones = ctx.sb("c_ones_sb", [1, 512], BF16)
        ctx.conesf = ctx.sb("c_onesf_sb", [1, 512], F32)
        ctx.conec = ctx.sb("c_onec_sb", [P, 1], BF16)
        ctx.cmask = ctx.sb("c_cmask_sb", [P, 4, 512], BF16)
        ctx.ctri = ctx.sb("c_tri_sb", [P, P], BF16)
        ctx.ceps = ctx.sb("c_eps_sb", [P, 2], F32)

        block = stack.enter_context(nc.Block())

        @block.sync
        def _(s):
            ctx.inc(s.dma_start(out=ctx.ident[:], in_=c_id[:]), "dS", 16)
            ctx.inc(s.dma_start(out=ctx.cones[:], in_=c_on[:]), "dS", 16)
            ctx.inc(s.dma_start(out=ctx.conesf[:], in_=c_onf[:]), "dS", 16)
            ctx.inc(s.dma_start(out=ctx.conec[:], in_=c_onc[:]), "dS", 16)
            ctx.inc(s.dma_start(out=ctx.cmask[:], in_=c_cm.rearrange("(d p) m -> p d m", p=P)), "dS", 16)
            ctx.inc(s.dma_start(out=ctx.ctri[:], in_=c_tri[:]), "dS", 16)
            s.wait_ge(ctx.sems["dS"], ctx.n["dS"])
        cload = ctx.n["dS"]

        @block.tensor
        def _(t):
            ctx.wait(t, "dS", cload)

        @block.vector
        def _(v):
            ctx.wait(v, "dS", cload)
            v.memset(ctx.ceps[:, 0:1], 1e-5)
            ctx.inc(v.memset(ctx.ceps[:, 1:2], 1e-6), "vS", 1)

        emit_prologue(ctx, block, scr, need, stage="casts")
        first_ln_done = False
        if phases and phases[0] in ("p1", "p3") and parts in ("all", "ln", "router", "sweeps"):
            with ExitStack() as lnscope:
                ztinfo = emit_ln(ctx, block, scr, xin, hold=lnscope, defer_zt=True)
                emit_prologue(ctx, block, scr, need, stage="rest")
                emit_zt(ctx, block, scr, *ztinfo)
                barrier(ctx, block)
            first_ln_done = True
        else:
            emit_prologue(ctx, block, scr, need, stage="rest")
            barrier(ctx, block)

        for pi, ph in enumerate(phases):
            src = xin if pi == 0 else scr["x"]
            dst = xout if pi == len(phases) - 1 else scr["x"]
            xtk = getattr(ctx, "last_xtok", None) if pi > 0 else None
            ctx.last_xtok = None

            def pre_ln():
                if xtk is None:
                    barrier(ctx, block)
                    emit_ln(ctx, block, scr, src)
                else:
                    emit_ln(ctx, block, scr, src, xtok=xtk)
                barrier(ctx, block)

            if ph in ("p1", "p3"):
                if parts in ("all", "ln", "router", "sweeps"):
                    if not (pi == 0 and first_ln_done):
                        pre_ln()
                if parts in ("all", "router", "sweeps"):
                    emit_router(ctx, block, scr)
                    barrier(ctx, block)
                if parts in ("all", "sweeps"):
                    if SPARSE_MOE:
                        with ExitStack() as ms:
                            sidx = ms.enter_context(nc.sbuf_tensor(f"sidx{pi}", [P, NT], U32))
                            emit_moe_indices(ctx, block, scr, sidx)
                            barrier(ctx, block)
                            emit_moe_sparse(ctx, block, scr, src, dst, 0.5, sidx)
                    else:
                        emit_moe_sweeps(ctx, block, scr, src, dst, 0.5)
            elif ph == "p2":
                pre_ln()
                emit_perf(ctx, block, scr, src, dst)
            elif ph in ("p4", "p6"):
                pre_ln()
                emit_l1ffn(ctx, block, scr, src, dst)
            elif ph == "p5":
                pre_ln()
                emit_l1att(ctx, block, scr, src, dst, (c_rc, c_rs))
        barrier(ctx, block)

        if dbg:
            dz = nc.dram_tensor("dbg_z", [T, D], BF16, kind="ExternalOutput").ap()
            dzT = nc.dram_tensor("dbg_zT", [D, T], BF16, kind="ExternalOutput").ap()
            dmsk = nc.dram_tensor("dbg_msk", [E, T], BF16, kind="ExternalOutput").ap()

            @block.sync
            def _(s):
                ctx.inc(s.dma_start(out=dz, in_=scr["z"]), "dS", 16)
                ctx.inc(s.dma_start(out=dzT, in_=scr["zT"]), "dS", 16)
                if "msk" in scr:
                    ctx.inc(s.dma_start(out=dmsk, in_=scr["msk"]), "dS", 16)

        @block.sync
        def _(s):
            ctx.wait(s, "dS", ctx.n["dS"])

        @block.gpsimd
        def _(g):
            ctx.wait(g, "dG", ctx.n["dG"])
        barrier(ctx, block)
    return nc


def _host_consts():
    cs = {}
    cs["c_ident"] = np.eye(P).astype(bf16)
    cs["c_ones"] = np.ones((1, 512), dtype=bf16)
    cs["c_onesf"] = np.ones((1, 512), dtype=np.float32)
    cs["c_onec"] = np.ones((P, 1), dtype=bf16)
    cm = np.zeros((4, P, 512), dtype=np.float32)
    for d in range(4):
        for sl in range(P):
            cm[d, sl, d * P + sl:] = 1.0
    cs["c_cmask"] = cm.reshape(4 * P, 512).astype(bf16)
    cs["c_e8"] = np.stack([np.ones(8), np.arange(8)], axis=1).astype(np.float32)
    cs["c_ecap"] = np.tile(np.arange(8) * CAP - 1.0, NT).reshape(1, 256).astype(np.float32)
    cs["c_tri"] = np.triu(np.ones((P, P))).astype(bf16)
    inv_freq = (10000.0 ** (-(np.arange(HD // 2, dtype=np.float64) * 2.0) / HD))
    pos = np.arange(SLEN, dtype=np.float64)
    ang = pos[:, None] * inv_freq[None, :]
    cosv = np.cos(ang).astype(np.float32)
    sinv = np.sin(ang).astype(np.float32)
    cs["c_rotc"] = np.concatenate([cosv, cosv], axis=1)
    cs["c_rots"] = np.concatenate([-sinv, sinv], axis=1)
    return cs


def run_phases(phases, x_full, weights):
    nc = build_nc(phases)
    cs = _host_consts()
    in_maps = []
    for c in range(NCORES):
        m = {nm: weights[nm] for nm in INPUT_NAMES}
        m.update(cs)
        m["xin"] = np.ascontiguousarray(x_full[2 * c:2 * c + 2].reshape(T, D))
        in_maps.append(m)
    res = run_bass_kernel_spmd(nc, in_maps, core_ids=list(range(NCORES)))
    out = np.stack([res.results[c]["xout"].reshape(2, SLEN, D) for c in range(NCORES)])
    return out.reshape(16, SLEN, D)


def kernel(**inputs):
    w = {nm: np.ascontiguousarray(np.asarray(inputs[nm], dtype=np.float32)) for nm in INPUT_NAMES}
    x = np.asarray(inputs["L0_attn_x"], dtype=np.float32)
    x = run_phases(["p1", "p2", "p3", "p4", "p5", "p6"], x, w)
    return x.astype(np.float32)


def emit_l1ffn(ctx, block, scr, src_ap, dst_ap):
    inp = ctx.inputs
    NCH = T // 512
    CPS = NCH // NSENT

    with ExitStack() as outer:
        eno = outer.enter_context
        sscl = eno(ctx.nc.sbuf_tensor("f4scl", [1, E, NSENT], F32))
        srep = eno(ctx.nc.sbuf_tensor("f4rep", [64, E, NSENT], F32))
        srows = eno(ctx.nc.sbuf_tensor("f4rows", [1, NSENT, E, P], BF16))

        with ExitStack() as ls:
            en = ls.enter_context
            rtb = [en(ctx.nc.sbuf_tensor(f"f4b{i}", [P, DK, 512], BF16)) for i in range(2)]
            szs = en(ctx.nc.sbuf_tensor("f4zs", [P, DK, NSENT], F32))
            sr8 = en(ctx.nc.sbuf_tensor("f4r8", [P, DK], F32))
            sWr = en(ctx.nc.sbuf_tensor("f4wr", [P, DK, E], F32))
            sbr = en(ctx.nc.sbuf_tensor("f4br", [1, E], F32))
            scols = en(ctx.nc.sbuf_tensor("f4cols", [P, DK, 6], F32))
            sbmm = en(ctx.nc.sbuf_tensor("f4bmm", [P, DK], F32))
            srs = en(ctx.nc.sbuf_tensor("f4rs", [P, DK], F32))
            ssd2 = en(ctx.nc.sbuf_tensor("f4sd", [P, DK], F32))
            szbn = en(ctx.nc.sbuf_tensor("f4zbn", [P, DK, NSENT], F32))
            slog = en(ctx.nc.sbuf_tensor("f4log", [1, E], F32))
            smx = en(ctx.nc.sbuf_tensor("f4mx", [1, 1], F32))
            sex = en(ctx.nc.sbuf_tensor("f4ex", [1, E], F32))
            ssum = en(ctx.nc.sbuf_tensor("f4sum", [1, 1], F32))
            sprob = en(ctx.nc.sbuf_tensor("f4prob", [1, E], F32))
            sgate = en(ctx.nc.sbuf_tensor("f4gate", [1, 1], F32))
            smsk1 = en(ctx.nc.sbuf_tensor("f4msk", [1, E], F32))
            plog = en(ctx.nc.psum_tensor("f4plog", [1, E], F32))
            prep = en(ctx.nc.psum_tensor("f4prep", [64, E], F32))
            prow = en(ctx.nc.psum_tensor("f4prow", [1, P], F32))
            tb = {k: ctx.n[k] for k in SEMS}
            tok_c = {}

            @block.sync
            def _(s):
                s.wait_ge(ctx.sems["dS"], tb["dS"])
                ctx.inc(s.dma_start(out=sWr[:], in_=r_pkm(inp["L1_ffn_Wr"])), "dS", 16)
                ctx.inc(s.dma_start(out=sbr[:], in_=inp["L1_ffn_br"].unsqueeze(0)), "dS", 16)
                for j, nm in enumerate(["ln_g", "ln_b", "bn_g", "bn_b", "bn_mean", "bn_var"]):
                    ctx.inc(s.dma_start(out=scols[:, :, j], in_=r_pk(inp[f"L1_ffn_{nm}"]), allow_slow_non_contiguous=True), "dS", 16)
                s.wait_ge(ctx.sems["dS"], ctx.n["dS"])
                for c in range(NCH):
                    if c >= 2:
                        ctx.wait(s, "vS", tb["vS"] + 2 * (c - 1))
                    sem = f"d{c % 2}"
                    tok_c[c] = (sem, ctx.inc(s.dma_start(out=rtb[c % 2][:], in_=r_pkm(scr["zT"][:, c * 512:(c + 1) * 512])), sem, 16))
            ldtok = tb["dS"] + 16 * 8

            @block.vector
            def _(v):
                ctx.wait(v, "dS", ldtok)
                for c in range(NCH):
                    s_ = c // CPS
                    ctx.wait(v, tok_c[c][0], tok_c[c][1])
                    ctx.inc(v.tensor_reduce(sr8[:].unsqueeze(2), rtb[c % 2][:], axis=AX.X, op=ALU.add), "vS", 1)
                    ctx.wait(v, "vS", ctx.n["vS"])
                    if c % CPS == 0:
                        ctx.inc(v.tensor_copy(szs[:, :, s_], sr8[:]), "vS", 1)
                    else:
                        ctx.inc(v.tensor_tensor(szs[:, :, s_], szs[:, :, s_], sr8[:], ALU.add), "vS", 1)
                    ctx.wait(v, "vS", ctx.n["vS"])
                ctx.inc(v.tensor_tensor(sbmm[:], scols[:, :, 1], scols[:, :, 4], ALU.subtract), "vS", 1)
                ctx.wait(v, "sS", tb["sS"] + 1)
                ctx.inc(v.reciprocal(srs[:], ssd2[:]), "vS", 1)
                ctx.wait(v, "vS", ctx.n["vS"])
                for s_ in range(NSENT):
                    ctx.inc(v.tensor_scalar(szbn[:, :, s_], szs[:, :, s_], 1.0 / SLEN, None, ALU.mult), "vS", 1)
                    ctx.wait(v, "vS", ctx.n["vS"])
                    for op, other in [(ALU.mult, scols[:, :, 0]), (ALU.add, sbmm[:]), (ALU.mult, srs[:]),
                                      (ALU.mult, scols[:, :, 2]), (ALU.add, scols[:, :, 3])]:
                        ctx.inc(v.tensor_tensor(szbn[:, :, s_], szbn[:, :, s_], other, op), "vS", 1)
                        ctx.wait(v, "vS", ctx.n["vS"])
            vA = ctx.n["vS"]

            @block.scalar
            def _(s):
                ctx.wait(s, "dS", ldtok)
                ctx.wait(s, "vS", tb["vS"] + 2 * NCH + 1)
                ctx.inc(s.activation(ssd2[:], scols[:, :, 5], AF.Sqrt, bias=ctx.ceps[:, 0:1]), "sS", 1)

            for s_ in range(NSENT):
                kb = {k: ctx.n[k] for k in SEMS}

                @block.tensor
                def _(t):
                    ctx.wait(t, "vS", vA if s_ == 0 else kb["vS"])
                    ctx.wait(t, "dS", ldtok)
                    for k in range(DK):
                        t.matmul(plog[:], szbn[:, k:k + 1, s_], sWr[:, k, :], start=(k == 0), stop=False)
                    ctx.inc(t.matmul(plog[:], ctx.conesf[0:1, 0:1], sbr[:], start=False, stop=True), "tS", 1)

                @block.vector
                def _(v):
                    ctx.wait(v, "tS", kb["tS"] + 1)
                    for fn in [lambda: v.tensor_copy(slog[:], plog[:]),
                               lambda: v.reduce_max(smx[:], slog[:], axis=AX.X),
                               lambda: v.tensor_scalar(sex[:], slog[:], smx[:], None, ALU.subtract)]:
                        ctx.inc(fn(), "vS", 1)
                        ctx.wait(v, "vS", ctx.n["vS"])

                @block.scalar
                def _(s):
                    ctx.wait(s, "vS", kb["vS"] + 3)
                    ctx.inc(s.activation(sex[:], sex[:], AF.Exp), "sS", 1)

                @block.vector
                def _(v):
                    ctx.wait(v, "sS", kb["sS"] + 1)
                    for fn in [lambda: v.reduce_sum(ssum[:], sex[:], axis=AX.X),
                               lambda: v.reciprocal(ssum[:], ssum[:]),
                               lambda: v.tensor_scalar(sprob[:], sex[:], ssum[:], None, ALU.mult),
                               lambda: v.reduce_max(sgate[:], sprob[:], axis=AX.X),
                               lambda: v.tensor_scalar(smsk1[:], sprob[:], sgate[:], None, ALU.is_ge),
                               lambda: v.tensor_scalar(sscl[:, :, s_], smsk1[:], sgate[:], 0.5, ALU.mult, ALU.mult)]:
                        ctx.inc(fn(), "vS", 1)
                        ctx.wait(v, "vS", ctx.n["vS"])

                @block.tensor
                def _(t):
                    ctx.wait(t, "vS", kb["vS"] + 9)
                    ctx.inc(t.matmul(prep[:], ctx.conesf[0:1, 0:64], sscl[0:1, :, s_], start=True, stop=True), "tS", 1)
                    for e in range(E):
                        ctx.wait(t, "vS", kb["vS"] + 10 + e)
                        ctx.inc(t.matmul(prow[:], sscl[0:1, e:e + 1, s_], ctx.conesf[0:1, 0:P], start=True, stop=True), "tS", 1)

                @block.vector
                def _(v):
                    ctx.wait(v, "tS", kb["tS"] + 2)
                    ctx.inc(v.tensor_copy(srep[:, :, s_], prep[:]), "vS", 1)
                    for e in range(E):
                        ctx.wait(v, "tS", kb["tS"] + 2 + e + 1)
                        ctx.inc(v.tensor_copy(srows[:, s_, e, :], prow[:]), "vS", 1)

        barrier(ctx, block)

        with ExitStack() as ls:
            en = ls.enter_context
            rtb = [en(ctx.nc.sbuf_tensor(f"f4c{i}", [P, DK, 512], BF16)) for i in range(2)]
            sWd = en(ctx.nc.sbuf_tensor("f4wd", [P, DK, E, HD], BF16))
            sbdr = en(ctx.nc.sbuf_tensor("f4bdr", [1, E, HD], BF16))
            sWu = en(ctx.nc.sbuf_tensor("f4wu", [64, E, D], BF16))
            sbur = en(ctx.nc.sbuf_tensor("f4bur", [1, E, D], BF16))
            shtf = en(ctx.nc.sbuf_tensor("f4htf", [64, 512], F32))
            sht = en(ctx.nc.sbuf_tensor("f4ht", [64, E, 512], BF16))
            sxf = [en(ctx.nc.sbuf_tensor(f"f4x{i}", [P, D], F32)) for i in range(2)]
            phT = [en(ctx.nc.psum_tensor(f"f4ph{i}", [64, 512], F32)) for i in range(2)]
            py = [en(ctx.nc.psum_tensor(f"f4py{i}", [P, D], F32)) for i in range(2)]
            b = {k: ctx.n[k] for k in (SEMS + MOE_SEMS)}
            tok_c = {}
            tok_xg = {}
            tok_st = {}

            @block.sync
            def _(s):
                s.wait_ge(ctx.sems["dS"], b["dS"])
                for e in range(E):
                    ctx.inc(s.dma_start(out=sWd[:, :, e, :], in_=r_pkm(scr["WdP"][e])), "dS", 16)
                    ctx.inc(s.dma_start(out=sbdr[:, e, :], in_=scr["bdP"][e].unsqueeze(0)), "dS", 16)
                ctx.inc(s.dma_start(out=sWu[:], in_=scr["Wup_bf"].rearrange("e p m -> p e m")), "dS", 16)
                ctx.inc(s.dma_start(out=sbur[:], in_=scr["bup_bf"].unsqueeze(0)), "dS", 16)
                s.wait_ge(ctx.sems["dS"], ctx.n["dS"])
                for gc in range(NCH):
                    if gc >= 2:
                        ctx.wait(s, "gu", b["gu"] + E * (gc - 1))
                    sem = f"d{gc % 2}"
                    tok_c[gc] = (sem, ctx.inc(s.dma_start(out=rtb[gc % 2][:], in_=r_pkm(scr["zT"][:, gc * 512:(gc + 1) * 512])), sem, 16))
            wtok = b["dS"] + 16 * 18

            @block.gpsimd
            def _(g):
                for i in range(NT):
                    j = i % 2
                    if i >= 2:
                        ctx.wait(g, tok_st[i - 2][0], tok_st[i - 2][1])
                    sem = f"g{2 + j}"
                    tok_xg[i] = (sem, ctx.inc(g.dma_start(out=sxf[j][:], in_=src_ap[i * P:(i + 1) * P, :]), sem, 16))
                    ctx.wait(g, "yf", b["yf"] + i + 1)
                    tok_st[i] = (f"g{j}", ctx.inc(g.dma_start(out=dst_ap[i * P:(i + 1) * P, :], in_=sxf[j][:]), f"g{j}", 16))

            @block.tensor
            def _(t):
                n = 0
                m = 0
                ctx.wait(t, "dS", wtok)
                for gc in range(NCH):
                    s_ = gc // CPS
                    ctx.wait(t, tok_c[gc][0], tok_c[gc][1])
                    for e in range(E):
                        par = n % 2
                        if n >= 2:
                            ctx.wait(t, "sl", b["sl"] + n - 1)
                        for k in range(DK):
                            t.matmul(phT[par][:], sWd[:, k, e, :], rtb[gc % 2][:, k, :], start=(k == 0), stop=False)
                        mm = t.matmul(phT[par][:], sbdr[:, e, :], ctx.cones[0:1, 0:512], start=False, stop=True)
                        ctx.inc(mm, "gu", 1)
                        n += 1
                    ctx.wait(t, "hs", b["hs"] + E * (gc + 1))
                    for ts in range(4):
                        pyb = py[m % 2]
                        if m >= 2:
                            ctx.wait(t, "yf", b["yf"] + m - 1)
                        for e in range(E):
                            t.matmul(pyb[:, 0:512], sht[:, e, ts * P:(ts + 1) * P], sWu[:, e, 0:512], start=(e == 0), stop=False)
                            t.matmul(pyb[:, 512:1024], sht[:, e, ts * P:(ts + 1) * P], sWu[:, e, 512:1024], start=(e == 0), stop=False)
                            t.matmul(pyb[:, 0:512], srows[:, s_, e, :], sbur[:, e, 0:512], start=False, stop=(e == E - 1))
                            mm = t.matmul(pyb[:, 512:1024], srows[:, s_, e, :], sbur[:, e, 512:1024], start=False, stop=(e == E - 1))
                        ctx.inc(mm, "yr", 1)
                        m += 1

            @block.scalar
            def _(s):
                for n in range(NCH * E):
                    par = n % 2
                    ctx.wait(s, "gu", b["gu"] + n + 1)
                    if n >= 1:
                        ctx.wait(s, "hs", b["hs"] + n)
                    ctx.inc(s.activation(shtf[:], phT[par][:], AF.Relu), "sl", 1)

            @block.vector
            def _(v):
                n = 0
                m = 0
                for gc in range(NCH):
                    s_ = gc // CPS
                    for e in range(E):
                        ctx.wait(v, "sl", b["sl"] + n + 1)
                        if gc >= 1 and e == 0:
                            ctx.wait(v, "yr", b["yr"] + 4 * gc)
                        ctx.inc(v.tensor_scalar(sht[:, e, :], shtf[:], srep[:, e:e + 1, s_], None, ALU.mult), "hs", 1)
                        n += 1
                    for ts in range(4):
                        i = gc * 4 + ts
                        ctx.wait(v, "yr", b["yr"] + m + 1)
                        ctx.wait(v, tok_xg[i][0], tok_xg[i][1])
                        ctx.inc(v.tensor_tensor(sxf[i % 2][:], sxf[i % 2][:], py[m % 2][:], ALU.add), "yf", 1)
                        m += 1

            ctx.last_xtok = dict(tok_st)


def _tok24(nh):
    return 24 * (nh // NH) + nh % NH


def _l5_e2(nh):
    return 96 * (nh // NH) + 4 * (nh % NH)


def emit_l1att(ctx, block, scr, src_ap, dst_ap, rot):
    c_rc, c_rs = rot
    NTS = SLEN // P

    with ExitStack() as outer:
        eno = outer.enter_context
        sv = eno(ctx.nc.sbuf_tensor("a5v", [P, NTS, D], BF16))
        sqT = eno(ctx.nc.sbuf_tensor("a5qT", [P, NH // 2, SLEN], BF16))
        skT = eno(ctx.nc.sbuf_tensor("a5kT", [P, NH // 2, SLEN], BF16))
        sg1T = eno(ctx.nc.sbuf_tensor("a5g1", [64, 2 * 4, 512], BF16))
        srotc = [eno(ctx.nc.sbuf_tensor(f"a5rc{i}", [P, HD], F32)) for i in range(2)]
        srots = [eno(ctx.nc.sbuf_tensor(f"a5rs{i}", [P, HD], F32)) for i in range(2)]
        sWg2 = eno(ctx.nc.sbuf_tensor("a5wg2", [64, D], BF16))
        sWoP = eno(ctx.nc.sbuf_tensor("a5wop", [P, DK, D], BF16))
        sWoB = eno(ctx.nc.sbuf_tensor("a5wob", [1, D], BF16))

        for s_ in range(NSENT):
            with ExitStack() as ls:
                en = ls.enter_context
                sWq = en(ctx.nc.sbuf_tensor(f"s{s_}a5wq", [P, DK, D], BF16))
                sWk = en(ctx.nc.sbuf_tensor(f"s{s_}a5wk", [P, DK, D], BF16))
                sWv = en(ctx.nc.sbuf_tensor(f"s{s_}a5wv", [P, DK, D], BF16))
                sWg1 = en(ctx.nc.sbuf_tensor(f"s{s_}a5wg1", [P, DK, HD], BF16))
                szt = [en(ctx.nc.sbuf_tensor(f"s{s_}a5zt{i}", [P, DK, P], BF16)) for i in range(2)]
                rtb = [en(ctx.nc.sbuf_tensor(f"s{s_}a5rt0", [P, DK, 512], BF16))] * 2
                stmpf = en(ctx.nc.sbuf_tensor(f"s{s_}a5tmf", [P, D], F32))
                sswap = en(ctx.nc.sbuf_tensor(f"s{s_}a5swp", [P, D], F32))
                srotq = en(ctx.nc.sbuf_tensor(f"s{s_}a5rq", [P, D], BF16))
                pqkv = [en(ctx.nc.psum_tensor(f"s{s_}a5pq{i}", [P, D], F32)) for i in range(2)]
                ptr = [en(ctx.nc.psum_tensor(f"s{s_}a5ptr{i}", [64, P], BF16)) for i in range(2)]
                pg1 = en(ctx.nc.psum_tensor(f"s{s_}a5pg1", [64, 512], F32))
                b = {k: ctx.n[k] for k in (SEMS + MOE_SEMS)}
                tok_z = {}
                tok_g = {}

                def qkv_free(n):
                    m = n - 2
                    i2, p2 = divmod(m, 3)
                    if p2 <= 1:
                        return ("sl", b["sl"] + 2 * i2 + p2 + 1)
                    return ("hs", b["hs"] + i2 + 1)

                @block.sync
                def _(s):
                    s.wait_ge(ctx.sems["dS"], b["dS"])
                    if s_ == 0:
                        ctx.inc(s.dma_start(out=sWg2[:], in_=scr["Wg2_bf"][:]), "dS", 16)
                        ctx.inc(s.dma_start(out=sWoP[:], in_=r_pkm(scr["WoP_aug"][0:D, :])), "dS", 16)
                        ctx.inc(s.dma_start(out=sWoB[:], in_=scr["WoP_aug"][1024:1025, :]), "dS", 16)
                    ctx.inc(s.dma_start(out=sWq[:], in_=r_pkm(scr["Wq_bf"])), "dS", 16)
                    ctx.inc(s.dma_start(out=sWk[:], in_=r_pkm(scr["Wk_bf"])), "dS", 16)
                    ctx.inc(s.dma_start(out=sWv[:], in_=r_pkm(scr["Wv_bf"])), "dS", 16)
                    ctx.inc(s.dma_start(out=sWg1[:], in_=r_pkm(scr["Wg1_bf"])), "dS", 16)
                    s.wait_ge(ctx.sems["dS"], ctx.n["dS"])
                    for i in range(NTS):
                        if i >= 2:
                            ctx.wait(s, "gu", b["gu"] + 3 * (i - 1))
                        sem = f"d{i % 2}"
                        tok_z[i] = (sem, ctx.inc(s.dma_start(out=szt[i % 2][:], in_=r_pkm(scr["zT"][:, (s_ * SLEN + i * P):(s_ * SLEN + (i + 1) * P)])), sem, 16))
                        rsem = f"d{4 + i % 2}"
                        if i >= 2:
                            ctx.wait(s, "zp", b["zp"] + 2 * i - 2)
                        s.dma_start(out=srotc[i % 2][:], in_=c_rc[i * P:(i + 1) * P, :]).then_inc(ctx.sems[rsem], 16)
                        ctx.n[rsem] += 16
                        tok_r = ctx.inc(s.dma_start(out=srots[i % 2][:], in_=c_rs[i * P:(i + 1) * P, :]), rsem, 16)
                        tok_z[(i, "rot")] = (rsem, tok_r)
                    for c in range(4):
                        if c >= 1:
                            ctx.wait(s, "yr", b["yr"] + c)
                        sem = "d2"
                        tok_g[c] = (sem, ctx.inc(s.dma_start(out=rtb[0][:], in_=r_pkm(scr["zT"][:, (s_ * SLEN + c * 512):(s_ * SLEN + (c + 1) * 512)])), sem, 16))
                wtok = ctx.n["dS"]

                @block.tensor
                def _(t):
                    ctx.wait(t, "dS", wtok)
                    ntr = 0
                    for i in range(NTS):
                        ctx.wait(t, tok_z[i][0], tok_z[i][1])
                        for part, W in [(0, sWq), (1, sWk), (2, sWv)]:
                            n = 3 * i + part
                            par = n % 2
                            if n >= 2:
                                fs, ft = qkv_free(n)
                                ctx.wait(t, fs, ft)
                            for half in range(2):
                                for k in range(DK):
                                    mm = t.matmul(pqkv[par][:, half * 512:(half + 1) * 512], szt[i % 2][:, k, :],
                                                  W[:, k, half * 512:(half + 1) * 512], start=(k == 0), stop=(k == DK - 1))
                            ctx.inc(mm, "gu", 1)
                        for qk in range(2):
                            ctx.wait(t, "zp", b["zp"] + 2 * i + qk + 1)
                            for h in range(NH):
                                if ntr >= 2:
                                    ctx.wait(t, "mke", b["mke"] + ntr - 1)
                                ctx.inc(t.transpose(ptr[ntr % 2][:], srotq[:, h * HD:(h + 1) * HD], ctx.ident[:]), "e1", 1)
                                ntr += 1
                    for c in range(4):
                        ctx.wait(t, tok_g[c][0], tok_g[c][1])
                        if c >= 1:
                            ctx.wait(t, "yf", b["yf"] + c)
                        for k in range(DK):
                            mm = t.matmul(pg1[:], sWg1[:, k, :], rtb[0][:, k, :], start=(k == 0), stop=(k == DK - 1))
                        ctx.inc(mm, "yr", 1)

                @block.scalar
                def _(s):
                    for i in range(NTS):
                        for qk in range(2):
                            n = 3 * i + qk
                            nz = 2 * i + qk
                            ctx.wait(s, "gu", b["gu"] + n + 1)
                            if nz >= 1:
                                ctx.wait(s, "zp", b["zp"] + nz)
                            ctx.inc(s.activation(stmpf[:], pqkv[n % 2][:], AF.Silu), "sl", 1)

                @block.vector
                def _(v):
                    ntr = 0
                    for i in range(NTS):
                        c1b = srotc[i % 2][:].unsqueeze(1).broadcast_to((P, NH, HD))
                        s1b = srots[i % 2][:].unsqueeze(1).broadcast_to((P, NH, HD))
                        for qk in range(2):
                            nz = 2 * i + qk
                            if qk == 0:
                                ctx.wait(v, tok_z[(i, "rot")][0], tok_z[(i, "rot")][1])
                            ctx.wait(v, "sl", b["sl"] + nz + 1)
                            if nz >= 1:
                                ctx.wait(v, "zp", b["zp"] + nz)
                            swv = sswap[:].rearrange("p (h two f) -> p h two f", two=2, f=HD // 2)
                            inv = stmpf[:].rearrange("p (h two f) -> p h two f", two=2, f=HD // 2)
                            v.tensor_copy(swv[:, :, 0, :], inv[:, :, 1, :])
                            ctx.inc(v.tensor_copy(swv[:, :, 1, :], inv[:, :, 0, :]), "vS", 1)
                            ctx.wait(v, "vS", ctx.n["vS"])
                            v.tensor_tensor(stmpf[:].rearrange("p (h f) -> p h f", f=HD),
                                            stmpf[:].rearrange("p (h f) -> p h f", f=HD), c1b, ALU.mult)
                            ctx.inc(v.tensor_tensor(sswap[:].rearrange("p (h f) -> p h f", f=HD),
                                                    sswap[:].rearrange("p (h f) -> p h f", f=HD), s1b, ALU.mult), "vS", 1)
                            ctx.wait(v, "vS", ctx.n["vS"])
                            if nz >= 1:
                                ctx.wait(v, "e1", b["e1"] + 16 * nz)
                            ctx.inc(v.tensor_tensor(srotq[:], stmpf[:], sswap[:], ALU.add), "zp", 1)
                            dstT = sqT if qk == 0 else skT
                            for h in range(NH):
                                ctx.wait(v, "e1", b["e1"] + ntr + 1)
                                ctx.inc(v.tensor_copy(dstT[64 * (h // 8):64 * (h // 8) + 64, h % 8, i * P:(i + 1) * P], ptr[ntr % 2][:]), "mke", 1)
                                ntr += 1
                        nv = 3 * i + 2
                        ctx.wait(v, "gu", b["gu"] + nv + 1)
                        ctx.inc(v.tensor_copy(sv[:, i, :], pqkv[nv % 2][:]), "hs", 1)
                    for c in range(4):
                        ctx.wait(v, "yr", b["yr"] + c + 1)
                        ctx.inc(v.tensor_copy(sg1T[:, s_ * 4 + c, :], pg1[:]), "yf", 1)

            barrier(ctx, block)

            with ExitStack() as ls:
                en = ls.enter_context
                sE = [en(ctx.nc.sbuf_tensor(f"s{s_}a5e{i}", [P, 512], BF16)) for i in range(2)]
                st64w = [en(ctx.nc.sbuf_tensor(f"s{s_}a5t64{i}", [64, 512], BF16)) for i in range(2)]
                sattn4 = en(ctx.nc.sbuf_tensor(f"s{s_}a5at", [P, 4, D], BF16))
                sgt = en(ctx.nc.sbuf_tensor(f"s{s_}a5gt", [P, D], BF16))
                sato = en(ctx.nc.sbuf_tensor(f"s{s_}a5ato", [P, D], F32))
                snorm = en(ctx.nc.sbuf_tensor(f"s{s_}a5nrm", [P, D], BF16))
                snormT = en(ctx.nc.sbuf_tensor(f"s{s_}a5nrT", [P, DK, P], BF16))
                st12 = en(ctx.nc.sbuf_tensor(f"s{s_}a5st", [P, 12], F32))
                smv = en(ctx.nc.sbuf_tensor(f"s{s_}a5mv", [P, 2], F32))
                ssd = en(ctx.nc.sbuf_tensor(f"s{s_}a5sd", [P, 1], F32))
                srv = en(ctx.nc.sbuf_tensor(f"s{s_}a5rv", [P, 1], F32))
                sxf = [en(ctx.nc.sbuf_tensor(f"s{s_}a5x{i}", [P, D], F32)) for i in range(2)]
                pE = [en(ctx.nc.psum_tensor(f"s{s_}a5pE{i}", [P, 512], F32)) for i in range(2)]
                pav = en(ctx.nc.psum_tensor(f"s{s_}a5pav", [64, 512], F32))
                ptn = en(ctx.nc.psum_tensor(f"s{s_}a5ptn", [P, P], BF16))
                pgt = en(ctx.nc.psum_tensor(f"s{s_}a5pgt", [P, D], F32))
                pxo = en(ctx.nc.psum_tensor(f"s{s_}a5pxo", [P, D], F32))
                b3 = {k: ctx.n[k] for k in (SEMS + MOE_SEMS)}
                tok_xg = {}
                tok_st = {}
                NTC4 = NTS // 4

                def cum_pairs(tc):
                    return 32 * tc * (tc + 1)


                @block.gpsimd
                def _(g):
                    for i in range(NTS):
                        gt = s_ * NTS + i
                        j = i % 2
                        if i >= 2:
                            ctx.wait(g, tok_st[i - 2][0], tok_st[i - 2][1])
                        sem = f"g{2 + j}"
                        tok_xg[i] = (sem, ctx.inc(g.dma_start(out=sxf[j][:], in_=src_ap[gt * P:(gt + 1) * P, :]), sem, 16))
                        ctx.wait(g, "yf", b3["yf"] + i + 1)
                        tok_st[i] = (f"g{j}", ctx.inc(g.dma_start(out=dst_ap[gt * P:(gt + 1) * P, :], in_=sxf[j][:]), f"g{j}", 16))

                if s_ == 0:
                    ctx.last_xtok = {}
                for i_ in range(NTS):
                    ctx.last_xtok[s_ * NTS + i_] = tok_st[i_]

                @block.tensor
                def _(t):
                    ne = 0
                    for tc in range(NTC4):
                        jmax = 4 * (tc + 1)
                        for h in range(NH):
                            pb = 64 * (h // 8)
                            hr = h % 8
                            nh = tc * NH + h
                            for j in range(jmax):
                                par = ne % 2
                                if ne >= 2:
                                    ctx.wait(t, "hs", b3["hs"] + ne - 1)
                                ctx.inc(t.matmul(pE[par][:], skT[pb:pb + 64, hr, j * P:(j + 1) * P],
                                                 sqT[pb:pb + 64, hr, tc * 512:(tc + 1) * 512], start=True, stop=True), "gu", 1)
                                if j >= 1:
                                    ctx.wait(t, "hs", b3["hs"] + ne)
                                    ctx.inc(t.matmul(pav[:], sv[:, j - 1, h * HD:(h + 1) * HD], sE[(ne - 1) % 2][:],
                                                     start=(j - 1 == 0), stop=False), "yr", 1)
                                ne += 1
                            ctx.wait(t, "hs", b3["hs"] + ne)
                            ctx.inc(t.matmul(pav[:], sv[:, jmax - 1, h * HD:(h + 1) * HD], sE[(ne - 1) % 2][:],
                                             start=(jmax - 1 == 0), stop=True), "yr", 1)
                            ctx.wait(t, "mke", b3["mke"] + 48 * tc + h + 1)
                            for q in range(4):
                                if h == 0 and q == 0:
                                    ctx.wait(t, "mke", b3["mke"] + 48 * tc)
                                if nh > 0 or q > 0:
                                    ctx.wait(t, "zp", b3["zp"] + 68 * tc + 4 * h + q)
                                ctx.inc(t.transpose(ptn[:, 0:64], st64w[nh % 2][:, q * P:(q + 1) * P], ctx.ident[0:64, 0:64]), "e2", 1)
                        for q in range(4):
                            i = tc * 4 + q
                            if i >= 1:
                                ctx.wait(t, "sl", b3["sl"] + i)
                            t.matmul(pgt[:, 0:512], sg1T[:, s_ * 4 + tc, q * P:(q + 1) * P], sWg2[:, 0:512], start=True, stop=True)
                            ctx.inc(t.matmul(pgt[:, 512:1024], sg1T[:, s_ * 4 + tc, q * P:(q + 1) * P], sWg2[:, 512:1024], start=True, stop=True), "e3", 1)
                            ctx.wait(t, "zp", b3["zp"] + 68 * tc + 64 + q + 1)
                            for k in range(DK):
                                if k == 0:
                                    ctx.wait(t, "zp", b3["zp"] + 68 * tc + 64)
                                    ctx.wait(t, "mke", b3["mke"] + 48 * tc + 16 + 8 * q)
                                else:
                                    ctx.wait(t, "mke", b3["mke"] + 48 * tc + 16 + 8 * q + k)
                                ctx.inc(t.transpose(ptn[:], snorm[:, k * P:(k + 1) * P], ctx.ident[:]), "e2", 1)
                            ctx.wait(t, "mke", b3["mke"] + 48 * tc + 16 + 8 * (q + 1))
                            if i >= 1:
                                ctx.wait(t, "yf", b3["yf"] + i)
                            for half in range(2):
                                for k in range(DK):
                                    t.matmul(pxo[:, half * 512:(half + 1) * 512], snormT[:, k, :], sWoP[:, k, half * 512:(half + 1) * 512], start=(k == 0), stop=False)
                                mm = t.matmul(pxo[:, half * 512:(half + 1) * 512], ctx.cones[0:1, 0:P], sWoB[:, half * 512:(half + 1) * 512], start=False, stop=True)
                            ctx.inc(mm, "e1", 1)

                @block.scalar
                def _(s):
                    for i in range(NTS):
                        ctx.wait(s, "e3", b3["e3"] + i + 1)
                        if i >= 1:
                            ctx.wait(s, "vS", b3["vS"] + 4 * (i - 1) + 1)
                        ctx.inc(s.activation(sgt[:], pgt[:], AF.Sigmoid), "sl", 1)
                        ctx.wait(s, "vS", b3["vS"] + 4 * i + 3)
                        ctx.inc(s.activation(ssd[:], smv[:, 1:2], AF.Sqrt, bias=ctx.ceps[:, 0:1]), "sS", 1)

                @block.vector
                def _(v):
                    ne = 0
                    for tc in range(NTC4):
                        jmax = 4 * (tc + 1)
                        for h in range(NH):
                            nh = tc * NH + h
                            if h == 0 and tc >= 1:
                                ctx.wait(v, "vS", b3["vS"] + 16 * tc)
                            for j in range(jmax):
                                par = ne % 2
                                ctx.wait(v, "gu", b3["gu"] + ne + 1)
                                if ne >= 2:
                                    ctx.wait(v, "yr", b3["yr"] + ne - 1)
                                if j >= 4 * tc:
                                    ctx.inc(v.tensor_tensor(sE[par][:], pE[par][:], ctx.cmask[:, j - 4 * tc, :], ALU.mult), "hs", 1)
                                else:
                                    ctx.inc(v.tensor_copy(sE[par][:], pE[par][:]), "hs", 1)
                                ne += 1
                            ctx.wait(v, "yr", b3["yr"] + ne)
                            if nh >= 2:
                                ctx.wait(v, "e2", b3["e2"] + _l5_e2(nh - 2) + 4)
                            ctx.inc(v.tensor_copy(st64w[nh % 2][:], pav[:]), "mke", 1)
                            for q in range(4):
                                ctx.wait(v, "e2", b3["e2"] + _l5_e2(nh) + q + 1)
                                ctx.inc(v.tensor_copy(sattn4[:, q, h * HD:(h + 1) * HD], ptn[:, 0:64]), "zp", 1)
                        for q in range(4):
                            i = tc * 4 + q
                            vb = b3["vS"] + 4 * i
                            ctx.wait(v, "sl", b3["sl"] + i + 1)
                            ctx.wait(v, "zp", b3["zp"] + 68 * tc + 64)
                            ctx.inc(v.tensor_tensor(sato[:], sattn4[:, q, :], sgt[:], ALU.mult), "vS", 1)
                            ctx.wait(v, "vS", vb + 1)
                            v.bn_stats(st12[:, 0:6], sato[:, 0:512])
                            ctx.inc(v.bn_stats(st12[:, 6:12], sato[:, 512:1024]), "vS", 1)
                            ctx.wait(v, "vS", vb + 2)
                            ctx.inc(v.bn_aggr(smv[:], st12[:]), "vS", 1)
                            ctx.wait(v, "sS", b3["sS"] + i + 1)
                            ctx.inc(v.reciprocal(srv[:], ssd[:]), "vS", 1)
                            ctx.wait(v, "vS", vb + 4)
                            ctx.inc(v.tensor_scalar(snorm[:], sato[:], smv[:, 0:1], srv[:], ALU.subtract, ALU.mult), "zp", 1)
                            for k in range(DK):
                                ctx.wait(v, "e2", b3["e2"] + 96 * tc + 64 + 8 * q + k + 1)
                                ctx.inc(v.tensor_copy(snormT[:, k, :], ptn[:]), "mke", 1)
                            ctx.wait(v, "e1", b3["e1"] + i + 1)
                            ctx.wait(v, tok_xg[i][0], tok_xg[i][1])
                            ctx.inc(v.tensor_tensor(sxf[i % 2][:], sxf[i % 2][:], pxo[:], ALU.add), "yf", 1)

            barrier(ctx, block)


def emit_perf(ctx, block, scr, src_ap, dst_ap):
    NTS = SLEN // P
    NCB = 6

    def nphi(nc_):
        return 4 * (nc_ // NCB) + max(0, nc_ % NCB - 2)

    with ExitStack() as outer:
        eno = outer.enter_context
        swb = eno(ctx.nc.sbuf_tensor("p2wb", [P, DK, 3 * D], BF16))
        sbb = eno(ctx.nc.sbuf_tensor("p2bb", [1, 3 * D], BF16))
        spj = eno(ctx.nc.sbuf_tensor("p2pj", [P, DK, D], BF16))
        spjB = eno(ctx.nc.sbuf_tensor("p2pjB", [1, D], BF16))
        sqp = eno(ctx.nc.sbuf_tensor("p2qp", [P, NTS, D], BF16))
        skp = eno(ctx.nc.sbuf_tensor("p2kp", [P, NTS, D], BF16))
        svp = eno(ctx.nc.sbuf_tensor("p2vp", [P, NTS, D], BF16))
        skv = eno(ctx.nc.sbuf_tensor("p2kv", [64, NH * (HD + 1)], BF16))
        tb0 = {k: ctx.n[k] for k in SEMS}

        @block.sync
        def _(s):
            s.wait_ge(ctx.sems["dS"], tb0["dS"])
            ctx.inc(s.dma_start(out=swb[:], in_=r_pkm(scr["Wbig"][0:D, :])), "dS", 16)
            ctx.inc(s.dma_start(out=sbb[:], in_=scr["Wbig"][1024:1025, :]), "dS", 16)
            ctx.inc(s.dma_start(out=spj[:], in_=r_pkm(scr["projT_aug"][0:D, :])), "dS", 16)
            ctx.inc(s.dma_start(out=spjB[:], in_=scr["projT_aug"][1024:1025, :]), "dS", 16)
            s.wait_ge(ctx.sems["dS"], ctx.n["dS"])
        wtok0 = ctx.n["dS"]

        for s_ in range(NSENT):
            with ExitStack() as ls:
                en = ls.enter_context
                szt = [en(ctx.nc.sbuf_tensor(f"x{s_}p2z{i}", [P, DK, P], BF16)) for i in range(2)]
                sneg = [en(ctx.nc.sbuf_tensor(f"x{s_}p2n{i}", [P, 8], F32)) for i in range(2)]
                pbig = [en(ctx.nc.psum_tensor(f"x{s_}p2b{i}", [P, 512], F32)) for i in range(2)]
                b = {k: ctx.n[k] for k in (SEMS + MOE_SEMS)}
                tok_z = {}

                @block.sync
                def _(s):
                    for i in range(NTS):
                        if i >= 2:
                            ctx.wait(s, "gu", b["gu"] + NCB * (i - 1))
                        sem = f"d{i % 2}"
                        tok_z[i] = (sem, ctx.inc(s.dma_start(out=szt[i % 2][:], in_=r_pkm(scr["zT"][:, (s_ * SLEN + i * P):(s_ * SLEN + (i + 1) * P)])), sem, 16))

                @block.tensor
                def _(t):
                    ctx.wait(t, "dS", wtok0)
                    nc_ = 0
                    for i in range(NTS):
                        ctx.wait(t, tok_z[i][0], tok_z[i][1])
                        for cb in range(NCB):
                            par = nc_ % 2
                            if nc_ >= 2:
                                ctx.wait(t, "hs", b["hs"] + nc_ - 1)
                            for k in range(DK):
                                t.matmul(pbig[par][:], szt[i % 2][:, k, :], swb[:, k, cb * 512:(cb + 1) * 512], start=(k == 0), stop=False)
                            mm = t.matmul(pbig[par][:], ctx.cones[0:1, 0:P], sbb[0:1, cb * 512:(cb + 1) * 512], start=False, stop=True)
                            ctx.inc(mm, "gu", 1)
                            nc_ += 1

                @block.scalar
                def _(s):
                    nc_ = 0
                    for i in range(NTS):
                        for cb in range(NCB):
                            par = nc_ % 2
                            if cb >= 2:
                                dst = sqp if cb < 4 else skp
                                col = (cb - 2) % 2
                                ctx.wait(s, "gu", b["gu"] + nc_ + 1)
                                ctx.wait(s, "mk", b["mk"] + 2 * (nphi(nc_) + 1))
                                for hseg in range(8):
                                    ctx.inc(s.activation(dst[:, i, col * 512 + hseg * HD: col * 512 + (hseg + 1) * HD],
                                                         pbig[par][:, hseg * HD:(hseg + 1) * HD], AF.Exp,
                                                         bias=sneg[nphi(nc_) % 2][:, hseg:hseg + 1]), "sl", 1)
                            nc_ += 1

                @block.vector
                def _(v):
                    nc_ = 0
                    for i in range(NTS):
                        for cb in range(NCB):
                            par = nc_ % 2
                            if cb < 2:
                                ctx.wait(v, "gu", b["gu"] + nc_ + 1)
                                ctx.inc(v.tensor_copy(svp[:, i, cb * 512:(cb + 1) * 512], pbig[par][:]), "hs", 1)
                            else:
                                dst = sqp if cb < 4 else skp
                                col = (cb - 2) % 2
                                np_ = nphi(nc_)
                                ctx.wait(v, "gu", b["gu"] + nc_ + 1)
                                if np_ >= 2:
                                    ctx.wait(v, "sl", b["sl"] + 8 * (np_ - 1))
                                ctx.inc(v.reduce_max(sneg[np_ % 2][:].unsqueeze(2),
                                                     pbig[par][:].rearrange("p (h f) -> p h f", f=HD),
                                                     axis=AX.X, negate=True), "mk", 1)
                                ctx.wait(v, "mk", ctx.n["mk"])
                                ctx.inc(v.tensor_scalar(sneg[np_ % 2][:], sneg[np_ % 2][:], -LN8, None, ALU.add), "mk", 1)
                                ctx.wait(v, "sl", b["sl"] + 8 * (np_ + 1))
                                ctx.inc(v.tensor_scalar(dst[:, i, col * 512:(col + 1) * 512],
                                                        dst[:, i, col * 512:(col + 1) * 512], 1e-6, None, ALU.add), "hs", 1)
                            nc_ += 1

            barrier(ctx, block)

            with ExitStack() as ls:
                en = ls.enter_context
                pkv = en(ctx.nc.psum_tensor(f"x{s_}p2kv", [64, NH * P], F32))
                b = {k: ctx.n[k] for k in (SEMS + MOE_SEMS)}

                @block.tensor
                def _(t):
                    for i in range(NTS):
                        for h in range(NH):
                            st = (i == 0 and h % 4 == 0)
                            sp = (i == NTS - 1 and h % 4 == 3)
                            t.matmul(pkv[:, h * P:h * P + HD], skp[:, i, h * HD:(h + 1) * HD],
                                     svp[:, i, h * HD:(h + 1) * HD], start=st, stop=False)
                            mm = t.matmul(pkv[:, h * P + HD:h * P + HD + 1], skp[:, i, h * HD:(h + 1) * HD],
                                          ctx.conec[0:P, 0:1], start=False, stop=sp)
                    ctx.inc(mm, "e1", 1)

                @block.vector
                def _(v):
                    ctx.wait(v, "e1", b["e1"] + 1)
                    for h in range(NH):
                        ctx.inc(v.tensor_copy(skv[:, h * (HD + 1):(h + 1) * (HD + 1)], pkv[:, h * P:h * P + HD + 1]), "e2", 1)

            barrier(ctx, block)

            with ExitStack() as ls:
                en = ls.enter_context
                sqpt = [en(ctx.nc.sbuf_tensor(f"x{s_}p2qt{i}", [64, P], BF16)) for i in range(2)]
                sattn = en(ctx.nc.sbuf_tensor(f"x{s_}p2at", [P, D], BF16))
                satT = en(ctx.nc.sbuf_tensor(f"x{s_}p2aT", [P, DK, P], BF16))
                sdiv = en(ctx.nc.sbuf_tensor(f"x{s_}p2dv", [P, 1], F32))
                sxf = [en(ctx.nc.sbuf_tensor(f"x{s_}p2x{i}", [P, D], F32)) for i in range(2)]
                ptr = [en(ctx.nc.psum_tensor(f"x{s_}p2tr{i}", [64, P], BF16)) for i in range(2)]
                paug = [en(ctx.nc.psum_tensor(f"x{s_}p2au{i}", [P, HD + 1], F32)) for i in range(2)]
                ptn = en(ctx.nc.psum_tensor(f"x{s_}p2tn", [P, P], BF16))
                pxo = en(ctx.nc.psum_tensor(f"x{s_}p2xo", [P, D], F32))
                b = {k: ctx.n[k] for k in (SEMS + MOE_SEMS)}
                tok_xg = {}
                tok_st = {}

                @block.gpsimd
                def _(g):
                    for i in range(NTS):
                        gt = s_ * NTS + i
                        j = i % 2
                        if i >= 2:
                            ctx.wait(g, tok_st[i - 2][0], tok_st[i - 2][1])
                        sem = f"g{2 + j}"
                        tok_xg[i] = (sem, ctx.inc(g.dma_start(out=sxf[j][:], in_=src_ap[gt * P:(gt + 1) * P, :]), sem, 16))
                        ctx.wait(g, "yf", b["yf"] + i + 1)
                        tok_st[i] = (f"g{j}", ctx.inc(g.dma_start(out=dst_ap[gt * P:(gt + 1) * P, :], in_=sxf[j][:]), f"g{j}", 16))

                if s_ == 0:
                    ctx.last_xtok = {}
                for i_ in range(NTS):
                    ctx.last_xtok[s_ * NTS + i_] = tok_st[i_]

                @block.tensor
                def _(t):
                    nh = 0
                    for i in range(NTS):
                        for h in range(NH):
                            if nh >= 2:
                                ctx.wait(t, "mke", b["mke"] + nh - 1)
                            ctx.inc(t.transpose(ptr[nh % 2][:], sqp[:, i, h * HD:(h + 1) * HD], ctx.ident[:]), "e3", 1)
                            ctx.wait(t, "mke", b["mke"] + nh + 1)
                            if nh >= 2:
                                ctx.wait(t, "yr", b["yr"] + nh - 1)
                            ctx.inc(t.matmul(paug[nh % 2][:], sqpt[nh % 2][:], skv[:, h * (HD + 1):(h + 1) * (HD + 1)],
                                             start=True, stop=True), "gu", 1)
                            nh += 1
                        ctx.wait(t, "yr", b["yr"] + 16 * (i + 1))
                        for k in range(DK):
                            if i > 0 or k > 0:
                                ctx.wait(t, "zp", b["zp"] + i * DK + k)
                            ctx.inc(t.transpose(ptn[:], sattn[:, k * P:(k + 1) * P], ctx.ident[:]), "e2", 1)
                        ctx.wait(t, "zp", b["zp"] + (i + 1) * DK)
                        if i >= 1:
                            ctx.wait(t, "yf", b["yf"] + i)
                        for half in range(2):
                            for k in range(DK):
                                t.matmul(pxo[:, half * 512:(half + 1) * 512], satT[:, k, :], spj[:, k, half * 512:(half + 1) * 512], start=(k == 0), stop=False)
                            mm = t.matmul(pxo[:, half * 512:(half + 1) * 512], ctx.cones[0:1, 0:P], spjB[0:1, half * 512:(half + 1) * 512], start=False, stop=True)
                        ctx.inc(mm, "mk", 1)

                @block.vector
                def _(v):
                    nh = 0
                    for i in range(NTS):
                        for h in range(NH):
                            if h == 0 and i >= 1:
                                ctx.wait(v, "e2", b["e2"] + i * DK)
                            ctx.wait(v, "e3", b["e3"] + nh + 1)
                            ctx.inc(v.tensor_copy(sqpt[nh % 2][:], ptr[nh % 2][:]), "mke", 1)
                            ctx.wait(v, "gu", b["gu"] + nh + 1)
                            ctx.inc(v.tensor_scalar(sdiv[:], paug[nh % 2][:, HD:HD + 1], 1e-6, None, ALU.add), "vS", 1)
                            ctx.wait(v, "vS", ctx.n["vS"])
                            ctx.inc(v.reciprocal(sdiv[:], sdiv[:]), "vS", 1)
                            ctx.wait(v, "vS", ctx.n["vS"])
                            ctx.inc(v.tensor_scalar(sattn[:, h * HD:(h + 1) * HD], paug[nh % 2][:, 0:HD], sdiv[:], None, ALU.mult), "yr", 1)
                            nh += 1
                        for k in range(DK):
                            ctx.wait(v, "e2", b["e2"] + i * DK + k + 1)
                            ctx.inc(v.tensor_copy(satT[:, k, :], ptn[:]), "zp", 1)
                        ctx.wait(v, "mk", b["mk"] + i + 1)
                        ctx.wait(v, tok_xg[i][0], tok_xg[i][1])
                        ctx.inc(v.tensor_tensor(sxf[i % 2][:], sxf[i % 2][:], pxo[:], ALU.add), "yf", 1)

            barrier(ctx, block)


CAP = 768
NTS6 = CAP // P
U32 = mybir.dt.uint32


def emit_moe_indices(ctx, block, scr, sidx):
    import bass_rust
    with ExitStack() as ls:
        en = ls.enter_context
        mb = en(ctx.nc.sbuf_tensor("ixmb", [8, T], BF16))
        m32 = en(ctx.nc.sbuf_tensor("ixm", [32, T], F32))
        c32 = en(ctx.nc.sbuf_tensor("ixc", [32, T], F32))
        t32 = en(ctx.nc.sbuf_tensor("ixt", [32, T], F32))
        smT = en(ctx.nc.sbuf_tensor("ixsmt", [P, NT, 8], BF16))
        s8 = en(ctx.nc.sbuf_tensor("ixs8", [P, NT, 8], F32))
        slot = en(ctx.nc.sbuf_tensor("ixslt", [P, NT], F32))
        ctmp = en(ctx.nc.sbuf_tensor("ixct", [1, 256], F32))
        c2 = en(ctx.nc.sbuf_tensor("ixc2", [1, 256], F32))
        c2b = en(ctx.nc.sbuf_tensor("ixc2b", [1, 256], F32))
        scap = en(ctx.nc.sbuf_tensor("ixcap", [1, 256], F32))
        szt = [en(ctx.nc.sbuf_tensor(f"ixz{i}", [P, D], BF16)) for i in range(2)]
        pmT = [en(ctx.nc.psum_tensor(f"ixpt{i}", [P, 8], BF16)) for i in range(2)]
        pall = en(ctx.nc.psum_tensor("ixpa", [P, NT * 8], F32))
        prow = en(ctx.nc.psum_tensor("ixpr", [1, 256], F32))
        pbc = en(ctx.nc.psum_tensor("ixpb", [P, NT * 8], F32))
        b = {k: ctx.n[k] for k in (SEMS + MOE_SEMS)}
        first = not getattr(ctx, "zc_zeroed", False)
        ctx.zc_zeroed = True

        @block.sync
        def _(s):
            s.wait_ge(ctx.sems["dS"], b["dS"])
            ctx.inc(s.dma_start(out=mb[:], in_=scr["msk"][:]), "dS", 16)
            ctx.inc(s.dma_start(out=scap[:], in_=ctx.c_ecap[:]), "dS", 16)
            s.wait_ge(ctx.sems["dS"], ctx.n["dS"])
        ldtok = ctx.n["dS"]

        zctok = None
        if first:
            szero = en(ctx.nc.sbuf_tensor("ixzr", [P, D], BF16))

            @block.vector
            def _(v):
                ctx.inc(v.memset(szero[:], 0.0), "vS", 1)
            zrtok = ctx.n["vS"]

            @block.sync
            def _(s):
                ctx.wait(s, "vS", zrtok)
                for r in range(E * CAP // P):
                    ctx.inc(s.dma_start(out=scr["zc"][r * P:(r + 1) * P, :], in_=szero[:]), "dS", 16)
                s.wait_ge(ctx.sems["dS"], ctx.n["dS"])
            zctok = ctx.n["dS"]

        def vstep(v, ins):
            ctx.inc(ins, "vS", 1)
            ctx.wait(v, "vS", ctx.n["vS"])

        @block.vector
        def _(v):
            ctx.wait(v, "dS", ldtok)
            v.memset(m32[:], 0.0)
            vstep(v, v.memset(t32[:], 0.0))
            vstep(v, v.tensor_copy(m32[0:8, :], mb[:]))
            for sh in (1, 2, 4):
                mp = [(i - sh) if (8 > i >= sh) else 31 for i in range(32)]
                srcb = m32 if sh == 1 else c32
                vstep(v, v.stream_shuffle(t32[:], srcb[:], mp))
                vstep(v, v.tensor_tensor(c32[:], srcb[:], t32[:], ALU.add))
            vstep(v, v.tensor_scalar(t32[:], c32[:], 1.0, None, ALU.is_equal))
            vstep(v, v.tensor_tensor(m32[:], m32[:], t32[:], ALU.mult))
            ctx.inc(v.tensor_copy(mb[:], m32[0:8, :]), "mk", 1)
        vtf = ctx.n["mk"]

        @block.tensor
        def _(t):
            ctx.wait(t, "mk", vtf)
            for i in range(NT):
                if i >= 2:
                    ctx.wait(t, "yr", b["yr"] + i - 1)
                ctx.inc(t.transpose(pmT[i % 2][:], mb[:, i * P:(i + 1) * P], ctx.ident[0:8, 0:8]), "gu", 1)
            ctx.wait(t, "yr", b["yr"] + NT)
            ctx.inc(t.matmul(pall[:], ctx.ctri[:], smT[:].rearrange("p t e -> p (t e)"), start=True, stop=True), "sl", 1)
            ctx.inc(t.matmul(prow[:], ctx.conec[:], smT[:].rearrange("p t e -> p (t e)"), start=True, stop=True), "sl", 1)
            ctx.wait(t, "hs", b["hs"] + 1)
            ctx.inc(t.matmul(pbc[:], ctx.conesf[0:1, 0:P], c2b[:], start=True, stop=True), "sl", 1)

        @block.vector
        def _(v):
            for i in range(NT):
                ctx.wait(v, "gu", b["gu"] + i + 1)
                ctx.inc(v.tensor_copy(smT[:, i, :], pmT[i % 2][:]), "yr", 1)
            ctx.wait(v, "sl", b["sl"] + 2)
            vstep(v, v.tensor_copy(ctmp[:], prow[:]))
            vstep(v, v.memset(c2[:, 0:8], 0.0))
            vstep(v, v.tensor_copy(c2[:, 8:256], ctmp[:, 0:248]))
            a, bb = c2, c2b
            for s_ in (8, 16, 32, 64, 128):
                v.tensor_copy(bb[:, 0:s_], a[:, 0:s_])
                vstep(v, v.tensor_tensor(bb[:, s_:256], a[:, s_:256], a[:, 0:256 - s_], ALU.add))
                a, bb = bb, a
            assert a is c2b
            ctx.inc(v.tensor_tensor(c2b[:], a[:], scap[:], ALU.add), "hs", 1)
            ctx.wait(v, "sl", b["sl"] + 3)
            vstep(v, v.tensor_copy(s8[:].rearrange("p t e -> p (t e)"), pbc[:]))
            vstep(v, v.tensor_tensor(s8[:].rearrange("p t e -> p (t e)"), s8[:].rearrange("p t e -> p (t e)"), pall[:], ALU.add))
            vstep(v, v.tensor_tensor(s8[:].rearrange("p t e -> p (t e)"), s8[:].rearrange("p t e -> p (t e)"), smT[:].rearrange("p t e -> p (t e)"), ALU.mult))
            vstep(v, v.tensor_reduce(slot[:].unsqueeze(2), s8[:], axis=AX.X, op=ALU.add))
            vstep(v, v.tensor_scalar(slot[:], slot[:], float(E * CAP - 1), None, ALU.min))
            ctx.inc(v.tensor_copy(sidx[:], slot[:]), "zp", 1)
        slottok = ctx.n["zp"]

        tok_zl = {}

        @block.sync
        def _(s):
            for i in range(NT):
                sem = f"d{4 + i % 2}"
                if i >= 2:
                    ctx.wait(s, f"g{i % 2}", b[f"g{i % 2}"] + 16 * (i // 2))
                tok_zl[i] = (sem, ctx.inc(s.dma_start(out=szt[i % 2][:], in_=scr["z"][i * P:(i + 1) * P, :]), sem, 16))

        @block.gpsimd
        def _(g):
            ctx.wait(g, "zp", slottok)
            if zctok is not None:
                ctx.wait(g, "dS", zctok)
            for i in range(NT):
                ctx.wait(g, tok_zl[i][0], tok_zl[i][1])
                ctx.inc(g.indirect_dma_start(out=scr["zc"][:], out_offset=bass_rust.IndirectOffsetOnAxis(axis=0, ap=sidx[:, i:i + 1]),
                                             in_=szt[i % 2][:], in_offset=None), f"g{i % 2}", 16)
            g.wait_ge(ctx.sems["g0"], ctx.n["g0"])
            g.wait_ge(ctx.sems["g1"], ctx.n["g1"])


def emit_moe_sparse(ctx, block, scr, src_ap, dst_ap, half, sidx):
    import bass_rust
    b = {k: ctx.n[k] for k in (SEMS + MOE_SEMS)}
    with ExitStack() as ls:
        en = ls.enter_context
        swg = [en(ctx.nc.sbuf_tensor(f"swg{i}", [P, DK, IHW], BF16)) for i in range(2)]
        swu = [en(ctx.nc.sbuf_tensor(f"swu{i}", [P, DK, IHW], BF16)) for i in range(2)]
        swd = [en(ctx.nc.sbuf_tensor(f"swd{i}", [P, ITH, D], BF16)) for i in range(2)]
        szc = [en(ctx.nc.sbuf_tensor(f"szc{i}", [P, DK, CAP], BF16)) for i in range(2)]
        sh = en(ctx.nc.sbuf_tensor("ssh", [P, ITH, CAP], BF16))
        sil = [en(ctx.nc.sbuf_tensor(f"ssil{i}", [P, 512], BF16)) for i in range(2)]
        syb = en(ctx.nc.sbuf_tensor("ssyb", [P, NTS6, D], BF16))
        sxf = [en(ctx.nc.sbuf_tensor(f"ssx{i}", [P, D], F32)) for i in range(2)]
        syg = [en(ctx.nc.sbuf_tensor(f"ssyg{i}", [P, D], BF16)) for i in range(2)]
        pg = [en(ctx.nc.psum_tensor(f"spg{i}", [P, 512], F32)) for i in range(2)]
        pu = [en(ctx.nc.psum_tensor(f"spu{i}", [P, 512], F32)) for i in range(2)]
        py = [en(ctx.nc.psum_tensor(f"spy{i}", [P, D], F32)) for i in range(2)]
        TCS = [(0, 512), (512, 256)]
        tok_yw = {e: ("g3", b["g3"] + 16 * (e + 1)) for e in range(E)}

        tok_zc = {}
        tok_w = {}

        @block.sync
        def _(s):
            s.wait_ge(ctx.sems["g0"], b["g0"])
            s.wait_ge(ctx.sems["g1"], b["g1"])
            for e in range(E):
                sem = f"d{2 + e % 2}"
                if e >= 2:
                    ctx.wait(s, "gu", b["gu"] + 44 * (e - 1))
                for k in range(DK):
                    ctx.inc(s.dma_start(out=szc[e % 2][:, k, :], in_=scr["zc"][e * CAP:(e + 1) * CAP, k * P:(k + 1) * P], transpose=True), sem, 16)
                tok_zc[e] = (sem, ctx.n[sem])
                for ih in range(2):
                    w = 2 * e + ih
                    wsem = f"d{w % 2}"
                    if w >= 2:
                        ctx.wait(s, "yr", b["yr"] + 6 * (w - 1))
                    ihs = ih * IHW
                    ctx.inc(s.dma_start(out=swg[w % 2][:], in_=scr["gwT"][e][ih]), wsem, 16)
                    ctx.inc(s.dma_start(out=swu[w % 2][:], in_=scr["uwT"][e][ih]), wsem, 16)
                    ctx.inc(s.dma_start(out=swd[w % 2][:], in_=scr["dwT"][e][ih]), wsem, 16)
                    tok_w[w] = (wsem, ctx.n[wsem])

        @block.tensor
        def _(t):
            n = 0
            m = 0
            for e in range(E):
                ctx.wait(t, tok_zc[e][0], tok_zc[e][1])
                for ih in range(2):
                    w = 2 * e + ih
                    ctx.wait(t, tok_w[w][0], tok_w[w][1])
                    for (tco, tcn) in TCS:
                        for it in range(ITH):
                            par = n % 2
                            if n >= 2:
                                ctx.wait(t, "hs", b["hs"] + n - 1)
                            for k in range(DK):
                                t.matmul(pg[par][:, 0:tcn], swg[w % 2][:, k, it * P:(it + 1) * P], szc[e % 2][:, k, tco:tco + tcn], start=(k == 0), stop=(k == DK - 1))
                            for k in range(DK):
                                mm = t.matmul(pu[par][:, 0:tcn], swu[w % 2][:, k, it * P:(it + 1) * P], szc[e % 2][:, k, tco:tco + tcn], start=(k == 0), stop=(k == DK - 1))
                            ctx.inc(mm, "gu", 1)
                            n += 1
                    ctx.wait(t, "hs", b["hs"] + n)
                    for ts in range(NTS6):
                        pyb = py[m % 2]
                        if m >= 2:
                            ctx.wait(t, "yf", b["yf"] + m - 1)
                        for it in range(ITH):
                            t.matmul(pyb[:, 0:512], sh[:, it, ts * P:(ts + 1) * P], swd[w % 2][:, it, 0:512], start=(it == 0), stop=(it == ITH - 1))
                            mm = t.matmul(pyb[:, 512:1024], sh[:, it, ts * P:(ts + 1) * P], swd[w % 2][:, it, 512:1024], start=(it == 0), stop=(it == ITH - 1))
                        ctx.inc(mm, "yr", 1)
                        m += 1

        @block.scalar
        def _(s):
            n = 0
            for e in range(E):
                for ih in range(2):
                    for (tco, tcn) in TCS:
                        for it in range(ITH):
                            par = n % 2
                            ctx.wait(s, "gu", b["gu"] + n + 1)
                            if n >= 2:
                                ctx.wait(s, "hs", b["hs"] + n - 1)
                            ctx.inc(s.activation(sil[par][:, 0:tcn], pg[par][:, 0:tcn], AF.Silu), "sl", 1)
                            n += 1


        @block.vector
        def _(v):
            n = 0
            m = 0
            for e in range(E):
                for ih in range(2):
                    for (tco, tcn) in TCS:
                        for it in range(ITH):
                            par = n % 2
                            ctx.wait(v, "sl", b["sl"] + n + 1)
                            ctx.inc(v.tensor_tensor(sh[:, it, tco:tco + tcn], sil[par][:, 0:tcn], pu[par][:, 0:tcn], ALU.mult), "hs", 1)
                            n += 1
                    for ts in range(NTS6):
                        ctx.wait(v, "yr", b["yr"] + m + 1)
                        if ih == 0 and e >= 1:
                            ctx.wait(v, "g3", tok_yw[e - 1][1])
                        if ih == 0:
                            ctx.inc(v.tensor_copy(syb[:, ts, :], py[m % 2][:]), "yf", 1)
                        else:
                            ctx.inc(v.tensor_tensor(syb[:, ts, :], syb[:, ts, :], py[m % 2][:], ALU.add), "yf", 1)
                        m += 1
        @block.gpsimd
        def _(g):
            for e in range(E):
                ctx.wait(g, "yf", b["yf"] + 12 * (e + 1))
                tok = ctx.inc(g.dma_start(out=r_pkm(scr["yc"][e * CAP:(e + 1) * CAP, :]), in_=syb[:]), "g3", 16)
                assert tok == tok_yw[e][1]
            g.wait_ge(ctx.sems["g3"], ctx.n["g3"])
        ycdone = ctx.n["g3"]

        tok_xl = {}
        tok_gg = {}
        st_slot = lambda i: ("g2" if i % 2 == 0 else "g4")
        tok_st = {i: (st_slot(i), b[st_slot(i)] + 16 * (i // 2 + 1)) for i in range(NT)}

        @block.sync
        def _(s):
            for i in range(NT):
                j = i % 2
                sem = f"d{4 + j}"
                if i >= 2:
                    ctx.wait(s, "xo", b["xo"] + i - 1)
                    ctx.wait(s, tok_st[i - 2][0], tok_st[i - 2][1])
                tok_xl[i] = (sem, ctx.inc(s.dma_start(out=sxf[j][:], in_=src_ap[i * P:(i + 1) * P, :]), sem, 16))

        @block.gpsimd
        def _(g):
            g.wait_ge(ctx.sems["g3"], ycdone)
            for i in range(NT):
                j = i % 2
                sem = f"g{j}"
                if i >= 2:
                    ctx.wait(g, "xo", b["xo"] + i - 1)
                tok_gg[i] = (sem, ctx.inc(g.indirect_dma_start(out=syg[j][:], out_offset=None, in_=scr["yc"][:],
                                                               in_offset=bass_rust.IndirectOffsetOnAxis(axis=0, ap=sidx[:, i:i + 1])), sem, 16))
                ctx.wait(g, "xo", b["xo"] + i + 1)
                tok = ctx.inc(g.dma_start(out=dst_ap[i * P:(i + 1) * P, :], in_=sxf[j][:]), tok_st[i][0], 16)
                assert tok == tok_st[i][1]

        @block.vector
        def _(v):
            for i in range(NT):
                j = i % 2
                ctx.wait(v, tok_xl[i][0], tok_xl[i][1])
                ctx.wait(v, tok_gg[i][0], tok_gg[i][1])
                ctx.inc(v.scalar_tensor_tensor(sxf[j][:], syg[j][:], half, sxf[j][:], ALU.mult, ALU.add), "xo", 1)

        ctx.last_xtok = dict(tok_st)
